# version 03
# baseline (speedup 1.0000x reference)
import numpy as np
import ml_dtypes

import concourse.bass as bass
import concourse.mybir as mybir
from concourse.bass_utils import run_bass_kernel_spmd

F32 = mybir.dt.float32
BF16 = mybir.dt.bfloat16
I16 = mybir.dt.int16
AF = mybir.ActivationFunctionType
ALU = mybir.AluOpType
AX = mybir.AxisListType

B = 4
NTOK = 2048
C = 2048
D = 128
NH = 16
NCH = 8
MYTOK = 1024
HID = 5632
CCH = 16
HCH = 44
ATT_SCALE = float(D) ** -0.5
LN_EPS = 1e-5
RMS_EPS = 1e-6
QEPS = 1e-5

SIZES = {"qkv": 6144 * 2048, "gate": 2048 * 2048, "proj": 2048 * 2048,
         "w1": HID * 2048, "w2": 2048 * HID, "w3": HID * 2048}
SEGS = ["qkv", "gate", "proj", "w1", "w2", "w3"]
SEG_F = [SIZES[s] // 8 // 128 for s in SEGS]
WSH_F = sum(SEG_F)

_CACHE = {}


def build():
    nc = bass.Bass("TRN2", target_bir_lowering=False, debug=False)

    xT = nc.dram_tensor("xT", [C, MYTOK], F32, kind="ExternalInput")
    wqkvgT = nc.dram_tensor("wqkvgT", [C, NH * 512], F32, kind="ExternalInput")
    bqkvg = nc.dram_tensor("bqkvg", [128, NH * 4], F32, kind="ExternalInput")
    wprojT = nc.dram_tensor("wprojT", [C, C], F32, kind="ExternalInput")
    bproj = nc.dram_tensor("bproj", [128, CCH], F32, kind="ExternalInput")
    normw = nc.dram_tensor("normw", [128, CCH], F32, kind="ExternalInput")
    w1T = nc.dram_tensor("w1T", [C, HID], F32, kind="ExternalInput")
    w3T = nc.dram_tensor("w3T", [C, HID], F32, kind="ExternalInput")
    w2T = nc.dram_tensor("w2T", [HID, C], F32, kind="ExternalInput")
    wmod_sh = nc.dram_tensor("wmod_sh", [C, 1536], F32, kind="ExternalInput")
    cTin = nc.dram_tensor("cTin", [C, B], F32, kind="ExternalInput")
    wsh = nc.dram_tensor("wsh", [128, WSH_F], F32, kind="ExternalInput")
    bonehot = nc.dram_tensor("bonehot", [128, 4], F32, kind="ExternalInput")
    msel = nc.dram_tensor("msel", [128, 1], F32, kind="ExternalInput")
    identin = nc.dram_tensor("identin", [128, 128], F32, kind="ExternalInput")

    yT = nc.dram_tensor("yT", [C, MYTOK], F32, kind="ExternalOutput")

    mod_in = nc.dram_tensor("mod_in", [1536, 4], F32)
    mod_out = nc.dram_tensor("mod_out", [12288, 4], F32)
    wsum_in = nc.dram_tensor("wsum_in", [1, 8], F32)
    wsum_out = nc.dram_tensor("wsum_out", [1, 8], F32)
    hbuf = nc.dram_tensor("hbuf", [C, MYTOK], F32)
    qbuf = nc.dram_tensor("qbuf", [128, NH * MYTOK], F32)
    kbuf = nc.dram_tensor("kbuf", [128, NH * MYTOK], F32)
    vbuf = nc.dram_tensor("vbuf", [128, NH * MYTOK], F32)
    kvbuf = nc.dram_tensor("kvbuf", [128, NH * NCH * 128], F32)
    T_in = nc.dram_tensor("T_in", [128, NH * 128], F32)
    T_out = nc.dram_tensor("T_out", [256, NH * 128], F32)
    h3buf = nc.dram_tensor("h3buf", [HID, MYTOK], F32)
    xnewbuf = nc.dram_tensor("xnewbuf", [C, MYTOK], F32)

    ctxs = []

    def E(cm):
        ctxs.append(cm)
        return cm.__enter__()

    AR1 = E(nc.sbuf_tensor("AR1", [128, 24576], F32))
    AR2 = E(nc.sbuf_tensor("AR2", [128, 8192], F32))
    bufQ = E(nc.sbuf_tensor("bufQ", [128, 2048], F32))
    bufK = E(nc.sbuf_tensor("bufK", [128, 2048], F32))
    bufV = E(nc.sbuf_tensor("bufV", [128, 2048], F32))
    scrA = E(nc.sbuf_tensor("scrA", [128, 2048], F32))
    scrB = E(nc.sbuf_tensor("scrB", [128, 2048], F32))
    scrC = E(nc.sbuf_tensor("scrC", [128, 2048], F32))
    ex = E(nc.sbuf_tensor("ex", [128, 2048], F32))
    smaxrep = E(nc.sbuf_tensor("smaxrep", [128, 1024], F32))
    small = E(nc.sbuf_tensor("small", [128, 2048], F32))
    t16 = E(nc.sbuf_tensor("t16", [128, 1024], I16))
    stats = E(nc.sbuf_tensor("stats", [128, 256], F32))
    modcol = E(nc.sbuf_tensor("modcol", [128, 96], F32))
    kcol = E(nc.sbuf_tensor("kcol", [128, 16], F32))
    bqg = E(nc.sbuf_tensor("bqg", [128, NH * 4], F32))
    bpj = E(nc.sbuf_tensor("bpj", [128, CCH], F32))
    nwc = E(nc.sbuf_tensor("nwc", [128, CCH], F32))
    mselc = E(nc.sbuf_tensor("mselc", [128, 1], F32))
    bhot = E(nc.sbuf_tensor("bhot", [128, 4], F32))
    ident = E(nc.sbuf_tensor("ident", [128, 128], F32))
    ones = E(nc.sbuf_tensor("ones", [128, 128], F32))
    dcy = E(nc.sbuf_tensor("dcy", [128, 128], F32))
    Ssl = ex

    pbig0 = E(nc.psum_tensor("pbig0", [128, 2048], F32))
    pbig1 = E(nc.psum_tensor("pbig1", [128, 2048], F32))

    class Ctr:
        def __init__(self, name):
            self.h = E(nc.semaphore(name))
            self.n = 0

        def inc(self, ins, k=1):
            ins.then_inc(self.h, k)
            self.n += k
            return self.n

    class DSem:

        def __init__(self, name, nway=2):
            self.c = [Ctr(name + chr(97 + j)) for j in range(nway)]
            self.nway = nway

        def issue(self, ins, i):
            self.c[i % self.nway].inc(ins, 16)

        def wait(self, eng, i):
            eng.wait_ge(self.c[i % self.nway].h,
                        (i // self.nway + 1) * 16)

        def wait_all(self, eng):
            for c in self.c:
                eng.wait_ge(c.h, c.n)

    class Bank:

        def __init__(self, ctrs):
            self.c = ctrs
            self.b = [c.n for c in ctrs]
            self.nway = len(ctrs)

        def issue(self, ins, i):
            self.c[i % self.nway].inc(ins, 16)

        def wait(self, eng, i):
            eng.wait_ge(self.c[i % self.nway].h,
                        self.b[i % self.nway] + (i // self.nway + 1) * 16)

        def wait_all(self, eng):
            for c in self.c:
                eng.wait_ge(c.h, c.n)

    gIN = Ctr("gIN"); gSP = Ctr("gSP"); gGD = Ctr("gGD"); gCC = Ctr("gCC")
    gPE = Ctr("gPE"); gDV = Ctr("gDV"); gAC = Ctr("gAC"); gGP = Ctr("gGP")
    dWSH = DSem("dWSH"); dABS = Ctr("dABS"); dMOD = DSem("dMOD"); dMDM = Ctr("dMDM")
    dX1 = DSem("dX1"); dSQ1 = Ctr("dSQ1"); dMM1 = Ctr("dMM1")
    dXB = DSem("dXB"); dHB = Ctr("dHB"); dACB = Ctr("dACB"); dSPB = DSem("dSPB")
    dXC = DSem("dXC"); dQC = Ctr("dQC"); dHQ = Ctr("dHQ")
    dWL2 = DSem("dWL2"); dSG2 = Ctr("dSG2"); dMMq = Ctr("dMMq"); dDQq = Ctr("dDQq")
    dCHp = Ctr("dCHp"); dCHd = Ctr("dCHd")
    dDCY = Ctr("dDCY"); dTOL = Ctr("dTOL")
    dSC = Ctr("dSC"); dDN = Ctr("dDN")
    dGRP = Ctr("dGRP"); dGRD = Ctr("dGRD"); dREC = Ctr("dREC"); dRAD = Ctr("dRAD")
    dPJL = DSem("dPJL"); dPJS = Ctr("dPJS"); dPJM = Ctr("dPJM"); dPJD = Ctr("dPJD")
    dPQA = Ctr("dPQA"); dPQB = Ctr("dPQB")
    dX2L = DSem("dX2L"); dXNW = DSem("dXNW"); dLN2 = Ctr("dLN2")
    dXNL = DSem("dXNL"); dH2 = Ctr("dH2"); dH2A = Ctr("dH2A"); dH2S = DSem("dH2S")
    dH2L = DSem("dH2L"); dH2Q = Ctr("dH2Q")
    dFWL = DSem("dFWL", 4); dFWS = Ctr("dFWS"); dFMM = Ctr("dFMM"); dFH3 = Ctr("dFH3")
    dFSP = DSem("dFSP")
    dH3L = DSem("dH3L"); dH3Q = Ctr("dH3Q")
    dW2L = DSem("dW2L", 6); dW2S = Ctr("dW2S"); dYMM = Ctr("dYMM"); dYDQ = Ctr("dYDQ")
    dXN2 = Ctr("dXN2"); dXN2L = DSem("dXN2L"); dYOUT = DSem("dYOUT")
    dVA0 = Ctr("dVA0"); dVA1 = Ctr("dVA1"); dVA2 = Ctr("dVA2"); dVA3 = Ctr("dVA3")
    dFWV = Ctr("dFWV"); dW2V = Ctr("dW2V")

    block = E(nc.Block())

    def sec(name):
        return getattr(block, name)

    hqT = AR1[:, 16384:24576].bitcast(BF16)
    outf = AR1[:, 0:16384].rearrange("p (a b) -> p a b", a=16)
    olh = AR1[:, 0:16384].rearrange("p (a b) -> p b a", a=16)
    xtmp = AR1[:, 16384:24576]
    pjq = AR1[:, 16384:24576].bitcast(BF16).rearrange("p (a b) -> p a b", a=16)
    h2q = AR1[:, 16384:24576].bitcast(BF16).rearrange("p (a b) -> p a b", a=16)
    h3qf = AR1[:, 0:22528].bitcast(BF16)
    wqg = [AR2[:, 0:4096].bitcast(BF16).rearrange("p (a b) -> p a b", a=16),
           AR2[:, 4096:8192].bitcast(BF16).rearrange("p (a b) -> p a b", a=16)]
    wsg = [AR2[:, 0:2816].bitcast(BF16).rearrange("p (a b) -> p a b", a=44),
           AR2[:, 2816:5632].bitcast(BF16).rearrange("p (a b) -> p a b", a=44)]

    colT = small[:, 1664:1792]
    Tslot = [small[:, 0:128], small[:, 128:256]]
    kvt = [small[:, 256:384], small[:, 384:512]]
    ktvt = [small[:, 512:768], small[:, 768:1024]]
    kvsl = xtmp[:, 0:2048]
    bd = [xtmp[:, 2048:2176], xtmp[:, 2176:2304]]
    v2g = [xtmp[:, 2304:2432], xtmp[:, 2432:2560]]
    v2t = [xtmp[:, 2560:2688], xtmp[:, 2688:2816]]
    Mrep_sb = [xtmp[0:16, 4096:5120], xtmp[0:16, 5120:6144]]
    Mrow = [xtmp[32:33, 4096:5120], xtmp[32:33, 5120:6144]]
    rzrow = [xtmp[64:65, 4096:5120], xtmp[64:65, 5120:6144]]
    attnT = xtmp[0:16, 6144:8192]
    Mst = [stats[0:16, 128:192], stats[0:16, 192:256]]
    psc = pbig0[0:16, 0:1024]
    pscT = pbig1[0:16, 0:1024]
    pdn16 = pbig1[0:16, 1024:2048]
    pdn1 = pbig1[0:1, 1024:2048]
    pv = [pbig0[:, 1024:1152], pbig0[:, 1152:1280]]
    plo = [pbig0[:, 1280:1408], pbig0[:, 1408:1536]]
    prec = [pbig0[:, 1536:1664], pbig0[:, 1664:1792]]

    @sec("sync")
    def _(sync):
        gIN.inc(sync.dma_start(out=ident[:, :], in_=identin[:, :]), 16)
        gIN.inc(sync.dma_start(out=bqg[:, :], in_=bqkvg[:, :]), 16)
        gIN.inc(sync.dma_start(out=bpj[:, :], in_=bproj[:, :]), 16)
        gIN.inc(sync.dma_start(out=nwc[:, :], in_=normw[:, :]), 16)
        gIN.inc(sync.dma_start(out=mselc[:, :], in_=msel[:, :]), 16)
        gIN.inc(sync.dma_start(out=bhot[:, :], in_=bonehot[:, :]), 16)
        gIN.inc(sync.dma_start(
            out=small[:, 1792:1856].rearrange("p (a b) -> p a b", a=16),
            in_=cTin[:, :].rearrange("(a p) b -> p a b", p=128)), 16)
    P0_CONST = gIN.n

    @sec("vector")
    def _(v):
        gDV.inc(v.memset(ones[:, :], 1.0))
        v.drain()
    P0_ONES = gDV.n

    @sec("scalar")
    def _(sc):
        sc.wait_ge(gIN.h, P0_CONST)
        sc.activation(small[:, 1792:1856], small[:, 1792:1856], AF.Silu)
        gAC.inc(sc.drain())
    P0_SILU = gAC.n

    seg_off = np.cumsum([0] + SEG_F).tolist()
    pieces = []
    for si, f in enumerate(SEG_F):
        o = 0
        while o < f:
            w = min(2048, f - o)
            pieces.append((seg_off[si] + o, w, si))
            o += w
    NP = len(pieces)

    @sec("sync")
    def _(sync):
        for pi, (fo, w, si) in enumerate(pieces):
            if pi >= 2:
                sync.wait_ge(dABS.h, pi - 1)
            dWSH.issue(sync.dma_start(out=(scrA if pi % 2 == 0 else scrB)[:, 0:w],
                                      in_=wsh[:, fo:fo + w]), pi)

    @sec("scalar")
    def _(sc):
        for pi, (fo, w, si) in enumerate(pieces):
            dWSH.wait(sc, pi)
            sc.activation(scrC[:, 0:w], (scrA if pi % 2 == 0 else scrB)[:, 0:w],
                          AF.Abs, accum_out=stats[:, 144 + pi:145 + pi])
            dABS.inc(sc.drain())

    @sec("tensor")
    def _(t):
        t.wait_ge(dABS.h, NP)
        t.wait_ge(gDV.h, P0_ONES)
        for si in range(6):
            cols = [pi for pi, (_, _, s2) in enumerate(pieces) if s2 == si]
            c0, n2 = min(cols), len(cols)
            t.matmul(pbig0[0:n2, 1024 + si:1025 + si],
                     stats[:, 144 + c0:144 + c0 + n2], ones[:, 0:1],
                     start=True, stop=True)
        gPE.inc(t.drain())
    P0_WMM1 = gPE.n

    @sec("vector")
    def _(v):
        v.wait_ge(gPE.h, P0_WMM1)
        for si in range(6):
            cols = [pi for pi, (_, _, s2) in enumerate(pieces) if s2 == si]
            n2 = len(cols)
            v.tensor_copy(small[0:n2, 1100 + si:1101 + si],
                          pbig0[0:n2, 1024 + si:1025 + si])
            v.drain()
        gDV.inc(v.nop())
    P0_WCP = gDV.n

    @sec("tensor")
    def _(t):
        t.wait_ge(gDV.h, P0_WCP)
        for si in range(6):
            cols = [pi for pi, (_, _, s2) in enumerate(pieces) if s2 == si]
            n2 = len(cols)
            t.matmul(pbig0[0:1, 1040 + si:1041 + si],
                     small[0:n2, 1100 + si:1101 + si], ones[0:n2, 0:1],
                     start=True, stop=True)
        gPE.inc(t.drain())
    P0_WMM2 = gPE.n

    @sec("vector")
    def _(v):
        v.wait_ge(gPE.h, P0_WMM2)
        v.tensor_copy(small[0:1, 1110:1116], pbig0[0:1, 1040:1046])
        v.drain()
        v.memset(small[0:1, 1116:1118], 0.0)
        gDV.inc(v.drain())
    P0_WROW = gDV.n

    @sec("gpsimd")
    def _(g):
        g.wait_ge(gDV.h, P0_WROW)
        gGD.inc(g.dma_start(out=wsum_in[:, 0:8], in_=small[0:1, 1110:1118]), 16)
        g.wait_ge(gGD.h, gGD.n)
        gCC.inc(g.collective_compute(
            "AllReduce", ALU.add, replica_groups=[list(range(8))],
            ins=[wsum_in[:, :].opt()], outs=[wsum_out[:, :].opt()]))
    P0_WAR = gCC.n

    @sec("sync")
    def _(sync):
        sync.wait_ge(gCC.h, P0_WAR)
        gIN.inc(sync.dma_start(out=small[0:1, 1120:1128], in_=wsum_out[:, :]), 16)
    P0_WLD = gIN.n

    @sec("vector")
    def _(v):
        v.wait_ge(gIN.h, P0_WLD)
        for si, nm in enumerate(SEGS):
            v.tensor_scalar(small[0:1, 1130 + si:1131 + si],
                            small[0:1, 1120 + si:1121 + si],
                            1.0 / SIZES[nm], QEPS, ALU.mult, ALU.max)
            v.drain()
        v.tensor_scalar_mul(small[0:1, 1140:1143], small[0:1, 1130:1133],
                            2.0 / 127.0)
        v.drain()
        v.tensor_scalar_mul(small[0:1, 1143:1146], small[0:1, 1133:1136], 0.5)
        v.drain()
        v.tensor_scalar_mul(small[0:1, 1146:1149], small[0:1, 1133:1136],
                            -1.0 / 127.0)
        gDV.inc(v.drain())
    P0_KROW = gDV.n

    @sec("tensor")
    def _(t):
        t.wait_ge(gDV.h, P0_KROW)
        t.matmul(pbig0[:, 1056:1065], ones[0:1, :], small[0:1, 1140:1149],
                 start=True, stop=True)
        gPE.inc(t.drain())
    P0_KMM = gPE.n

    @sec("vector")
    def _(v):
        v.wait_ge(gPE.h, P0_KMM)
        v.tensor_copy(kcol[:, 0:9], pbig0[:, 1056:1065])
        v.drain()
        v.tensor_scalar_mul(kcol[:, 9:12], kcol[:, 3:6], -1.0)
        gDV.inc(v.drain())
    P0_KCOL = gDV.n

    @sec("sync")
    def _(sync):
        sync.wait_ge(dABS.h, NP)
        for ic in range(CCH):
            if ic >= 2:
                sync.wait_ge(dMDM.h, ic - 1)
            dMOD.issue(sync.dma_start(
                out=(scrA if ic % 2 == 0 else scrB)[:, 0:1536],
                in_=wmod_sh[ic * 128:(ic + 1) * 128, :]), ic)

    @sec("tensor")
    def _(t):
        t.wait_ge(gAC.h, P0_SILU)
        cs = small[:, 1792:1856].rearrange("p (a b) -> p a b", a=16)
        for ic in range(CCH):
            dMOD.wait(t, ic)
            if ic >= 1:
                t.wait_ge(dVA0.h, ic)
            wt = (scrA if ic % 2 == 0 else scrB)[:, 0:1536].rearrange(
                "p (j o) -> p j o", j=12)
            for jt in range(12):
                t.matmul(pbig0[:, 1536 + jt * 4:1540 + jt * 4], wt[:, jt, :],
                         cs[:, ic, :], start=True, stop=True)
            dMDM.inc(t.drain())

    @sec("vector")
    def _(v):
        v.memset(stats[:, 64:112], 0.0)
        v.drain()
        for ic in range(CCH):
            v.wait_ge(dMDM.h, ic + 1)
            v.tensor_add(stats[:, 64:112], stats[:, 64:112], pbig0[:, 1536:1584])
            dVA0.inc(v.drain())
        gDV.inc(v.nop())
    P0_MACC = gDV.n

    @sec("gpsimd")
    def _(g):
        g.wait_ge(gDV.h, P0_MACC)
        gGD.inc(g.dma_start(
            out=mod_in[:, :].rearrange("(jt jl) b -> jl jt b", jl=128),
            in_=stats[:, 64:112].rearrange("p (jt b) -> p jt b", jt=12)), 16)
        g.wait_ge(gGD.h, gGD.n)
        gCC.inc(g.collective_compute(
            "AllGather", ALU.bypass, replica_groups=[list(range(8))],
            ins=[mod_in[:, :].opt()], outs=[mod_out[:, :].opt()]))
    P0_MAG = gCC.n

    @sec("sync")
    def _(sync):
        sync.wait_ge(gCC.h, P0_MAG)
        gIN.inc(sync.dma_start(
            out=small[:, 1024:1408].rearrange("p (s b) -> p s b", s=96),
            in_=mod_out[:, :].rearrange("(s p) b -> p s b", p=128)), 16)
    P0_MLD = gIN.n

    @sec("vector")
    def _(v):
        v.wait_ge(gIN.h, P0_MLD)
        mo = small[:, 1024:1408].rearrange("p (s b) -> p s b", s=96)
        v.tensor_scalar_mul(modcol[:, :], mo[:, :, 0], bhot[:, 0:1])
        v.drain()
        for b2 in range(1, 4):
            v.scalar_tensor_tensor(modcol[:, :], mo[:, :, b2], bhot[:, b2:b2 + 1],
                                   modcol[:, :], ALU.mult, ALU.add)
            v.drain()
        for s0 in (32, 80):
            sl = modcol[:, s0:s0 + 16]
            v.tensor_scalar(stats[:, 176:192], sl, 0.0, None, ALU.is_ge)
            v.drain()
            v.tensor_scalar(stats[:, 176:192], stats[:, 176:192], 2.0, -1.0,
                            ALU.mult, ALU.add)
            v.drain()
            v.tensor_scalar_mul(stats[:, 192:208], sl, -1.0)
            v.drain()
            v.tensor_max(stats[:, 192:208], stats[:, 192:208], sl)
            v.drain()
            v.tensor_scalar_max(stats[:, 192:208], stats[:, 192:208], 0.25)
            v.drain()
            v.tensor_mul(sl, stats[:, 176:192], stats[:, 192:208])
            v.drain()
        for s0 in (16, 64):
            v.tensor_scalar_add(modcol[:, s0:s0 + 16], modcol[:, s0:s0 + 16], 1.0)
            v.drain()
        gDV.inc(v.nop())
    P0_MODC = gDV.n

    def emit_col2rep(cols_fn, G, dest_fn, pre=()):
        @sec("tensor")
        def _(t):
            for cc, vv in pre:
                t.wait_ge(cc.h, vv)
            t.wait_ge(gDV.h, gDV.n)
            t.transpose(pbig1[0:G, 1536:1664], cols_fn(), ident[:, :])
            gPE.inc(t.drain())
        m1 = gPE.n

        @sec("vector")
        def _(v):
            v.wait_ge(gPE.h, m1)
            v.tensor_copy(colT[0:G, :], pbig1[0:G, 1536:1664])
            gDV.inc(v.drain())
        m2 = gDV.n

        @sec("sync")
        def _(sync):
            sync.wait_ge(gDV.h, m2)
            gIN.inc(sync.dma_start(out=small[0:1, 0:G * 128],
                                   in_=colT[0:G, :]), 16)
        m3 = gIN.n

        @sec("tensor")
        def _(t):
            t.wait_ge(gIN.h, m3)
            for j in range(0, G * 128, 512):
                w = min(512, G * 128 - j)
                t.matmul(pbig1[:, j:j + w], ones[0:1, :], small[0:1, j:j + w],
                         start=True, stop=True)
            gPE.inc(t.drain())
        m4 = gPE.n

        @sec("vector")
        def _(v):
            v.wait_ge(gPE.h, m4)
            v.tensor_copy(dest_fn(), pbig1[:, 0:G * 128])
            gDV.inc(v.drain())
        return gDV.n

    @sec("sync")
    def _(sync):
        sync.wait_ge(dMDM.h, CCH)
        for ic in range(CCH):
            if ic >= 2:
                sync.wait_ge(dMM1.h, ic - 1)
            dX1.issue(sync.dma_start(
                out=(scrA if ic % 2 == 0 else scrB)[:, 0:1024],
                in_=xT[ic * 128:(ic + 1) * 128, :]), ic)

    @sec("scalar")
    def _(sc):
        for ic in range(CCH):
            dX1.wait(sc, ic)
            if ic >= 1:
                sc.wait_ge(dMM1.h, ic)
            sc.activation(scrC[:, 0:1024],
                          (scrA if ic % 2 == 0 else scrB)[:, 0:1024], AF.Square)
            dSQ1.inc(sc.drain())

    @sec("tensor")
    def _(t):
        t.wait_ge(dVA0.h, CCH)
        for ic in range(CCH):
            t.wait_ge(dSQ1.h, ic + 1)
            if ic >= 1:
                t.wait_ge(dVA1.h, ic)
            xin_t = (scrA if ic % 2 == 0 else scrB)
            for tg in range(8):
                t.matmul(pbig0[:, 1536 + tg:1537 + tg],
                         xin_t[:, tg * 128:(tg + 1) * 128], ones[:, 0:1],
                         start=True, stop=True)
                t.matmul(pbig0[:, 1544 + tg:1545 + tg],
                         scrC[:, tg * 128:(tg + 1) * 128], ones[:, 0:1],
                         start=True, stop=True)
            dMM1.inc(t.drain())

    @sec("vector")
    def _(v):
        v.memset(stats[:, 0:16], 0.0)
        v.drain()
        for ic in range(CCH):
            v.wait_ge(dMM1.h, ic + 1)
            v.tensor_add(stats[:, 0:16], stats[:, 0:16], pbig0[:, 1536:1552])
            dVA1.inc(v.drain())
        v.tensor_scalar_mul(stats[:, 32:40], stats[:, 0:8], 1.0 / C)
        v.drain()
        v.tensor_scalar_mul(stats[:, 40:48], stats[:, 8:16], 1.0 / C)
        v.drain()
        v.tensor_mul(stats[:, 208:216], stats[:, 32:40], stats[:, 32:40])
        v.drain()
        v.tensor_sub(stats[:, 40:48], stats[:, 40:48], stats[:, 208:216])
        v.drain()
        v.tensor_scalar_add(stats[:, 40:48], stats[:, 40:48], LN_EPS)
        gDV.inc(v.drain())
    P1_VAR = gDV.n

    @sec("scalar")
    def _(sc):
        sc.wait_ge(gDV.h, P1_VAR)
        sc.activation(stats[:, 40:48], stats[:, 40:48], AF.Sqrt)
        gAC.inc(sc.drain())
    P1_SQRT = gAC.n

    @sec("vector")
    def _(v):
        v.wait_ge(gAC.h, P1_SQRT)
        v.reciprocal(stats[:, 40:48], stats[:, 40:48])
        gDV.inc(v.drain())

    emit_col2rep(lambda: stats[:, 32:40], 8, lambda: bufK[:, 0:1024])
    P1_REPS = emit_col2rep(lambda: stats[:, 40:48], 8, lambda: bufV[:, 0:1024])

    @sec("vector")
    def _(v):
        v.wait_ge(gDV.h, P1_REPS)
        v.memset(bufQ[:, 0:1024], -1e30)
        v.memset(ex[:, 0:1024], 1e30)
        gDV.inc(v.drain())
    P1B_ACC0 = gDV.n

    @sec("sync")
    def _(sync):
        for ic in range(CCH):
            if ic >= 2:
                sync.wait_ge(dHB.h, 3 * (ic - 1))
            dXB.issue(sync.dma_start(
                out=(scrA if ic % 2 == 0 else scrB)[:, 0:1024],
                in_=xT[ic * 128:(ic + 1) * 128, :]), ic)

    @sec("gpsimd")
    def _(g):
        g.wait_ge(gDV.h, P1B_ACC0)
        for ic in range(CCH):
            dXB.wait(g, ic)
            if ic >= 1:
                dSPB.wait(g, ic - 1)
            xin_t = (scrA if ic % 2 == 0 else scrB)
            g.tensor_sub(scrC[:, 0:1024], xin_t[:, 0:1024], bufK[:, 0:1024])
            g.drain()
            g.tensor_mul(scrC[:, 0:1024], scrC[:, 0:1024], bufV[:, 0:1024])
            g.drain()
            g.tensor_scalar(scrC[:, 0:1024], scrC[:, 0:1024],
                            modcol[:, 16 + ic:17 + ic],
                            modcol[:, ic:ic + 1], ALU.mult, ALU.add)
            g.drain()
            dHB.inc(g.nop(), 3)

    @sec("vector")
    def _(v):
        for ic in range(CCH):
            v.wait_ge(dHB.h, 3 * (ic + 1))
            v.tensor_max(bufQ[:, 0:1024], bufQ[:, 0:1024], scrC[:, 0:1024])
            v.drain()
            v.tensor_tensor(ex[:, 0:1024], ex[:, 0:1024], scrC[:, 0:1024],
                            ALU.min)
            dACB.inc(v.drain())

    @sec("scalar")
    def _(sc):
        for ic in range(CCH):
            sc.wait_ge(dACB.h, ic + 1)
            dSPB.issue(sc.dma_start(out=hbuf[ic * 128:(ic + 1) * 128, :],
                                    in_=scrC[:, 0:1024]), ic)

    @sec("tensor")
    def _(t):
        t.wait_ge(dACB.h, CCH)
        for tg in range(8):
            t.transpose(pbig0[:, tg * 128:(tg + 1) * 128],
                        bufQ[:, tg * 128:(tg + 1) * 128], ident[:, :])
            t.transpose(pbig1[:, tg * 128:(tg + 1) * 128],
                        ex[:, tg * 128:(tg + 1) * 128], ident[:, :])
        gPE.inc(t.drain())
    P1B_TR = gPE.n

    @sec("vector")
    def _(v):
        v.wait_ge(gPE.h, P1B_TR)
        v.reduce_max(stats[:, 64:72],
                     pbig0[:, 0:1024].rearrange("p (a b) -> p a b", a=8),
                     axis=AX.X)
        v.drain()
        v.reduce_max(stats[:, 80:88],
                     pbig1[:, 0:1024].rearrange("p (a b) -> p a b", a=8),
                     axis=AX.X, negate=True)
        v.drain()
        v.tensor_max(stats[:, 64:72], stats[:, 64:72], stats[:, 80:88])
        v.drain()
        v.tensor_scalar_max(stats[:, 64:72], stats[:, 64:72], QEPS)
        v.drain()
        v.reciprocal(stats[:, 96:104], stats[:, 64:72])
        v.drain()
        v.tensor_scalar_mul(stats[:, 96:104], stats[:, 96:104], 127.0)
        gDV.inc(v.drain())

    emit_col2rep(lambda: stats[:, 64:72], 8, lambda: smaxrep[:, 0:1024])
    P1B_REPS = emit_col2rep(lambda: stats[:, 96:104], 8, lambda: bufK[:, 0:1024])

    @sec("sync")
    def _(sync):
        sync.wait_ge(gDV.h, P1B_REPS)
        dSPB.wait_all(sync)
        for ic in range(CCH):
            if ic >= 2:
                sync.wait_ge(dQC.h, ic - 1)
            dXC.issue(sync.dma_start(
                out=(scrA if ic % 2 == 0 else scrB)[:, 0:1024],
                in_=hbuf[ic * 128:(ic + 1) * 128, :]), ic)

    @sec("gpsimd")
    def _(g):
        for ic in range(CCH):
            dXC.wait(g, ic)
            if ic >= 1:
                g.wait_ge(dHQ.h, ic)
            xin_t = (scrA if ic % 2 == 0 else scrB)
            g.tensor_mul(scrC[:, 0:1024], xin_t[:, 0:1024], bufK[:, 0:1024])
            g.drain()
            g.tensor_scalar(t16[:, :], scrC[:, 0:1024], 1.0, 256.5,
                            ALU.mult, ALU.add)
            dQC.inc(g.drain())

    @sec("scalar")
    def _(sc):
        for ic in range(CCH):
            sc.wait_ge(dQC.h, ic + 1)
            sc.activation(hqT[:, ic * 1024:(ic + 1) * 1024], t16[:, :],
                          AF.Copy, bias=-256.0)
            dHQ.inc(sc.drain())
    P1C_DONE = dHQ.n

    @sec("sync")
    def _(sync):
        sync.wait_ge(dQC.h, CCH)
        for hh in range(NH):
            for p4 in range(4):
                k = hh * 4 + p4
                if k >= 2:
                    sync.wait_ge(dSG2.h, k - 1)
                st = scrA if k % 2 == 0 else scrB
                dWL2.issue(sync.dma_start(
                    out=st[:, :].rearrange("p (a b) -> p a b", a=4),
                    in_=wqkvgT[:, hh * 512:(hh + 1) * 512].rearrange(
                        "(q p) o -> p q o", p=128)[:, p4 * 4:(p4 + 1) * 4, :]),
                    k)

    def gp_sign(g, hh):
        for p4 in range(4):
            k = hh * 4 + p4
            dWL2.wait(g, k)
            if hh >= 2 and p4 == 0:
                g.wait_ge(dMMq.h, 4 * (hh - 1))
            st = scrA if k % 2 == 0 else scrB
            dst = (AR2[:, 0:4096] if hh % 2 == 0 else
                   AR2[:, 4096:8192]).bitcast(BF16)
            g.tensor_scalar(dst[:, p4 * 2048:(p4 + 1) * 2048],
                            st[:, :], 0.0, 0.5, ALU.is_gt, ALU.subtract)
            dSG2.inc(g.drain())

    @sec("gpsimd")
    def _(g):
        gp_sign(g, 0)
        gp_sign(g, 1)

    hq3 = hqT.rearrange("p (a b) -> p a b", a=16)
    G0 = gAC.n
    GP0 = gGP.n
    dSPQ = Bank([dWSH.c[0], dWSH.c[1], dABS])
    dKVS = Bank([dMOD.c[0], dMOD.c[1]])
    dTIN = Bank([dMDM, dVA0])

    for hh in range(NH):
        par = hh % 2
        wv = wqg[par]
        sgn_ready = (hh * 4 + 4)

        @sec("tensor")
        def _(t, hh=hh, wv=wv, sgn_ready=sgn_ready):
            t.wait_ge(dSG2.h, sgn_ready)
            if hh == 0:
                t.wait_ge(dHQ.h, P1C_DONE)
                t.wait_ge(gDV.h, P0_KCOL)
                t.wait_ge(gDV.h, P0_MODC)
            else:
                t.wait_ge(dCHd.h, 24 * hh)
                t.wait_ge(dDQq.h, 4 * hh)
            for ti in range(4):
                ptile = pbig0 if ti < 2 else pbig1
                off = 1024 * (ti % 2)
                for kc in range(CCH):
                    lhs = wv[:, kc, ti * 128:(ti + 1) * 128]
                    for h5 in range(2):
                        t.matmul(ptile[:, off + h5 * 512:off + (h5 + 1) * 512],
                                 lhs, hq3[:, kc, h5 * 512:(h5 + 1) * 512],
                                 start=(kc == 0), stop=(kc == CCH - 1))
                dMMq.inc(t.drain())

        @sec("vector")
        def _(v, hh=hh):
            base = 4 * hh
            for ti, dst in ((0, bufQ), (1, bufK), (2, bufV)):
                v.wait_ge(dMMq.h, base + ti + 1)
                if hh > 0:
                    dSPQ.wait(v, 3 * (hh - 1) + ti)
                    if ti >= 1:
                        v.wait_ge(dCHp.h, 16 * hh)
                ptile = pbig0 if ti < 2 else pbig1
                off = 1024 * (ti % 2)
                v.tensor_mul(dst[:, 0:1024], ptile[:, off:off + 1024],
                             smaxrep[:, 0:1024])
                v.drain()
                v.tensor_scalar(dst[:, 0:1024], dst[:, 0:1024], kcol[:, 0:1],
                                bqg[:, hh * 4 + ti:hh * 4 + ti + 1],
                                ALU.mult, ALU.add)
                dDQq.inc(v.drain())
            v.wait_ge(dMMq.h, base + 4)
            if hh > 0:
                v.wait_ge(gAC.h, G0 + hh)
            v.tensor_mul(ex[:, 0:1024], pbig1[:, 1024:2048], smaxrep[:, 0:1024])
            dDQq.inc(v.drain())

        @sec("scalar")
        def _(sc, hh=hh):
            for ti, src in ((0, bufQ), (1, bufK), (2, bufV)):
                sc.wait_ge(dDQq.h, 4 * hh + ti + 1)
                dst = (qbuf, kbuf, vbuf)[ti]
                dSPQ.issue(sc.dma_start(
                    out=dst[:, hh * 1024:(hh + 1) * 1024],
                    in_=src[:, 0:1024]), 3 * hh + ti)
            sc.wait_ge(dDQq.h, 4 * hh + 4)
            sc.activation(ex[:, 0:1024], ex[:, 0:1024], AF.Sigmoid,
                          bias=bqg[:, hh * 4 + 3:hh * 4 + 4],
                          scale=kcol[:, 1:2])
            gAC.inc(sc.drain())

        @sec("vector")
        def _(v, hh=hh):
            v.wait_ge(gAC.h, G0 + hh + 1)
            v.reduce_sum(dcy[:, hh * 8:(hh + 1) * 8],
                         ex[:, 0:1024].rearrange("p (a b) -> p a b", a=8),
                         axis=AX.X)
            v.drain()
            v.tensor_scalar_mul(dcy[:, hh * 8:(hh + 1) * 8],
                                dcy[:, hh * 8:(hh + 1) * 8], 1.0 / 128.0)
            dDCY.inc(v.drain())

        @sec("tensor")
        def _(t, hh=hh):
            t.wait_ge(dDQq.h, 4 * hh + 3)
            for ci in range(NCH):
                k8 = 8 * hh + ci
                cp = ci % 2
                if k8 >= 2:
                    t.wait_ge(dCHd.h, 3 * (k8 - 2) + 1)
                t.transpose(pbig0[:, 256 * cp:256 * cp + 128],
                            bufK[:, ci * 128:(ci + 1) * 128], ident[:, :])
                t.transpose(pbig0[:, 256 * cp + 128:256 * cp + 256],
                            bufV[:, ci * 128:(ci + 1) * 128], ident[:, :])
                dCHp.inc(t.drain())
                t.wait_ge(dCHd.h, 3 * k8 + 1)
                if k8 >= 2:
                    t.wait_ge(dCHd.h, 3 * (k8 - 2) + 3)
                t.matmul(pbig0[:, 1024 + 128 * cp:1152 + 128 * cp],
                         ktvt[cp][:, 0:128], ktvt[cp][:, 128:256],
                         start=True, stop=True)
                dCHp.inc(t.drain())

        @sec("vector")
        def _(v, hh=hh):
            for ci in range(NCH):
                k8 = 8 * hh + ci
                cp = ci % 2
                v.wait_ge(dCHp.h, 2 * k8 + 1)
                if k8 >= 2:
                    v.wait_ge(dCHp.h, 2 * (k8 - 2) + 2)
                v.tensor_copy(ktvt[cp][:, :], pbig0[:, 256 * cp:256 * cp + 256])
                dCHd.inc(v.drain())
                v.wait_ge(dCHp.h, 2 * k8 + 2)
                if k8 >= 2:
                    v.wait_ge(gGP.h, GP0 + k8 - 1)
                    dKVS.wait(v, k8 - 2)
                v.tensor_copy(kvt[cp][:, :],
                              pbig0[:, 1024 + 128 * cp:1152 + 128 * cp])
                dCHd.inc(v.drain(), 2)

        @sec("gpsimd")
        def _(g, hh=hh):
            g.wait_ge(dDCY.h, hh + 1)
            if hh >= 2:
                dTIN.wait(g, hh - 2)
            slot = Tslot[hh % 2]
            g.memset(slot, 0.0)
            g.drain()
            for ci in range(NCH):
                k8 = 8 * hh + ci
                g.wait_ge(dCHd.h, 3 * k8 + 3)
                g.tensor_scalar_mul(slot, slot, dcy[:, k8:k8 + 1])
                g.drain()
                g.tensor_add(slot, slot, kvt[ci % 2][:, :])
                gGP.inc(g.drain())
            if hh + 2 < NH:
                gp_sign(g, hh + 2)

        @sec("scalar")
        def _(sc, hh=hh):
            for ci in range(NCH):
                k8 = 8 * hh + ci
                sc.wait_ge(dCHd.h, 3 * k8 + 3)
                dKVS.issue(sc.dma_start(
                    out=kvbuf[:, k8 * 128:(k8 + 1) * 128],
                    in_=kvt[ci % 2][:, :]), k8)
            sc.wait_ge(gGP.h, GP0 + 8 * (hh + 1))
            dTIN.issue(sc.dma_start(out=T_in[:, hh * 128:(hh + 1) * 128],
                                    in_=Tslot[hh % 2]), hh)

    @sec("gpsimd")
    def _(g):
        dTIN.wait_all(g)
        gCC.inc(g.collective_compute(
            "AllGather", ALU.bypass,
            replica_groups=[[0, 1], [2, 3], [4, 5], [6, 7]],
            ins=[T_in[:, :].opt()], outs=[T_out[:, :].opt()]))
    P2_AG = gCC.n

    @sec("sync")
    def _(sync):
        sync.wait_ge(gCC.h, P2_AG)
        dTOL.inc(sync.dma_start(out=Ssl[:, :], in_=T_out[0:128, :]), 16)
    P2_TOL = dTOL.n

    @sec("gpsimd")
    def _(g):
        g.wait_ge(dTOL.h, P2_TOL)
        g.tensor_scalar_mul(Ssl[:, :], Ssl[:, :], mselc[:, 0:1])
        gGP.inc(g.drain())
    P2_SINIT = gGP.n

    def chbufs(ci):
        return (bufQ, bufK, bufV) if ci % 2 == 0 else (scrA, scrB, scrC)

    ath = [attnT[:, 0:1024], attnT[:, 1024:2048]]

    dQCL = Bank([dXB.c[0], dXB.c[1], dSPB.c[0], dSPB.c[1],
                 dXC.c[0], dXC.c[1]])
    dKVL = Bank([dX1.c[0], dX1.c[1]])
    dMRW = Bank([dSQ1, dMM1])
    bdP = [dWL2.c[0], dWL2.c[1]]
    bdB = [c.n for c in bdP]
    CHB = dCHp.n
    DHB = dCHd.n
    mtsl = [xtmp[0:64, 2816:2832], xtmp[0:64, 2832:2848]]
    pmt = pbig0[0:64, 1984:2000]

    @sec("vector")
    def _(v):
        v.wait_ge(dMMq.h, 64)
        v.memset(bd[0][:, :], 0.0)
        v.drain()
        v.memset(bd[1][:, :], 0.0)
        gDV.inc(v.drain())
    P2_BD0 = gDV.n

    @sec("sync")
    def _(sync):
        sync.wait_ge(gDV.h, P2_BD0)
        for ci in range(NCH):
            if ci == 0:
                dSPQ.wait_all(sync)
            if ci >= 2:
                sync.wait_ge(dSC.h, 10 * (ci - 1))
                sync.wait_ge(dGRD.h, 48 * (ci - 1))
                sync.wait_ge(dREC.h, 16 * (ci - 1))
            qb, kb, vb = chbufs(ci)
            for j3, (srcb, dstb) in enumerate(((qbuf, qb), (kbuf, kb),
                                               (vbuf, vb))):
                dQCL.issue(sync.dma_start(
                    out=dstb[:, 0:2048].rearrange("p (a b) -> p a b", a=16),
                    in_=srcb[:, :].rearrange("p (a b) -> p a b", a=16)[
                        :, :, ci * 128:(ci + 1) * 128]), 3 * ci + j3)
            for half in range(2):
                H = 2 * ci + half
                sync.wait_ge(dCHd.h, DHB + H + 1)
                if H >= 2:
                    sync.wait_ge(dSC.h, 5 * (H - 2) + 3)
                dMRW.issue(sync.dma_start(
                    out=Mrow[half][:, :], in_=mtsl[half][:, :]), H)
            for half in range(2):
                H = 2 * ci + half
                sync.wait_ge(dDN.h, 5 * H + 5)
                for j in range(8 * half, 8 * half + 8):
                    Gn = 16 * ci + j
                    if Gn >= 2:
                        sync.wait_ge(dGRP.h, 2 * (Gn - 2) + 2)
                    for l4 in range(8):
                        l = 8 * j + l4
                        bdP[j % 2].inc(sync.dma_start(
                            out=bd[j % 2][16 * l4:16 * l4 + 16,
                                          16 * l4:16 * l4 + 16],
                            in_=attnT[:, l * 16:(l + 1) * 16]), 16)

    @sec("scalar")
    def _(sc):
        for ci in range(NCH):
            for half in range(2):
                H = 2 * ci + half
                sc.wait_ge(dDN.h, 5 * H + 3)
                sc.activation(ath[half], ath[half], AF.Exp)
                gAC.inc(sc.drain())
            if ci == NCH - 1:
                continue
            if ci == 0:
                dKVS.wait_all(sc)
                sc.wait_ge(dMMq.h, 64)
            else:
                sc.wait_ge(gGP.h, P2_SINIT + 16 * ci)
            dKVL.issue(sc.dma_start(
                out=kvsl[:, :].rearrange("p (a b) -> p a b", a=16),
                in_=kvbuf[:, :].rearrange("p (a b) -> p a b", a=16)[
                    :, :, ci * 128:(ci + 1) * 128]), ci)

    ACB = gAC.n - 16

    @sec("tensor")
    def _(t):
        t.wait_ge(dCHd.h, 384)
        t.wait_ge(dDQq.h, 64)
        for ci in range(NCH):
            qb, kb, vb = chbufs(ci)
            qhl = qb[:, 0:2048].rearrange("p (a b) -> p a b", a=16)
            khl = kb[:, 0:2048].rearrange("p (a b) -> p a b", a=16)
            dQCL.wait(t, 3 * ci + 0)
            dQCL.wait(t, 3 * ci + 1)
            for half in range(2):
                H = 2 * ci + half
                if H >= 1:
                    t.wait_ge(dDN.h, 5 * (H - 1) + 1)
                for l in range(64 * half, 64 * half + 64):
                    lo = (l - 64 * half) * 16
                    t.matmul(psc[:, lo:lo + 16], qhl[:, :, l], khl[:, :, l],
                             start=True, stop=True)
                dSC.inc(t.drain())
                if H >= 1:
                    t.wait_ge(dDN.h, 5 * (H - 1) + 3)
                for l in range(64 * half, 64 * half + 64):
                    lo = (l - 64 * half) * 16
                    t.matmul(pscT[:, lo:lo + 16], khl[:, :, l], qhl[:, :, l],
                             start=True, stop=True)
                dSC.inc(t.drain())
                t.wait_ge(dDN.h, 5 * H + 1)
                if H >= 2:
                    t.wait_ge(dCHd.h, DHB + H - 1)
                t.transpose(pmt[:, :], Mst[half][:, 0:64], ident[0:16, 0:16])
                dCHp.inc(t.drain())
                dMRW.wait(t, H)
                if H >= 1:
                    t.wait_ge(dDN.h, 5 * (H - 1) + 5)
                for s5 in range(2):
                    t.matmul(pdn16[:, s5 * 512:(s5 + 1) * 512],
                             ones[32:33, 0:16],
                             Mrow[half][:, s5 * 512:(s5 + 1) * 512],
                             start=True, stop=True)
                dSC.inc(t.drain())
                t.wait_ge(gAC.h, ACB + H + 1)
                t.wait_ge(dDN.h, 5 * H + 2)
                for s5 in range(2):
                    t.matmul(pdn1[:, s5 * 512:(s5 + 1) * 512], ones[0:16, 0:1],
                             ath[half][:, s5 * 512:(s5 + 1) * 512],
                             start=True, stop=True)
                dSC.inc(t.drain())
                t.wait_ge(dDN.h, 5 * H + 4)
                for s5 in range(2):
                    t.matmul(pdn16[:, s5 * 512:(s5 + 1) * 512],
                             ones[64:65, 0:16],
                             rzrow[half][:, s5 * 512:(s5 + 1) * 512],
                             start=True, stop=True)
                dSC.inc(t.drain())
            for j in range(16):
                Gn = 16 * ci + j
                jp = j % 2
                t.wait_ge(dGRD.h, 3 * Gn + 1)
                if Gn >= 2:
                    t.wait_ge(dGRD.h, 3 * (Gn - 2) + 2)
                t.transpose(pv[jp][:, :], v2g[jp][:, :], ident[:, :])
                dGRP.inc(t.drain())
                t.wait_ge(dGRD.h, 3 * Gn + 2)
                t.wait_ge(bdP[Gn % 2].h,
                          bdB[Gn % 2] + 128 * (Gn // 2 + 1))
                if Gn >= 2:
                    t.wait_ge(dGRD.h, 3 * (Gn - 2) + 3)
                t.matmul(plo[jp][:, :], v2t[jp][:, :], bd[jp][:, :],
                         start=True, stop=True)
                dGRP.inc(t.drain())
            t.wait_ge(gGP.h, P2_SINIT + 16 * ci)
            for h in range(16):
                hp = h % 2
                if 16 * ci + h >= 2:
                    t.wait_ge(dRAD.h, 16 * ci + h - 1)
                t.matmul(prec[hp][:, :], Ssl[:, h * 128:(h + 1) * 128],
                         qb[:, h * 128:(h + 1) * 128], start=True, stop=True)
                dREC.inc(t.drain())

    @sec("vector")
    def _(v):
        for ci in range(NCH):
            qb, kb, vb = chbufs(ci)
            vlh = vb[:, 0:2048].rearrange("p (a b) -> p b a", a=16)
            for half in range(2):
                H = 2 * ci + half
                v.wait_ge(dSC.h, 5 * H + 1)
                if H >= 2:
                    v.wait_ge(dCHp.h, CHB + H - 1)
                v.reduce_max(Mst[half][:, 0:64],
                             psc[:, :].rearrange("p (a b) -> p a b", a=64),
                             axis=AX.X)
                dDN.inc(v.drain())
                v.wait_ge(dCHp.h, CHB + H + 1)
                if H >= 2:
                    dMRW.wait(v, H - 2)
                v.tensor_copy(mtsl[half][:, :], pmt[:, :])
                dCHd.inc(v.drain())
                v.wait_ge(dSC.h, 5 * H + 3)
                v.tensor_copy(Mrep_sb[half][:, :], pdn16[:, :])
                dDN.inc(v.drain())
                if H >= 2:
                    v.wait_ge(bdP[0].h, bdB[0] + 128 * 4 * (H - 1))
                    v.wait_ge(bdP[1].h, bdB[1] + 128 * 4 * (H - 1))
                v.scalar_tensor_tensor(ath[half], pscT[:, :], ATT_SCALE,
                                       Mrep_sb[half][:, :], ALU.mult,
                                       ALU.subtract)
                dDN.inc(v.drain())
                v.wait_ge(dSC.h, 5 * H + 4)
                if H >= 2:
                    v.wait_ge(dSC.h, 5 * (H - 2) + 5)
                v.reciprocal(rzrow[half][:, :], pdn1[:, :])
                dDN.inc(v.drain())
                v.wait_ge(dSC.h, 5 * H + 5)
                v.tensor_mul(ath[half], ath[half], pdn16[:, :])
                dDN.inc(v.drain())
            dQCL.wait(v, 3 * ci + 2)
            for j in range(16):
                Gn = 16 * ci + j
                jp = j % 2
                if Gn >= 2:
                    v.wait_ge(dGRP.h, 2 * (Gn - 2) + 1)
                v.tensor_copy(
                    v2g[jp][:, :].rearrange("p (a b) -> p a b", a=8),
                    vlh[:, 8 * j:8 * j + 8, :])
                dGRD.inc(v.drain())
                v.wait_ge(dGRP.h, 2 * Gn + 1)
                v.tensor_copy(v2t[jp][:, :], pv[jp][:, :])
                dGRD.inc(v.drain())
                v.wait_ge(dGRP.h, 2 * Gn + 2)
                v.tensor_copy(
                    olh[:, ci * 128 + 8 * j:ci * 128 + 8 * j + 8, :],
                    plo[jp][:, :].rearrange("p (a b) -> p a b", a=8))
                dGRD.inc(v.drain())
            for h in range(16):
                v.wait_ge(dREC.h, 16 * ci + h + 1)
                v.tensor_add(outf[:, h, ci * 128:(ci + 1) * 128],
                             outf[:, h, ci * 128:(ci + 1) * 128],
                             prec[h % 2][:, :])
                dRAD.inc(v.drain())

    @sec("gpsimd")
    def _(g):
        for ci in range(NCH - 1):
            dKVL.wait(g, ci)
            g.wait_ge(dREC.h, 16 * (ci + 1))
            for h in range(16):
                g.tensor_scalar_mul(Ssl[:, h * 128:(h + 1) * 128],
                                    Ssl[:, h * 128:(h + 1) * 128],
                                    dcy[:, h * 8 + ci:h * 8 + ci + 1])
                g.drain()
                g.tensor_add(Ssl[:, h * 128:(h + 1) * 128],
                             Ssl[:, h * 128:(h + 1) * 128],
                             kvsl[:, h * 128:(h + 1) * 128])
                g.drain()
            gGP.inc(g.nop(), 16)
    P2_OUTF = 128

    def emit_projload(sync, ot):
        dPJL.issue(sync.dma_start(
            out=(scrA if ot % 2 == 0 else scrB)[:, :].rearrange(
                "p (a b) -> p a b", a=16),
            in_=wprojT[:, ot * 128:(ot + 1) * 128].rearrange(
                "(a p) o -> p a o", p=128)), ot)

    @sec("sync")
    def _(sync):
        sync.wait_ge(dREC.h, 128)
        sync.wait_ge(dGRD.h, 384)
        emit_projload(sync, 0)
        emit_projload(sync, 1)

    @sec("gpsimd")
    def _(g):
        for ot in range(CCH):
            dPJL.wait(g, ot)
            if ot < 2:
                g.wait_ge(dMMq.h, 64)
            if ot >= 2:
                g.wait_ge(dPJM.h, ot - 1)
            st = scrA if ot % 2 == 0 else scrB
            par = ot % 2
            dst = (AR2[:, 0:2816] if par == 0 else AR2[:, 2816:5632]).bitcast(BF16)
            g.tensor_scalar(dst[:, 0:2048], st[:, :], 0.0, 0.5,
                            ALU.is_gt, ALU.subtract)
            dPJS.inc(g.drain())

    @sec("scalar")
    def _(sc):
        sc.wait_ge(dRAD.h, P2_OUTF)
        for fc in range(CCH):
            if fc >= 1:
                sc.wait_ge(dPJD.h, fc)
            sc.activation(scrC[:, 0:1024], outf[:, fc, :], AF.Square)
            dLN2.inc(sc.drain())

    @sec("tensor")
    def _(t):
        for fc in range(CCH):
            t.wait_ge(dLN2.h, fc + 1)
            if fc >= 1:
                t.wait_ge(dVA2.h, fc)
            for tg in range(8):
                t.matmul(pbig1[:, 1536 + tg:1537 + tg],
                         scrC[:, tg * 128:(tg + 1) * 128], ones[:, 0:1],
                         start=True, stop=True)
            dPJD.inc(t.drain())
    RMS_MM = dPJD.n

    @sec("vector")
    def _(v):
        v.memset(stats[:, 0:8], 0.0)
        v.drain()
        for fc in range(CCH):
            v.wait_ge(dPJD.h, fc + 1)
            v.tensor_add(stats[:, 0:8], stats[:, 0:8], pbig1[:, 1536:1544])
            dVA2.inc(v.drain())
        v.tensor_scalar(stats[:, 8:16], stats[:, 0:8], 1.0 / C, RMS_EPS,
                        ALU.mult, ALU.add)
        gDV.inc(v.drain())
    P4_MS = gDV.n

    @sec("scalar")
    def _(sc):
        sc.wait_ge(gDV.h, P4_MS)
        sc.activation(stats[:, 8:16], stats[:, 8:16], AF.Sqrt)
        gAC.inc(sc.drain())
    P4_SQ = gAC.n

    @sec("vector")
    def _(v):
        v.wait_ge(gAC.h, P4_SQ)
        v.reciprocal(stats[:, 8:16], stats[:, 8:16])
        gDV.inc(v.drain())

    emit_col2rep(lambda: stats[:, 8:16], 8, lambda: bufV[:, 0:1024])

    @sec("vector")
    def _(v):
        v.wait_ge(gDV.h, gDV.n)
        v.memset(bufQ[:, 0:1024], -1e30)
        v.memset(bufQ[:, 1024:2048], 1e30)
        v.drain()
        for fc in range(CCH):
            v.tensor_mul(scrC[:, 0:1024], outf[:, fc, :], bufV[:, 0:1024])
            v.drain()
            v.tensor_scalar_mul(scrC[:, 0:1024], scrC[:, 0:1024],
                                nwc[:, fc:fc + 1])
            v.drain()
            v.tensor_max(bufQ[:, 0:1024], bufQ[:, 0:1024], scrC[:, 0:1024])
            v.drain()
            v.tensor_tensor(bufQ[:, 1024:2048], bufQ[:, 1024:2048],
                            scrC[:, 0:1024], ALU.min)
            v.drain()
        gDV.inc(v.nop())
    P4_ACC = gDV.n

    @sec("tensor")
    def _(t):
        t.wait_ge(gDV.h, P4_ACC)
        for tg in range(8):
            t.transpose(pbig1[:, tg * 128:(tg + 1) * 128],
                        bufQ[:, tg * 128:(tg + 1) * 128], ident[:, :])
            t.transpose(pbig1[:, 1024 + tg * 128:1024 + (tg + 1) * 128],
                        bufQ[:, 1024 + tg * 128:1024 + (tg + 1) * 128],
                        ident[:, :])
        gPE.inc(t.drain())
    P4_TR = gPE.n

    @sec("vector")
    def _(v):
        v.wait_ge(gPE.h, P4_TR)
        v.reduce_max(stats[:, 16:24],
                     pbig1[:, 0:1024].rearrange("p (a b) -> p a b", a=8),
                     axis=AX.X)
        v.drain()
        v.reduce_max(stats[:, 24:32],
                     pbig1[:, 1024:2048].rearrange("p (a b) -> p a b", a=8),
                     axis=AX.X, negate=True)
        v.drain()
        v.tensor_max(stats[:, 16:24], stats[:, 16:24], stats[:, 24:32])
        v.drain()
        v.tensor_scalar_max(stats[:, 16:24], stats[:, 16:24], QEPS)
        v.drain()
        v.reciprocal(stats[:, 24:32], stats[:, 16:24])
        v.drain()
        v.tensor_scalar_mul(stats[:, 24:32], stats[:, 24:32], 127.0)
        gDV.inc(v.drain())

    emit_col2rep(lambda: stats[:, 16:24], 8, lambda: bufK[:, 1024:2048])
    P4_REP2 = emit_col2rep(lambda: stats[:, 24:32], 8, lambda: bufK[:, 0:1024])

    @sec("vector")
    def _(v):
        v.wait_ge(gDV.h, P4_REP2)
        v.tensor_mul(bufV[:, 0:1024], bufV[:, 0:1024], bufK[:, 0:1024])
        gDV.inc(v.drain())
    P4_COMB = gDV.n

    @sec("vector")
    def _(v):
        v.wait_ge(gDV.h, P4_COMB)
        for fc in range(CCH):
            if fc >= 1:
                v.wait_ge(dPQB.h, fc)
            v.tensor_mul(scrC[:, 0:1024], outf[:, fc, :], bufV[:, 0:1024])
            v.drain()
            v.tensor_scalar(t16[:, 0:1024], scrC[:, 0:1024], nwc[:, fc:fc + 1],
                            256.5, ALU.mult, ALU.add)
            dPQA.inc(v.drain())

    @sec("scalar")
    def _(sc):
        for fc in range(CCH):
            sc.wait_ge(dPQA.h, fc + 1)
            sc.activation(pjq[:, fc, :], t16[:, 0:1024], AF.Copy, bias=-256.0)
            dPQB.inc(sc.drain())
    P4_PJQ = dPQB.n

    @sec("sync")
    def _(sync):
        sync.wait_ge(dPQB.h, P4_PJQ)
        for ot in range(CCH):
            if ot >= 2:
                dXNW.wait(sync, ot - 2)
            dst = bufK[:, 0:1024] if ot % 2 == 0 else bufV[:, 1024:2048]
            dX2L.issue(sync.dma_start(out=dst,
                                      in_=xT[ot * 128:(ot + 1) * 128, :]), ot)
            if ot + 2 < CCH:
                sync.wait_ge(dPJS.h, ot + 1)
                emit_projload(sync, ot + 2)

    def p4_statmm(t, j):
        t.wait_ge(dH2.h, j + 1)
        if j >= 1:
            t.wait_ge(dVA3.h, j)
        for tg in range(8):
            t.matmul(pbig1[:, 1536 + tg:1537 + tg],
                     scrC[:, tg * 128:(tg + 1) * 128], ones[:, 0:1],
                     start=True, stop=True)
            t.matmul(pbig1[:, 1544 + tg:1545 + tg],
                     small[:, 768 + tg * 128:768 + (tg + 1) * 128],
                     ones[:, 0:1], start=True, stop=True)
        dH2A.inc(t.drain())

    @sec("tensor")
    def _(t):
        t.wait_ge(dPQB.h, P4_PJQ)
        for ot in range(CCH):
            t.wait_ge(dPJS.h, ot + 1)
            if ot >= 2:
                t.wait_ge(dPJD.h, RMS_MM + ot - 1)
            par = ot % 2
            ptile = pbig0[:, 0:1024] if par == 0 else pbig0[:, 1024:2048]
            for kc in range(CCH):
                lhs = (AR2[:, 0:2816] if par == 0 else
                       AR2[:, 2816:5632]).bitcast(BF16)[:, kc * 128:(kc + 1) * 128]
                for h5 in range(2):
                    t.matmul(ptile[:, h5 * 512:(h5 + 1) * 512], lhs,
                             pjq[:, kc, h5 * 512:(h5 + 1) * 512],
                             start=(kc == 0), stop=(kc == CCH - 1))
            dPJM.inc(t.drain())
            if ot >= 2:
                p4_statmm(t, ot - 2)
        p4_statmm(t, CCH - 2)
        p4_statmm(t, CCH - 1)

    def p4_statacc(v, j):
        v.wait_ge(dH2A.h, j + 1)
        v.tensor_add(stats[:, 32:48], stats[:, 32:48], pbig1[:, 1536:1552])
        dVA3.inc(v.drain())

    @sec("vector")
    def _(v):
        v.memset(stats[:, 32:48], 0.0)
        v.drain()
        for ot in range(CCH):
            if ot >= 2:
                p4_statacc(v, ot - 2)
            v.wait_ge(dPJM.h, ot + 1)
            par = ot % 2
            ptile = pbig0[:, 0:1024] if par == 0 else pbig0[:, 1024:2048]
            v.tensor_mul(scrC[:, 1024:2048], ptile, bufK[:, 1024:2048])
            v.drain()
            v.tensor_scalar(scrC[:, 1024:2048], scrC[:, 1024:2048],
                            kcol[:, 2:3], bpj[:, ot:ot + 1], ALU.mult, ALU.add)
            dPJD.inc(v.drain())
            dX2L.wait(v, ot)
            if ot >= 1:
                v.wait_ge(dH2.h, ot)
                v.wait_ge(dH2A.h, ot)
            x2t = bufK[:, 0:1024] if ot % 2 == 0 else bufV[:, 1024:2048]
            v.scalar_tensor_tensor(scrC[:, 0:1024], scrC[:, 1024:2048],
                                   modcol[:, 32 + ot:33 + ot], x2t,
                                   ALU.mult, ALU.add)
            dXN2.inc(v.drain())
        p4_statacc(v, CCH - 2)
        p4_statacc(v, CCH - 1)

    @sec("scalar")
    def _(sc):
        for ot in range(CCH):
            sc.wait_ge(dXN2.h, ot + 1)
            dXNW.issue(sc.dma_start(out=xnewbuf[ot * 128:(ot + 1) * 128, :],
                                    in_=scrC[:, 0:1024]), ot)
            dXNW.wait(sc, ot)
            sc.activation(small[:, 768:1792], scrC[:, 0:1024], AF.Square)
            dH2.inc(sc.drain())


    @sec("vector")
    def _(v):
        v.wait_ge(dVA3.h, CCH)
        v.tensor_scalar_mul(stats[:, 32:40], stats[:, 32:40], 1.0 / C)
        v.drain()
        v.tensor_scalar_mul(stats[:, 40:48], stats[:, 40:48], 1.0 / C)
        v.drain()
        v.tensor_mul(stats[:, 48:56], stats[:, 32:40], stats[:, 32:40])
        v.drain()
        v.tensor_sub(stats[:, 40:48], stats[:, 40:48], stats[:, 48:56])
        v.drain()
        v.tensor_scalar_add(stats[:, 40:48], stats[:, 40:48], LN_EPS)
        gDV.inc(v.drain())
    P5_VAR = gDV.n

    @sec("scalar")
    def _(sc):
        sc.wait_ge(gDV.h, P5_VAR)
        sc.activation(stats[:, 40:48], stats[:, 40:48], AF.Sqrt)
        gAC.inc(sc.drain())
    P5_SQRT = gAC.n

    @sec("vector")
    def _(v):
        v.wait_ge(gAC.h, P5_SQRT)
        v.reciprocal(stats[:, 40:48], stats[:, 40:48])
        gDV.inc(v.drain())

    emit_col2rep(lambda: stats[:, 32:40], 8, lambda: ex[:, 0:1024])
    P5_REPS = emit_col2rep(lambda: stats[:, 40:48], 8, lambda: ex[:, 1024:2048])

    @sec("vector")
    def _(v):
        v.wait_ge(gDV.h, P5_REPS)
        v.memset(bufQ[:, 0:1024], -1e30)
        v.memset(bufQ[:, 1024:2048], 1e30)
        gDV.inc(v.drain())
    P5_ACC0 = gDV.n

    @sec("sync")
    def _(sync):
        sync.wait_ge(gDV.h, P5_ACC0)
        dXNW.wait_all(sync)
        for ic in range(CCH):
            if ic >= 2:
                dH2S.wait(sync, ic - 2)
            dst = scrA if ic % 2 == 0 else scrB
            dXNL.issue(sync.dma_start(out=dst[:, 0:1024],
                                      in_=xnewbuf[ic * 128:(ic + 1) * 128, :]), ic)

    @sec("gpsimd")
    def _(g):
        for ic in range(CCH):
            dXNL.wait(g, ic)
            if ic >= 1:
                dH2S.wait(g, ic - 1)
            xin_t = (scrA if ic % 2 == 0 else scrB)[:, 0:1024]
            g.tensor_sub(scrC[:, 0:1024], xin_t, ex[:, 0:1024])
            g.drain()
            g.tensor_mul(scrC[:, 0:1024], scrC[:, 0:1024], ex[:, 1024:2048])
            g.drain()
            g.tensor_scalar(scrC[:, 0:1024], scrC[:, 0:1024],
                            modcol[:, 64 + ic:65 + ic],
                            modcol[:, 48 + ic:49 + ic], ALU.mult, ALU.add)
            dH2.inc(g.drain())

    @sec("vector")
    def _(v):
        for ic in range(CCH):
            v.wait_ge(dH2.h, CCH + ic + 1)
            v.tensor_max(bufQ[:, 0:1024], bufQ[:, 0:1024], scrC[:, 0:1024])
            v.drain()
            v.tensor_tensor(bufQ[:, 1024:2048], bufQ[:, 1024:2048],
                            scrC[:, 0:1024], ALU.min)
            dH2A.inc(v.drain())

    @sec("scalar")
    def _(sc):
        for ic in range(CCH):
            sc.wait_ge(dH2A.h, CCH + ic + 1)
            dH2S.issue(sc.dma_start(out=hbuf[ic * 128:(ic + 1) * 128, 0:1024],
                                    in_=scrC[:, 0:1024]), ic)

    @sec("tensor")
    def _(t):
        t.wait_ge(dH2A.h, 2 * CCH)
        for tg in range(8):
            t.transpose(pbig1[:, tg * 128:(tg + 1) * 128],
                        bufQ[:, tg * 128:(tg + 1) * 128], ident[:, :])
            t.transpose(pbig1[:, 1024 + tg * 128:1024 + (tg + 1) * 128],
                        bufQ[:, 1024 + tg * 128:1024 + (tg + 1) * 128],
                        ident[:, :])
        gPE.inc(t.drain())
    P5_TR = gPE.n

    @sec("vector")
    def _(v):
        v.wait_ge(gPE.h, P5_TR)
        v.reduce_max(stats[:, 48:56],
                     pbig1[:, 0:1024].rearrange("p (a b) -> p a b", a=8),
                     axis=AX.X)
        v.drain()
        v.reduce_max(stats[:, 56:64],
                     pbig1[:, 1024:2048].rearrange("p (a b) -> p a b", a=8),
                     axis=AX.X, negate=True)
        v.drain()
        v.tensor_max(stats[:, 48:56], stats[:, 48:56], stats[:, 56:64])
        v.drain()
        v.tensor_scalar_max(stats[:, 48:56], stats[:, 48:56], QEPS)
        v.drain()
        v.reciprocal(stats[:, 56:64], stats[:, 48:56])
        v.drain()
        v.tensor_scalar_mul(stats[:, 56:64], stats[:, 56:64], 127.0)
        gDV.inc(v.drain())

    P5_R127 = emit_col2rep(lambda: stats[:, 56:64], 8, lambda: bufV[:, 0:1024])

    @sec("sync")
    def _(sync):
        sync.wait_ge(gDV.h, P5_R127)
        dH2S.wait_all(sync)
        for ic in range(CCH):
            if ic >= 2:
                sync.wait_ge(dH2Q.h, ic - 1)
            dst = scrA if ic % 2 == 0 else scrB
            dH2L.issue(sync.dma_start(out=dst[:, 0:1024],
                                      in_=hbuf[ic * 128:(ic + 1) * 128, 0:1024]),
                       ic)

    @sec("gpsimd")
    def _(g):
        for ic in range(CCH):
            dH2L.wait(g, ic)
            if ic >= 1:
                g.wait_ge(dH2Q.h, ic)
            xin_t = (scrA if ic % 2 == 0 else scrB)[:, 0:1024]
            g.tensor_mul(scrC[:, 0:1024], xin_t, bufV[:, 0:1024])
            g.drain()
            g.tensor_scalar(t16[:, 0:1024], scrC[:, 0:1024], 1.0, 256.5,
                            ALU.mult, ALU.add)
            dH2.inc(g.drain())

    @sec("scalar")
    def _(sc):
        for ic in range(CCH):
            sc.wait_ge(dH2.h, 2 * CCH + ic + 1)
            sc.activation(h2q[:, ic, :], t16[:, 0:1024], AF.Copy, bias=-256.0)
            dH2Q.inc(sc.drain())
    P5_H2Q = dH2Q.n

    emit_col2rep(lambda: stats[:, 48:56], 8, lambda: bufK[:, 0:1024])

    @sec("vector")
    def _(v):
        v.wait_ge(gDV.h, gDV.n)
        v.tensor_scalar_mul(bufK[:, 1024:2048], bufK[:, 0:1024], kcol[:, 8:9])
        v.drain()
        v.tensor_scalar_mul(bufK[:, 0:1024], bufK[:, 0:1024], kcol[:, 6:7])
        gDV.inc(v.drain())
    P5_REP13 = gDV.n

    def p6a_st(ot):
        if ot % 2 == 0:
            return scrA, scrB
        return small, bufV

    @sec("sync")
    def _(sync):
        sync.wait_ge(dH2Q.h, P5_H2Q)
        sync.wait_ge(gDV.h, P5_REP13)
        for ot in range(HCH):
            if ot >= 2:
                sync.wait_ge(dFWV.h, 2 * (ot - 1))
            s1t, s3t = p6a_st(ot)
            dFWL.issue(sync.dma_start(
                out=s1t[:, :].rearrange("p (a b) -> p a b", a=16),
                in_=w1T[:, ot * 128:(ot + 1) * 128].rearrange(
                    "(a p) o -> p a o", p=128)), 2 * ot)
            dFWL.issue(sync.dma_start(
                out=s3t[:, :].rearrange("p (a b) -> p a b", a=16),
                in_=w3T[:, ot * 128:(ot + 1) * 128].rearrange(
                    "(a p) o -> p a o", p=128)), 2 * ot + 1)

    @sec("gpsimd")
    def _(g):
        for ot in range(HCH):
            dFWL.wait(g, 2 * ot)
            dFWL.wait(g, 2 * ot + 1)
            if ot >= 2:
                g.wait_ge(dFMM.h, ot - 1)
            s1t, s3t = p6a_st(ot)
            par = ot % 2
            w1d = (AR2[:, 0:2816] if par == 0 else AR2[:, 2816:5632]).bitcast(
                BF16)[:, 0:2048]
            w3d = (AR2[:, 0:2816] if par == 0 else AR2[:, 2816:5632]).bitcast(
                BF16)[:, 2048:4096]
            for (wt, dst, thr, nthr) in ((s1t, w1d, 3, 9), (s3t, w3d, 5, 11)):
                g.tensor_scalar(dst, wt[:, :], kcol[:, thr:thr + 1], None,
                                ALU.is_ge)
                g.drain()
            dFWS.inc(g.nop(), 2)

    @sec("tensor")
    def _(t):
        for ot in range(HCH):
            t.wait_ge(dFWV.h, 2 * (ot + 1))
            if ot >= 2:
                t.wait_ge(dFH3.h, 3 * (ot - 2) + 2)
            par = ot % 2
            ptile = pbig0 if par == 0 else pbig1
            wb = (AR2[:, 0:2816] if par == 0 else AR2[:, 2816:5632]).bitcast(BF16)
            for kc in range(CCH):
                l1 = wb[:, kc * 128:(kc + 1) * 128]
                l3 = wb[:, 2048 + kc * 128:2048 + (kc + 1) * 128]
                for h5 in range(2):
                    t.matmul(ptile[:, h5 * 512:(h5 + 1) * 512], l1,
                             h2q[:, kc, h5 * 512:(h5 + 1) * 512],
                             start=(kc == 0), stop=(kc == CCH - 1))
                    t.matmul(ptile[:, 1024 + h5 * 512:1024 + (h5 + 1) * 512], l3,
                             h2q[:, kc, h5 * 512:(h5 + 1) * 512],
                             start=(kc == 0), stop=(kc == CCH - 1))
            dFMM.inc(t.drain())

    P6A_GAC0 = gAC.n

    def p6a_pass2(v, ot):
        v.wait_ge(dFWS.h, 2 * (ot + 1))
        s1t, s3t = p6a_st(ot)
        par2 = ot % 2
        w1d = (AR2[:, 0:2816] if par2 == 0 else AR2[:, 2816:5632]).bitcast(
            BF16)[:, 0:2048]
        w3d = (AR2[:, 0:2816] if par2 == 0 else AR2[:, 2816:5632]).bitcast(
            BF16)[:, 2048:4096]
        for (wt, dst, nthr) in ((s1t, w1d, 9), (s3t, w3d, 11)):
            v.scalar_tensor_tensor(dst, wt[:, :], kcol[:, nthr:nthr + 1],
                                   dst, ALU.is_le, ALU.subtract)
            dFWV.inc(v.drain())

    @sec("vector")
    def _(v):
        v.memset(ex[:, 0:1024], -1e30)
        v.memset(ex[:, 1024:2048], 1e30)
        v.drain()
        p6a_pass2(v, 0)
        p6a_pass2(v, 1)
        for ot in range(HCH):
            if ot + 2 < HCH:
                p6a_pass2(v, ot + 2)
            v.wait_ge(dFMM.h, ot + 1)
            if ot >= 2:
                dFSP.wait(v, ot - 2)
            par = ot % 2
            ptile = pbig0 if par == 0 else pbig1
            h3o = scrC[:, 0:1024] if ot % 2 == 0 else scrC[:, 1024:2048]
            v.tensor_mul(bufQ[:, 0:1024], ptile[:, 0:1024], bufK[:, 0:1024])
            dFH3.inc(v.drain())
            v.tensor_mul(bufQ[:, 1024:2048], ptile[:, 1024:2048],
                         bufK[:, 1024:2048])
            dFH3.inc(v.drain())
            gAC_t = P6A_GAC0 + ot + 1

            v.wait_ge(gAC.h, gAC_t)
            v.tensor_mul(h3o, bufQ[:, 0:1024], bufQ[:, 1024:2048])
            v.drain()
            v.tensor_max(ex[:, 0:1024], ex[:, 0:1024], h3o)
            v.drain()
            v.tensor_tensor(ex[:, 1024:2048], ex[:, 1024:2048], h3o, ALU.min)
            dFH3.inc(v.drain())

    @sec("scalar")
    def _(sc):
        for ot in range(HCH):
            sc.wait_ge(dFH3.h, 3 * ot + 1)
            sc.activation(bufQ[:, 0:1024], bufQ[:, 0:1024], AF.Silu)
            gAC.inc(sc.drain())
            sc.wait_ge(dFH3.h, 3 * ot + 3)
            h3o = scrC[:, 0:1024] if ot % 2 == 0 else scrC[:, 1024:2048]
            dFSP.issue(sc.dma_start(out=h3buf[ot * 128:(ot + 1) * 128, :],
                                    in_=h3o), ot)

    @sec("tensor")
    def _(t):
        t.wait_ge(dFH3.h, 3 * HCH)
        for tg in range(8):
            t.transpose(pbig1[:, tg * 128:(tg + 1) * 128],
                        ex[:, tg * 128:(tg + 1) * 128], ident[:, :])
            t.transpose(pbig1[:, 1024 + tg * 128:1024 + (tg + 1) * 128],
                        ex[:, 1024 + tg * 128:1024 + (tg + 1) * 128],
                        ident[:, :])
        gPE.inc(t.drain())
    P6_TR = gPE.n

    @sec("vector")
    def _(v):
        v.wait_ge(gPE.h, P6_TR)
        v.reduce_max(stats[:, 64:72],
                     pbig1[:, 0:1024].rearrange("p (a b) -> p a b", a=8),
                     axis=AX.X)
        v.drain()
        v.reduce_max(stats[:, 72:80],
                     pbig1[:, 1024:2048].rearrange("p (a b) -> p a b", a=8),
                     axis=AX.X, negate=True)
        v.drain()
        v.tensor_max(stats[:, 64:72], stats[:, 64:72], stats[:, 72:80])
        v.drain()
        v.tensor_scalar_max(stats[:, 64:72], stats[:, 64:72], QEPS)
        v.drain()
        v.reciprocal(stats[:, 72:80], stats[:, 64:72])
        v.drain()
        v.tensor_scalar_mul(stats[:, 72:80], stats[:, 72:80], 127.0)
        gDV.inc(v.drain())

    emit_col2rep(lambda: stats[:, 72:80], 8, lambda: bufK[:, 0:1024])
    P6_R2A = emit_col2rep(lambda: stats[:, 64:72], 8, lambda: bufK[:, 1024:2048])

    @sec("vector")
    def _(v):
        v.wait_ge(gDV.h, P6_R2A)
        v.tensor_scalar_mul(bufK[:, 1024:2048], bufK[:, 1024:2048], kcol[:, 7:8])
        gDV.inc(v.drain())
    P6_REP2 = gDV.n

    @sec("sync")
    def _(sync):
        sync.wait_ge(gDV.h, P6_REP2)
        dFSP.wait_all(sync)
        for jc in range(HCH):
            if jc >= 2:
                sync.wait_ge(dH3Q.h, jc - 1)
            dst = scrA[:, 0:1024] if jc % 2 == 0 else scrA[:, 1024:2048]
            dH3L.issue(sync.dma_start(out=dst,
                                      in_=h3buf[jc * 128:(jc + 1) * 128, :]), jc)

    @sec("gpsimd")
    def _(g):
        for jc in range(HCH):
            dH3L.wait(g, jc)
            if jc >= 1:
                g.wait_ge(dH3Q.h, jc)
            src = scrA[:, 0:1024] if jc % 2 == 0 else scrA[:, 1024:2048]
            g.tensor_mul(scrB[:, 0:1024], src, bufK[:, 0:1024])
            g.drain()
            g.tensor_scalar(t16[:, 0:1024], scrB[:, 0:1024], 1.0, 256.5,
                            ALU.mult, ALU.add)
            dH2.inc(g.drain())

    @sec("scalar")
    def _(sc):
        for jc in range(HCH):
            sc.wait_ge(dH2.h, 3 * CCH + jc + 1)
            sc.activation(h3qf[:, jc * 1024:(jc + 1) * 1024], t16[:, 0:1024],
                          AF.Copy, bias=-256.0)
            dH3Q.inc(sc.drain())
    P6B_H3Q = dH3Q.n

    W2P = [(0, 16), (16, 16), (32, 12)]

    @sec("sync")
    def _(sync):
        sync.wait_ge(dH3Q.h, P6B_H3Q)
        for o2 in range(CCH):
            for pi, (k0, kn) in enumerate(W2P):
                k = o2 * 3 + pi
                if k >= 2:
                    sync.wait_ge(dW2V.h, k - 1)
                st = scrC if k % 2 == 0 else small
                dW2L.issue(sync.dma_start(
                    out=st[:, 0:kn * 128].rearrange("p (a b) -> p a b", a=kn),
                    in_=w2T[k0 * 128:(k0 + kn) * 128,
                            o2 * 128:(o2 + 1) * 128].rearrange(
                        "(a p) o -> p a o", p=128)), k)
            if o2 >= 2:
                dYOUT.wait(sync, o2 - 2)
            dst = bufV[:, 0:1024] if o2 % 2 == 0 else bufV[:, 1024:2048]
            dXN2L.issue(sync.dma_start(out=dst,
                                       in_=xnewbuf[o2 * 128:(o2 + 1) * 128, :]),
                        o2)

    @sec("gpsimd")
    def _(g):
        for o2 in range(CCH):
            par = o2 % 2
            for pi, (k0, kn) in enumerate(W2P):
                k = o2 * 3 + pi
                dW2L.wait(g, k)
                if o2 >= 2 and pi == 0:
                    g.wait_ge(dYMM.h, o2 - 1)
                st = scrC if k % 2 == 0 else small
                dst = (AR2[:, 0:2816] if par == 0 else
                       AR2[:, 2816:5632]).bitcast(BF16)[:,
                                                        k0 * 128:(k0 + kn) * 128]
                g.tensor_scalar(dst, st[:, 0:kn * 128], kcol[:, 4:5], None,
                                ALU.is_ge)
                dW2S.inc(g.drain())


    @sec("tensor")
    def _(t):
        for o2 in range(CCH):
            t.wait_ge(dW2V.h, 3 * (o2 + 1))
            if o2 >= 2:
                t.wait_ge(dYDQ.h, o2 - 1)
            par = o2 % 2
            ptile = pbig0[:, 0:1024] if par == 0 else pbig1[:, 0:1024]
            wb = (AR2[:, 0:2816] if par == 0 else AR2[:, 2816:5632]).bitcast(BF16)
            for kc in range(HCH):
                lhs = wb[:, kc * 128:(kc + 1) * 128]
                for h5 in range(2):
                    t.matmul(ptile[:, h5 * 512:(h5 + 1) * 512], lhs,
                             h3qf[:, kc * 1024 + h5 * 512:
                                  kc * 1024 + (h5 + 1) * 512],
                             start=(kc == 0), stop=(kc == HCH - 1))
            dYMM.inc(t.drain())

    def p6b_pass2(v, o2):
        par2 = o2 % 2
        for pi, (k0, kn) in enumerate(W2P):
            k = o2 * 3 + pi
            v.wait_ge(dW2S.h, k + 1)
            st = scrC if k % 2 == 0 else small
            dst = (AR2[:, 0:2816] if par2 == 0 else
                   AR2[:, 2816:5632]).bitcast(BF16)[:, k0 * 128:(k0 + kn) * 128]
            v.scalar_tensor_tensor(dst, st[:, 0:kn * 128], kcol[:, 10:11],
                                   dst, ALU.is_le, ALU.subtract)
            dW2V.inc(v.drain())

    @sec("vector")
    def _(v):
        p6b_pass2(v, 0)
        p6b_pass2(v, 1)
        for o2 in range(CCH):
            if o2 + 2 < CCH:
                p6b_pass2(v, o2 + 2)
            v.wait_ge(dYMM.h, o2 + 1)
            par = o2 % 2
            ptile = pbig0[:, 0:1024] if par == 0 else pbig1[:, 0:1024]
            v.tensor_mul(scrB[:, 1024:2048], ptile, bufK[:, 1024:2048])
            dYDQ.inc(v.drain())
            dXN2L.wait(v, o2)
            if o2 >= 2:
                dYOUT.wait(v, o2 - 2)
            xnt = bufV[:, 0:1024] if o2 % 2 == 0 else bufV[:, 1024:2048]
            dst = bufQ[:, 0:1024] if o2 % 2 == 0 else bufQ[:, 1024:2048]
            v.scalar_tensor_tensor(dst, scrB[:, 1024:2048],
                                   modcol[:, 80 + o2:81 + o2], xnt,
                                   ALU.mult, ALU.add)
            dH2.inc(v.drain())

    @sec("scalar")
    def _(sc):
        for o2 in range(CCH):
            sc.wait_ge(dH2.h, 3 * CCH + HCH + o2 + 1)
            src = bufQ[:, 0:1024] if o2 % 2 == 0 else bufQ[:, 1024:2048]
            dYOUT.issue(sc.dma_start(out=yT[o2 * 128:(o2 + 1) * 128, :],
                                     in_=src), o2)

    @sec("gpsimd")
    def _(g):
        dYOUT.wait_all(g)

    for cm in reversed(ctxs):
        cm.__exit__(None, None, None)
    return nc


def _prep(inputs):
    x = np.ascontiguousarray(inputs["L0_attn_x"], dtype=np.float32)
    qkv_w = np.asarray(inputs["L0_attn_qkv_w"], dtype=np.float32)
    qkv_b = np.asarray(inputs["L0_attn_qkv_b"], dtype=np.float32)
    gate_w = np.asarray(inputs["L0_attn_gate_w"], dtype=np.float32)
    gate_b = np.asarray(inputs["L0_attn_gate_b"], dtype=np.float32)
    proj_w = np.asarray(inputs["L0_attn_proj_w"], dtype=np.float32)
    proj_b = np.asarray(inputs["L0_attn_proj_b"], dtype=np.float32)
    norm_w = np.asarray(inputs["L0_attn_norm_w"], dtype=np.float32)
    w1 = np.asarray(inputs["L0_ffn_w1"], dtype=np.float32)
    w2 = np.asarray(inputs["L0_ffn_w2"], dtype=np.float32)
    w3 = np.asarray(inputs["L0_ffn_w3"], dtype=np.float32)
    cc = np.asarray(inputs["L0_c"], dtype=np.float32)
    wmod = np.asarray(inputs["L0_wmod"], dtype=np.float32)

    w1T = np.ascontiguousarray(w1.T)
    w3T = np.ascontiguousarray(w3.T)
    w2T = np.ascontiguousarray(w2.T)
    cT = np.ascontiguousarray(cc.T)
    identv = np.eye(128, dtype=np.float32)

    seg_arr = {"qkv": qkv_w, "gate": gate_w, "proj": proj_w,
               "w1": w1, "w2": w2, "w3": w3}

    cols = []
    bcols = []
    for h in range(NH):
        qs = qkv_w[h * 128:(h + 1) * 128, :]
        ks = qkv_w[2048 + h * 128:2048 + (h + 1) * 128, :]
        vs = qkv_w[4096 + h * 128:4096 + (h + 1) * 128, :]
        gs = gate_w[h * 128:(h + 1) * 128, :]
        cols += [qs.T, ks.T, vs.T, gs.T]
        bcols += [qkv_b[h * 128:(h + 1) * 128],
                  qkv_b[2048 + h * 128:2048 + (h + 1) * 128],
                  qkv_b[4096 + h * 128:4096 + (h + 1) * 128],
                  gate_b[h * 128:(h + 1) * 128]]
    wqkvgT = np.ascontiguousarray(np.concatenate(cols, axis=1))
    bqkvg = np.ascontiguousarray(np.stack(bcols, axis=1))
    wprojT = np.ascontiguousarray(proj_w.T)
    normw = np.ascontiguousarray(norm_w.reshape(16, 128).T)
    bpjc = np.ascontiguousarray(proj_b.reshape(16, 128).T)

    maps = []
    for core in range(8):
        b = core // 2
        side = core % 2
        xTb = np.ascontiguousarray(x[b].T[:, side * 1024:(side + 1) * 1024])
        wmod_shc = np.ascontiguousarray(wmod[:, core * 1536:(core + 1) * 1536])
        wshp = []
        for s in SEGS:
            a = seg_arr[s].reshape(-1)
            n8 = a.size // 8
            wshp.append(a[core * n8:(core + 1) * n8].reshape(128, -1))
        wshc = np.ascontiguousarray(np.concatenate(wshp, axis=1))
        boh = np.zeros((128, 4), dtype=np.float32)
        boh[:, b] = 1.0
        mselv = np.full((128, 1), float(side), dtype=np.float32)
        maps.append({
            "xT": xTb, "wqkvgT": wqkvgT, "bqkvg": bqkvg,
            "wprojT": wprojT, "bproj": bpjc, "normw": normw,
            "w1T": w1T, "w3T": w3T, "w2T": w2T,
            "wmod_sh": wmod_shc, "cTin": cT, "wsh": wshc,
            "bonehot": boh, "msel": mselv, "identin": identv,
        })
    return maps


def kernel(**inputs):
    if "nc" not in _CACHE:
        _CACHE["nc"] = build()
    nc = _CACHE["nc"]
    maps = _prep(inputs)
    res = run_bass_kernel_spmd(nc, maps, core_ids=list(range(8)))
    y = np.empty((B, NTOK, C), dtype=np.float32)
    for core in range(8):
        b = core // 2
        side = core % 2
        y[b, side * 1024:(side + 1) * 1024, :] = res.results[core]["yT"].T
    return y


# version 06
# speedup vs baseline: 1.0055x; 1.0055x over previous
import numpy as np
import ml_dtypes

import concourse.bass as bass
import concourse.mybir as mybir
from concourse.bass_utils import run_bass_kernel_spmd

F32 = mybir.dt.float32
BF16 = mybir.dt.bfloat16
I16 = mybir.dt.int16
AF = mybir.ActivationFunctionType
ALU = mybir.AluOpType
AX = mybir.AxisListType

B = 4
NTOK = 2048
C = 2048
D = 128
NH = 16
NCH = 8
MYTOK = 1024
HID = 5632
CCH = 16
HCH = 44
ATT_SCALE = float(D) ** -0.5
LN_EPS = 1e-5
RMS_EPS = 1e-6
QEPS = 1e-5

SIZES = {"qkv": 6144 * 2048, "gate": 2048 * 2048, "proj": 2048 * 2048,
         "w1": HID * 2048, "w2": 2048 * HID, "w3": HID * 2048}
SEGS = ["qkv", "gate", "proj", "w1", "w2", "w3"]
SEG_F = [SIZES[s] // 8 // 128 for s in SEGS]
WSH_F = sum(SEG_F)

_CACHE = {}


def build():
    nc = bass.Bass("TRN2", target_bir_lowering=False, debug=False)

    xT = nc.dram_tensor("xT", [C, MYTOK], F32, kind="ExternalInput")
    wqkvgT = nc.dram_tensor("wqkvgT", [128, NH * 4 * 2048], F32,
                            kind="ExternalInput")
    bqkvg = nc.dram_tensor("bqkvg", [128, NH * 4], F32, kind="ExternalInput")
    wprojT = nc.dram_tensor("wprojT", [128, CCH * 2048], F32,
                            kind="ExternalInput")
    bproj = nc.dram_tensor("bproj", [128, CCH], F32, kind="ExternalInput")
    normw = nc.dram_tensor("normw", [128, CCH], F32, kind="ExternalInput")
    w1T = nc.dram_tensor("w1T", [128, HCH * 2048], F32, kind="ExternalInput")
    w3T = nc.dram_tensor("w3T", [128, HCH * 2048], F32, kind="ExternalInput")
    w2T = nc.dram_tensor("w2T", [128, CCH * HID], F32, kind="ExternalInput")
    wmod_sh = nc.dram_tensor("wmod_sh", [C, 1536], F32, kind="ExternalInput")
    cTin = nc.dram_tensor("cTin", [C, B], F32, kind="ExternalInput")
    wsh = nc.dram_tensor("wsh", [128, WSH_F], F32, kind="ExternalInput")
    bonehot = nc.dram_tensor("bonehot", [128, 4], F32, kind="ExternalInput")
    msel = nc.dram_tensor("msel", [128, 1], F32, kind="ExternalInput")
    identin = nc.dram_tensor("identin", [128, 128], F32, kind="ExternalInput")

    yT = nc.dram_tensor("yT", [C, MYTOK], F32, kind="ExternalOutput")

    mod_in = nc.dram_tensor("mod_in", [1536, 4], F32)
    mod_out = nc.dram_tensor("mod_out", [12288, 4], F32)
    wsum_in = nc.dram_tensor("wsum_in", [1, 8], F32)
    wsum_out = nc.dram_tensor("wsum_out", [1, 8], F32)
    hbuf = nc.dram_tensor("hbuf", [C, MYTOK], F32)
    qbuf = nc.dram_tensor("qbuf", [128, NH * MYTOK], F32)
    kbuf = nc.dram_tensor("kbuf", [128, NH * MYTOK], F32)
    vbuf = nc.dram_tensor("vbuf", [128, NH * MYTOK], F32)
    kvbuf = nc.dram_tensor("kvbuf", [128, NH * NCH * 128], F32)
    T_in = nc.dram_tensor("T_in", [128, NH * 128], F32)
    T_out = nc.dram_tensor("T_out", [256, NH * 128], F32)
    h3buf = nc.dram_tensor("h3buf", [HID, MYTOK], F32)
    xnewbuf = nc.dram_tensor("xnewbuf", [C, MYTOK], F32)

    ctxs = []

    def E(cm):
        ctxs.append(cm)
        return cm.__enter__()

    AR1 = E(nc.sbuf_tensor("AR1", [128, 24576], F32))
    AR2 = E(nc.sbuf_tensor("AR2", [128, 8192], F32))
    bufQ = E(nc.sbuf_tensor("bufQ", [128, 2048], F32))
    bufK = E(nc.sbuf_tensor("bufK", [128, 2048], F32))
    bufV = E(nc.sbuf_tensor("bufV", [128, 2048], F32))
    scrA = E(nc.sbuf_tensor("scrA", [128, 2048], F32))
    scrB = E(nc.sbuf_tensor("scrB", [128, 2048], F32))
    scrC = E(nc.sbuf_tensor("scrC", [128, 2048], F32))
    ex = E(nc.sbuf_tensor("ex", [128, 2048], F32))
    smaxrep = E(nc.sbuf_tensor("smaxrep", [128, 1024], F32))
    small = E(nc.sbuf_tensor("small", [128, 2048], F32))
    t16 = E(nc.sbuf_tensor("t16", [128, 1024], I16))
    stats = E(nc.sbuf_tensor("stats", [128, 256], F32))
    modcol = E(nc.sbuf_tensor("modcol", [128, 96], F32))
    kcol = E(nc.sbuf_tensor("kcol", [128, 16], F32))
    bqg = E(nc.sbuf_tensor("bqg", [128, NH * 4], F32))
    bpj = E(nc.sbuf_tensor("bpj", [128, CCH], F32))
    nwc = E(nc.sbuf_tensor("nwc", [128, CCH], F32))
    mselc = E(nc.sbuf_tensor("mselc", [128, 1], F32))
    bhot = E(nc.sbuf_tensor("bhot", [128, 4], F32))
    ident = E(nc.sbuf_tensor("ident", [128, 128], F32))
    ones = E(nc.sbuf_tensor("ones", [128, 128], F32))
    dcy = E(nc.sbuf_tensor("dcy", [128, 128], F32))
    Ssl = ex

    pbig0 = E(nc.psum_tensor("pbig0", [128, 2048], F32))
    pbig1 = E(nc.psum_tensor("pbig1", [128, 2048], F32))

    class Ctr:
        def __init__(self, name):
            self.h = E(nc.semaphore(name))
            self.n = 0

        def inc(self, ins, k=1):
            ins.then_inc(self.h, k)
            self.n += k
            return self.n

    class DSem:

        def __init__(self, name, nway=2):
            self.c = [Ctr(name + chr(97 + j)) for j in range(nway)]
            self.nway = nway

        def issue(self, ins, i):
            self.c[i % self.nway].inc(ins, 16)

        def wait(self, eng, i):
            eng.wait_ge(self.c[i % self.nway].h,
                        (i // self.nway + 1) * 16)

        def wait_all(self, eng):
            for c in self.c:
                eng.wait_ge(c.h, c.n)

    class Bank:

        def __init__(self, ctrs):
            self.c = ctrs
            self.b = [c.n for c in ctrs]
            self.nway = len(ctrs)

        def issue(self, ins, i):
            self.c[i % self.nway].inc(ins, 16)

        def wait(self, eng, i):
            eng.wait_ge(self.c[i % self.nway].h,
                        self.b[i % self.nway] + (i // self.nway + 1) * 16)

        def wait_all(self, eng):
            for c in self.c:
                eng.wait_ge(c.h, c.n)

    gIN = Ctr("gIN"); gSP = Ctr("gSP"); gGD = Ctr("gGD"); gCC = Ctr("gCC")
    gPE = Ctr("gPE"); gDV = Ctr("gDV"); gAC = Ctr("gAC"); gGP = Ctr("gGP")
    dWSH = DSem("dWSH"); dABS = Ctr("dABS"); dMOD = DSem("dMOD"); dMDM = Ctr("dMDM")
    dX1 = DSem("dX1"); dSQ1 = Ctr("dSQ1"); dMM1 = Ctr("dMM1")
    dXB = DSem("dXB"); dHB = Ctr("dHB"); dACB = Ctr("dACB"); dSPB = DSem("dSPB")
    dXC = DSem("dXC"); dQC = Ctr("dQC"); dHQ = Ctr("dHQ")
    dWL2 = DSem("dWL2"); dSG2 = Ctr("dSG2"); dMMq = Ctr("dMMq"); dDQq = Ctr("dDQq")
    dCHp = Ctr("dCHp"); dCHd = Ctr("dCHd")
    dDCY = Ctr("dDCY"); dTOL = Ctr("dTOL")
    dSC = Ctr("dSC"); dDN = Ctr("dDN")
    dGRP = Ctr("dGRP"); dGRD = Ctr("dGRD"); dREC = Ctr("dREC"); dRAD = Ctr("dRAD")
    dPJL = DSem("dPJL"); dPJS = Ctr("dPJS"); dPJM = Ctr("dPJM"); dPJD = Ctr("dPJD")
    dPQA = Ctr("dPQA"); dPQB = Ctr("dPQB")
    dX2L = DSem("dX2L"); dXNW = DSem("dXNW"); dLN2 = Ctr("dLN2")
    dXNL = DSem("dXNL"); dH2 = Ctr("dH2"); dH2A = Ctr("dH2A"); dH2S = DSem("dH2S")
    dH2L = DSem("dH2L"); dH2Q = Ctr("dH2Q")
    dFWL = DSem("dFWL", 4); dFWS = Ctr("dFWS"); dFMM = Ctr("dFMM"); dFH3 = Ctr("dFH3")
    dFSP = DSem("dFSP")
    dH3L = DSem("dH3L"); dH3Q = Ctr("dH3Q")
    dW2L = DSem("dW2L", 6); dW2S = Ctr("dW2S"); dYMM = Ctr("dYMM"); dYDQ = Ctr("dYDQ")
    dXN2 = Ctr("dXN2"); dXN2L = DSem("dXN2L"); dYOUT = DSem("dYOUT")
    dVA0 = Ctr("dVA0"); dVA1 = Ctr("dVA1"); dVA2 = Ctr("dVA2"); dVA3 = Ctr("dVA3")
    dFWV = Ctr("dFWV"); dW2V = Ctr("dW2V")

    block = E(nc.Block())

    def sec(name):
        return getattr(block, name)

    hqT = AR1[:, 16384:24576].bitcast(BF16)
    outf = AR1[:, 0:16384].rearrange("p (a b) -> p a b", a=16)
    olh = AR1[:, 0:16384].rearrange("p (a b) -> p b a", a=16)
    xtmp = AR1[:, 16384:24576]
    pjq = AR1[:, 16384:24576].bitcast(BF16).rearrange("p (a b) -> p a b", a=16)
    h2q = AR1[:, 16384:24576].bitcast(BF16).rearrange("p (a b) -> p a b", a=16)
    h3qf = AR1[:, 0:22528].bitcast(BF16)
    wqg = [AR2[:, 0:4096].bitcast(BF16).rearrange("p (a b) -> p a b", a=16),
           AR2[:, 4096:8192].bitcast(BF16).rearrange("p (a b) -> p a b", a=16)]
    wsg = [AR2[:, 0:2816].bitcast(BF16).rearrange("p (a b) -> p a b", a=44),
           AR2[:, 2816:5632].bitcast(BF16).rearrange("p (a b) -> p a b", a=44)]

    colT = small[:, 1664:1792]
    Tslot = [small[:, 0:128], small[:, 128:256]]
    kvt = [small[:, 256:384], small[:, 384:512]]
    ktvt = [small[:, 512:768], small[:, 768:1024]]
    kvsl = xtmp[:, 0:2048]
    bd = [xtmp[:, 2048:2176], xtmp[:, 2176:2304]]
    v2g = [xtmp[:, 2304:2432], xtmp[:, 2432:2560]]
    v2t = [xtmp[:, 2560:2688], xtmp[:, 2688:2816]]
    Mrep_sb = [xtmp[0:16, 4096:5120], xtmp[0:16, 5120:6144]]
    Mrow = [xtmp[32:33, 4096:5120], xtmp[32:33, 5120:6144]]
    rzrow = [xtmp[64:65, 4096:5120], xtmp[64:65, 5120:6144]]
    attnT = xtmp[0:16, 6144:8192]
    Mst = [stats[0:16, 128:192], stats[0:16, 192:256]]
    psc = pbig0[0:16, 0:1024]
    pscT = pbig1[0:16, 0:1024]
    pdn16 = pbig1[0:16, 1024:2048]
    pdn1 = pbig1[0:1, 1024:2048]
    pv = [pbig0[:, 1024:1152], pbig0[:, 1152:1280]]
    plo = [pbig0[:, 1280:1408], pbig0[:, 1408:1536]]
    prec = [pbig0[:, 1536:1664], pbig0[:, 1664:1792]]

    @sec("sync")
    def _(sync):
        gIN.inc(sync.dma_start(out=ident[:, :], in_=identin[:, :]), 16)
        gIN.inc(sync.dma_start(out=bqg[:, :], in_=bqkvg[:, :]), 16)
        gIN.inc(sync.dma_start(out=bpj[:, :], in_=bproj[:, :]), 16)
        gIN.inc(sync.dma_start(out=nwc[:, :], in_=normw[:, :]), 16)
        gIN.inc(sync.dma_start(out=mselc[:, :], in_=msel[:, :]), 16)
        gIN.inc(sync.dma_start(out=bhot[:, :], in_=bonehot[:, :]), 16)
        gIN.inc(sync.dma_start(
            out=small[:, 1792:1856].rearrange("p (a b) -> p a b", a=16),
            in_=cTin[:, :].rearrange("(a p) b -> p a b", p=128)), 16)
    P0_CONST = gIN.n

    @sec("vector")
    def _(v):
        gDV.inc(v.memset(ones[:, :], 1.0))
        v.drain()
    P0_ONES = gDV.n

    @sec("scalar")
    def _(sc):
        sc.wait_ge(gIN.h, P0_CONST)
        sc.activation(small[:, 1792:1856], small[:, 1792:1856], AF.Silu)
        gAC.inc(sc.drain())
    P0_SILU = gAC.n

    seg_off = np.cumsum([0] + SEG_F).tolist()
    pieces = []
    for si, f in enumerate(SEG_F):
        o = 0
        while o < f:
            w = min(2048, f - o)
            pieces.append((seg_off[si] + o, w, si))
            o += w
    NP = len(pieces)

    @sec("sync")
    def _(sync):
        for pi, (fo, w, si) in enumerate(pieces):
            if pi >= 2:
                sync.wait_ge(dABS.h, pi - 1)
            dWSH.issue(sync.dma_start(out=(scrA if pi % 2 == 0 else scrB)[:, 0:w],
                                      in_=wsh[:, fo:fo + w]), pi)

    @sec("scalar")
    def _(sc):
        for pi, (fo, w, si) in enumerate(pieces):
            dWSH.wait(sc, pi)
            sc.activation(scrC[:, 0:w], (scrA if pi % 2 == 0 else scrB)[:, 0:w],
                          AF.Abs, accum_out=stats[:, 144 + pi:145 + pi])
            dABS.inc(sc.drain())

    @sec("tensor")
    def _(t):
        t.wait_ge(dABS.h, NP)
        t.wait_ge(gDV.h, P0_ONES)
        for si in range(6):
            cols = [pi for pi, (_, _, s2) in enumerate(pieces) if s2 == si]
            c0, n2 = min(cols), len(cols)
            t.matmul(pbig0[0:n2, 1024 + si:1025 + si],
                     stats[:, 144 + c0:144 + c0 + n2], ones[:, 0:1],
                     start=True, stop=True)
        gPE.inc(t.drain())
    P0_WMM1 = gPE.n

    @sec("vector")
    def _(v):
        v.wait_ge(gPE.h, P0_WMM1)
        for si in range(6):
            cols = [pi for pi, (_, _, s2) in enumerate(pieces) if s2 == si]
            n2 = len(cols)
            v.tensor_copy(small[0:n2, 1100 + si:1101 + si],
                          pbig0[0:n2, 1024 + si:1025 + si])
            v.drain()
        gDV.inc(v.nop())
    P0_WCP = gDV.n

    @sec("tensor")
    def _(t):
        t.wait_ge(gDV.h, P0_WCP)
        for si in range(6):
            cols = [pi for pi, (_, _, s2) in enumerate(pieces) if s2 == si]
            n2 = len(cols)
            t.matmul(pbig0[0:1, 1040 + si:1041 + si],
                     small[0:n2, 1100 + si:1101 + si], ones[0:n2, 0:1],
                     start=True, stop=True)
        gPE.inc(t.drain())
    P0_WMM2 = gPE.n

    @sec("vector")
    def _(v):
        v.wait_ge(gPE.h, P0_WMM2)
        v.tensor_copy(small[0:1, 1110:1116], pbig0[0:1, 1040:1046])
        v.drain()
        v.memset(small[0:1, 1116:1118], 0.0)
        gDV.inc(v.drain())
    P0_WROW = gDV.n

    @sec("gpsimd")
    def _(g):
        g.wait_ge(gDV.h, P0_WROW)
        gGD.inc(g.dma_start(out=wsum_in[:, 0:8], in_=small[0:1, 1110:1118]), 16)
        g.wait_ge(gGD.h, gGD.n)
        gCC.inc(g.collective_compute(
            "AllReduce", ALU.add, replica_groups=[list(range(8))],
            ins=[wsum_in[:, :].opt()], outs=[wsum_out[:, :].opt()]))
    P0_WAR = gCC.n

    @sec("sync")
    def _(sync):
        sync.wait_ge(gCC.h, P0_WAR)
        gIN.inc(sync.dma_start(out=small[0:1, 1120:1128], in_=wsum_out[:, :]), 16)
    P0_WLD = gIN.n

    @sec("vector")
    def _(v):
        v.wait_ge(gIN.h, P0_WLD)
        for si, nm in enumerate(SEGS):
            v.tensor_scalar(small[0:1, 1130 + si:1131 + si],
                            small[0:1, 1120 + si:1121 + si],
                            1.0 / SIZES[nm], QEPS, ALU.mult, ALU.max)
            v.drain()
        v.tensor_scalar_mul(small[0:1, 1140:1143], small[0:1, 1130:1133],
                            2.0 / 127.0)
        v.drain()
        v.tensor_scalar_mul(small[0:1, 1143:1146], small[0:1, 1133:1136], 0.5)
        v.drain()
        v.tensor_scalar_mul(small[0:1, 1146:1149], small[0:1, 1133:1136],
                            -1.0 / 127.0)
        gDV.inc(v.drain())
    P0_KROW = gDV.n

    @sec("tensor")
    def _(t):
        t.wait_ge(gDV.h, P0_KROW)
        t.matmul(pbig0[:, 1056:1065], ones[0:1, :], small[0:1, 1140:1149],
                 start=True, stop=True)
        gPE.inc(t.drain())
    P0_KMM = gPE.n

    @sec("vector")
    def _(v):
        v.wait_ge(gPE.h, P0_KMM)
        v.tensor_copy(kcol[:, 0:9], pbig0[:, 1056:1065])
        v.drain()
        v.tensor_scalar_mul(kcol[:, 9:12], kcol[:, 3:6], -1.0)
        gDV.inc(v.drain())
    P0_KCOL = gDV.n

    @sec("sync")
    def _(sync):
        sync.wait_ge(dABS.h, NP)
        for ic in range(CCH):
            if ic >= 2:
                sync.wait_ge(dMDM.h, ic - 1)
            dMOD.issue(sync.dma_start(
                out=(scrA if ic % 2 == 0 else scrB)[:, 0:1536],
                in_=wmod_sh[ic * 128:(ic + 1) * 128, :]), ic)

    @sec("tensor")
    def _(t):
        t.wait_ge(gAC.h, P0_SILU)
        cs = small[:, 1792:1856].rearrange("p (a b) -> p a b", a=16)
        for ic in range(CCH):
            dMOD.wait(t, ic)
            if ic >= 1:
                t.wait_ge(dVA0.h, ic)
            wt = (scrA if ic % 2 == 0 else scrB)[:, 0:1536].rearrange(
                "p (j o) -> p j o", j=12)
            for jt in range(12):
                t.matmul(pbig0[:, 1536 + jt * 4:1540 + jt * 4], wt[:, jt, :],
                         cs[:, ic, :], start=True, stop=True)
            dMDM.inc(t.drain())

    @sec("vector")
    def _(v):
        v.memset(stats[:, 64:112], 0.0)
        v.drain()
        for ic in range(CCH):
            v.wait_ge(dMDM.h, ic + 1)
            v.tensor_add(stats[:, 64:112], stats[:, 64:112], pbig0[:, 1536:1584])
            dVA0.inc(v.drain())
        gDV.inc(v.nop())
    P0_MACC = gDV.n

    @sec("gpsimd")
    def _(g):
        g.wait_ge(gDV.h, P0_MACC)
        gGD.inc(g.dma_start(
            out=mod_in[:, :].rearrange("(jt jl) b -> jl jt b", jl=128),
            in_=stats[:, 64:112].rearrange("p (jt b) -> p jt b", jt=12)), 16)
        g.wait_ge(gGD.h, gGD.n)
        gCC.inc(g.collective_compute(
            "AllGather", ALU.bypass, replica_groups=[list(range(8))],
            ins=[mod_in[:, :].opt()], outs=[mod_out[:, :].opt()]))
    P0_MAG = gCC.n

    @sec("sync")
    def _(sync):
        sync.wait_ge(gCC.h, P0_MAG)
        gIN.inc(sync.dma_start(
            out=small[:, 1024:1408].rearrange("p (s b) -> p s b", s=96),
            in_=mod_out[:, :].rearrange("(s p) b -> p s b", p=128)), 16)
    P0_MLD = gIN.n

    @sec("vector")
    def _(v):
        v.wait_ge(gIN.h, P0_MLD)
        mo = small[:, 1024:1408].rearrange("p (s b) -> p s b", s=96)
        v.tensor_scalar_mul(modcol[:, :], mo[:, :, 0], bhot[:, 0:1])
        v.drain()
        for b2 in range(1, 4):
            v.scalar_tensor_tensor(modcol[:, :], mo[:, :, b2], bhot[:, b2:b2 + 1],
                                   modcol[:, :], ALU.mult, ALU.add)
            v.drain()
        for s0 in (32, 80):
            sl = modcol[:, s0:s0 + 16]
            v.tensor_scalar(stats[:, 176:192], sl, 0.0, None, ALU.is_ge)
            v.drain()
            v.tensor_scalar(stats[:, 176:192], stats[:, 176:192], 2.0, -1.0,
                            ALU.mult, ALU.add)
            v.drain()
            v.tensor_scalar_mul(stats[:, 192:208], sl, -1.0)
            v.drain()
            v.tensor_max(stats[:, 192:208], stats[:, 192:208], sl)
            v.drain()
            v.tensor_scalar_max(stats[:, 192:208], stats[:, 192:208], 0.25)
            v.drain()
            v.tensor_mul(sl, stats[:, 176:192], stats[:, 192:208])
            v.drain()
        for s0 in (16, 64):
            v.tensor_scalar_add(modcol[:, s0:s0 + 16], modcol[:, s0:s0 + 16], 1.0)
            v.drain()
        gDV.inc(v.nop())
    P0_MODC = gDV.n

    def emit_col2rep(cols_fn, G, dest_fn, pre=()):
        @sec("tensor")
        def _(t):
            for cc, vv in pre:
                t.wait_ge(cc.h, vv)
            t.wait_ge(gDV.h, gDV.n)
            t.transpose(pbig1[0:G, 1536:1664], cols_fn(), ident[:, :])
            gPE.inc(t.drain())
        m1 = gPE.n

        @sec("vector")
        def _(v):
            v.wait_ge(gPE.h, m1)
            v.tensor_copy(colT[0:G, :], pbig1[0:G, 1536:1664])
            gDV.inc(v.drain())
        m2 = gDV.n

        @sec("sync")
        def _(sync):
            sync.wait_ge(gDV.h, m2)
            gIN.inc(sync.dma_start(out=small[0:1, 0:G * 128],
                                   in_=colT[0:G, :]), 16)
        m3 = gIN.n

        @sec("tensor")
        def _(t):
            t.wait_ge(gIN.h, m3)
            for j in range(0, G * 128, 512):
                w = min(512, G * 128 - j)
                t.matmul(pbig1[:, j:j + w], ones[0:1, :], small[0:1, j:j + w],
                         start=True, stop=True)
            gPE.inc(t.drain())
        m4 = gPE.n

        @sec("vector")
        def _(v):
            v.wait_ge(gPE.h, m4)
            v.tensor_copy(dest_fn(), pbig1[:, 0:G * 128])
            gDV.inc(v.drain())
        return gDV.n

    @sec("sync")
    def _(sync):
        sync.wait_ge(dMDM.h, CCH)
        for ic in range(CCH):
            if ic >= 2:
                sync.wait_ge(dMM1.h, ic - 1)
            dX1.issue(sync.dma_start(
                out=(scrA if ic % 2 == 0 else scrB)[:, 0:1024],
                in_=xT[ic * 128:(ic + 1) * 128, :]), ic)

    @sec("scalar")
    def _(sc):
        for ic in range(CCH):
            dX1.wait(sc, ic)
            if ic >= 1:
                sc.wait_ge(dMM1.h, ic)
            sc.activation(scrC[:, 0:1024],
                          (scrA if ic % 2 == 0 else scrB)[:, 0:1024], AF.Square)
            dSQ1.inc(sc.drain())

    @sec("tensor")
    def _(t):
        t.wait_ge(dVA0.h, CCH)
        for ic in range(CCH):
            t.wait_ge(dSQ1.h, ic + 1)
            if ic >= 1:
                t.wait_ge(dVA1.h, ic)
            xin_t = (scrA if ic % 2 == 0 else scrB)
            for tg in range(8):
                t.matmul(pbig0[:, 1536 + tg:1537 + tg],
                         xin_t[:, tg * 128:(tg + 1) * 128], ones[:, 0:1],
                         start=True, stop=True)
                t.matmul(pbig0[:, 1544 + tg:1545 + tg],
                         scrC[:, tg * 128:(tg + 1) * 128], ones[:, 0:1],
                         start=True, stop=True)
            dMM1.inc(t.drain())

    @sec("vector")
    def _(v):
        v.memset(stats[:, 0:16], 0.0)
        v.drain()
        for ic in range(CCH):
            v.wait_ge(dMM1.h, ic + 1)
            v.tensor_add(stats[:, 0:16], stats[:, 0:16], pbig0[:, 1536:1552])
            dVA1.inc(v.drain())
        v.tensor_scalar_mul(stats[:, 32:40], stats[:, 0:8], 1.0 / C)
        v.drain()
        v.tensor_scalar_mul(stats[:, 40:48], stats[:, 8:16], 1.0 / C)
        v.drain()
        v.tensor_mul(stats[:, 208:216], stats[:, 32:40], stats[:, 32:40])
        v.drain()
        v.tensor_sub(stats[:, 40:48], stats[:, 40:48], stats[:, 208:216])
        v.drain()
        v.tensor_scalar_add(stats[:, 40:48], stats[:, 40:48], LN_EPS)
        gDV.inc(v.drain())
    P1_VAR = gDV.n

    @sec("scalar")
    def _(sc):
        sc.wait_ge(gDV.h, P1_VAR)
        sc.activation(stats[:, 40:48], stats[:, 40:48], AF.Sqrt)
        gAC.inc(sc.drain())
    P1_SQRT = gAC.n

    @sec("vector")
    def _(v):
        v.wait_ge(gAC.h, P1_SQRT)
        v.reciprocal(stats[:, 40:48], stats[:, 40:48])
        gDV.inc(v.drain())

    emit_col2rep(lambda: stats[:, 32:40], 8, lambda: bufK[:, 0:1024])
    P1_REPS = emit_col2rep(lambda: stats[:, 40:48], 8, lambda: bufV[:, 0:1024])

    @sec("vector")
    def _(v):
        v.wait_ge(gDV.h, P1_REPS)
        v.memset(bufQ[:, 0:1024], -1e30)
        v.memset(ex[:, 0:1024], 1e30)
        gDV.inc(v.drain())
    P1B_ACC0 = gDV.n

    @sec("sync")
    def _(sync):
        for ic in range(CCH):
            if ic >= 2:
                sync.wait_ge(dHB.h, 3 * (ic - 1))
            dXB.issue(sync.dma_start(
                out=(scrA if ic % 2 == 0 else scrB)[:, 0:1024],
                in_=xT[ic * 128:(ic + 1) * 128, :]), ic)

    @sec("gpsimd")
    def _(g):
        g.wait_ge(gDV.h, P1B_ACC0)
        for ic in range(CCH):
            dXB.wait(g, ic)
            if ic >= 1:
                dSPB.wait(g, ic - 1)
            xin_t = (scrA if ic % 2 == 0 else scrB)
            g.tensor_sub(scrC[:, 0:1024], xin_t[:, 0:1024], bufK[:, 0:1024])
            g.drain()
            g.tensor_mul(scrC[:, 0:1024], scrC[:, 0:1024], bufV[:, 0:1024])
            g.drain()
            g.tensor_scalar(scrC[:, 0:1024], scrC[:, 0:1024],
                            modcol[:, 16 + ic:17 + ic],
                            modcol[:, ic:ic + 1], ALU.mult, ALU.add)
            g.drain()
            dHB.inc(g.nop(), 3)

    @sec("vector")
    def _(v):
        for ic in range(CCH):
            v.wait_ge(dHB.h, 3 * (ic + 1))
            v.tensor_max(bufQ[:, 0:1024], bufQ[:, 0:1024], scrC[:, 0:1024])
            v.drain()
            v.tensor_tensor(ex[:, 0:1024], ex[:, 0:1024], scrC[:, 0:1024],
                            ALU.min)
            dACB.inc(v.drain())

    @sec("scalar")
    def _(sc):
        for ic in range(CCH):
            sc.wait_ge(dACB.h, ic + 1)
            dSPB.issue(sc.dma_start(out=hbuf[ic * 128:(ic + 1) * 128, :],
                                    in_=scrC[:, 0:1024]), ic)

    @sec("tensor")
    def _(t):
        t.wait_ge(dACB.h, CCH)
        for tg in range(8):
            t.transpose(pbig0[:, tg * 128:(tg + 1) * 128],
                        bufQ[:, tg * 128:(tg + 1) * 128], ident[:, :])
            t.transpose(pbig1[:, tg * 128:(tg + 1) * 128],
                        ex[:, tg * 128:(tg + 1) * 128], ident[:, :])
        gPE.inc(t.drain())
    P1B_TR = gPE.n

    @sec("vector")
    def _(v):
        v.wait_ge(gPE.h, P1B_TR)
        v.reduce_max(stats[:, 64:72],
                     pbig0[:, 0:1024].rearrange("p (a b) -> p a b", a=8),
                     axis=AX.X)
        v.drain()
        v.reduce_max(stats[:, 80:88],
                     pbig1[:, 0:1024].rearrange("p (a b) -> p a b", a=8),
                     axis=AX.X, negate=True)
        v.drain()
        v.tensor_max(stats[:, 64:72], stats[:, 64:72], stats[:, 80:88])
        v.drain()
        v.tensor_scalar_max(stats[:, 64:72], stats[:, 64:72], QEPS)
        v.drain()
        v.reciprocal(stats[:, 96:104], stats[:, 64:72])
        v.drain()
        v.tensor_scalar_mul(stats[:, 96:104], stats[:, 96:104], 127.0)
        gDV.inc(v.drain())

    emit_col2rep(lambda: stats[:, 64:72], 8, lambda: smaxrep[:, 0:1024])
    P1B_REPS = emit_col2rep(lambda: stats[:, 96:104], 8, lambda: bufK[:, 0:1024])

    @sec("sync")
    def _(sync):
        sync.wait_ge(gDV.h, P1B_REPS)
        dSPB.wait_all(sync)
        for ic in range(CCH):
            if ic >= 2:
                sync.wait_ge(dQC.h, ic - 1)
            dXC.issue(sync.dma_start(
                out=(scrA if ic % 2 == 0 else scrB)[:, 0:1024],
                in_=hbuf[ic * 128:(ic + 1) * 128, :]), ic)

    @sec("gpsimd")
    def _(g):
        for ic in range(CCH):
            dXC.wait(g, ic)
            if ic >= 1:
                g.wait_ge(dHQ.h, ic)
            xin_t = (scrA if ic % 2 == 0 else scrB)
            g.tensor_mul(scrC[:, 0:1024], xin_t[:, 0:1024], bufK[:, 0:1024])
            g.drain()
            g.tensor_scalar(t16[:, :], scrC[:, 0:1024], 1.0, 256.5,
                            ALU.mult, ALU.add)
            dQC.inc(g.drain())

    @sec("scalar")
    def _(sc):
        for ic in range(CCH):
            sc.wait_ge(dQC.h, ic + 1)
            sc.activation(hqT[:, ic * 1024:(ic + 1) * 1024], t16[:, :],
                          AF.Copy, bias=-256.0)
            dHQ.inc(sc.drain())
    P1C_DONE = dHQ.n

    @sec("sync")
    def _(sync):
        sync.wait_ge(dQC.h, CCH)
        for hh in range(NH):
            for p4 in range(4):
                k = hh * 4 + p4
                if k >= 2:
                    sync.wait_ge(dSG2.h, k - 1)
                st = scrA if k % 2 == 0 else scrB
                dWL2.issue(sync.dma_start(
                    out=st[:, :],
                    in_=wqkvgT[:, k * 2048:(k + 1) * 2048]), k)

    def gp_sign(g, hh):
        for p4 in range(4):
            k = hh * 4 + p4
            dWL2.wait(g, k)
            if hh >= 2 and p4 == 0:
                g.wait_ge(dMMq.h, 4 * (hh - 1))
            st = scrA if k % 2 == 0 else scrB
            dst = (AR2[:, 0:4096] if hh % 2 == 0 else
                   AR2[:, 4096:8192]).bitcast(BF16)
            g.tensor_scalar(dst[:, p4 * 2048:(p4 + 1) * 2048],
                            st[:, :], 0.0, 0.5, ALU.is_gt, ALU.subtract)
            dSG2.inc(g.drain())

    @sec("gpsimd")
    def _(g):
        gp_sign(g, 0)
        gp_sign(g, 1)

    hq3 = hqT.rearrange("p (a b) -> p a b", a=16)
    G0 = gAC.n
    GP0 = gGP.n
    dSPQ = Bank([dWSH.c[0], dWSH.c[1], dABS])
    dKVS = Bank([dMOD.c[0], dMOD.c[1]])
    dTIN = Bank([dMDM, dVA0])

    for hh in range(NH):
        par = hh % 2
        wv = wqg[par]
        sgn_ready = (hh * 4 + 4)

        @sec("tensor")
        def _(t, hh=hh, wv=wv, sgn_ready=sgn_ready):
            t.wait_ge(dSG2.h, sgn_ready)
            if hh == 0:
                t.wait_ge(dHQ.h, P1C_DONE)
                t.wait_ge(gDV.h, P0_KCOL)
                t.wait_ge(gDV.h, P0_MODC)
            else:
                t.wait_ge(dCHd.h, 24 * hh)
                t.wait_ge(dDQq.h, 4 * hh)
            for ti in range(4):
                ptile = pbig0 if ti < 2 else pbig1
                off = 1024 * (ti % 2)
                for kc in range(CCH):
                    lhs = wv[:, kc, ti * 128:(ti + 1) * 128]
                    for h5 in range(2):
                        t.matmul(ptile[:, off + h5 * 512:off + (h5 + 1) * 512],
                                 lhs, hq3[:, kc, h5 * 512:(h5 + 1) * 512],
                                 start=(kc == 0), stop=(kc == CCH - 1))
                dMMq.inc(t.drain())

        @sec("vector")
        def _(v, hh=hh):
            base = 4 * hh
            for ti, dst in ((0, bufQ), (1, bufK), (2, bufV)):
                v.wait_ge(dMMq.h, base + ti + 1)
                if hh > 0:
                    dSPQ.wait(v, 3 * (hh - 1) + ti)
                    if ti >= 1:
                        v.wait_ge(dCHp.h, 16 * hh)
                ptile = pbig0 if ti < 2 else pbig1
                off = 1024 * (ti % 2)
                v.tensor_mul(dst[:, 0:1024], ptile[:, off:off + 1024],
                             smaxrep[:, 0:1024])
                v.drain()
                v.tensor_scalar(dst[:, 0:1024], dst[:, 0:1024], kcol[:, 0:1],
                                bqg[:, hh * 4 + ti:hh * 4 + ti + 1],
                                ALU.mult, ALU.add)
                dDQq.inc(v.drain())
            v.wait_ge(dMMq.h, base + 4)
            if hh > 0:
                v.wait_ge(gAC.h, G0 + hh)
            v.tensor_mul(ex[:, 0:1024], pbig1[:, 1024:2048], smaxrep[:, 0:1024])
            dDQq.inc(v.drain())

        @sec("scalar")
        def _(sc, hh=hh):
            for ti, src in ((0, bufQ), (1, bufK), (2, bufV)):
                sc.wait_ge(dDQq.h, 4 * hh + ti + 1)
                dst = (qbuf, kbuf, vbuf)[ti]
                dSPQ.issue(sc.dma_start(
                    out=dst[:, hh * 1024:(hh + 1) * 1024],
                    in_=src[:, 0:1024]), 3 * hh + ti)
            sc.wait_ge(dDQq.h, 4 * hh + 4)
            sc.activation(ex[:, 0:1024], ex[:, 0:1024], AF.Sigmoid,
                          bias=bqg[:, hh * 4 + 3:hh * 4 + 4],
                          scale=kcol[:, 1:2])
            gAC.inc(sc.drain())

        @sec("vector")
        def _(v, hh=hh):
            v.wait_ge(gAC.h, G0 + hh + 1)
            v.reduce_sum(dcy[:, hh * 8:(hh + 1) * 8],
                         ex[:, 0:1024].rearrange("p (a b) -> p a b", a=8),
                         axis=AX.X)
            v.drain()
            v.tensor_scalar_mul(dcy[:, hh * 8:(hh + 1) * 8],
                                dcy[:, hh * 8:(hh + 1) * 8], 1.0 / 128.0)
            dDCY.inc(v.drain())

        @sec("tensor")
        def _(t, hh=hh):
            t.wait_ge(dDQq.h, 4 * hh + 3)
            for ci in range(NCH):
                k8 = 8 * hh + ci
                cp = ci % 2
                if k8 >= 2:
                    t.wait_ge(dCHd.h, 3 * (k8 - 2) + 1)
                t.transpose(pbig0[:, 256 * cp:256 * cp + 128],
                            bufK[:, ci * 128:(ci + 1) * 128], ident[:, :])
                t.transpose(pbig0[:, 256 * cp + 128:256 * cp + 256],
                            bufV[:, ci * 128:(ci + 1) * 128], ident[:, :])
                dCHp.inc(t.drain())
                t.wait_ge(dCHd.h, 3 * k8 + 1)
                if k8 >= 2:
                    t.wait_ge(dCHd.h, 3 * (k8 - 2) + 3)
                t.matmul(pbig0[:, 1024 + 128 * cp:1152 + 128 * cp],
                         ktvt[cp][:, 0:128], ktvt[cp][:, 128:256],
                         start=True, stop=True)
                dCHp.inc(t.drain())

        @sec("vector")
        def _(v, hh=hh):
            for ci in range(NCH):
                k8 = 8 * hh + ci
                cp = ci % 2
                v.wait_ge(dCHp.h, 2 * k8 + 1)
                if k8 >= 2:
                    v.wait_ge(dCHp.h, 2 * (k8 - 2) + 2)
                v.tensor_copy(ktvt[cp][:, :], pbig0[:, 256 * cp:256 * cp + 256])
                dCHd.inc(v.drain())
                v.wait_ge(dCHp.h, 2 * k8 + 2)
                if k8 >= 2:
                    v.wait_ge(gGP.h, GP0 + k8 - 1)
                    dKVS.wait(v, k8 - 2)
                v.tensor_copy(kvt[cp][:, :],
                              pbig0[:, 1024 + 128 * cp:1152 + 128 * cp])
                dCHd.inc(v.drain(), 2)

        @sec("gpsimd")
        def _(g, hh=hh):
            g.wait_ge(dDCY.h, hh + 1)
            if hh >= 2:
                dTIN.wait(g, hh - 2)
            slot = Tslot[hh % 2]
            g.memset(slot, 0.0)
            g.drain()
            for ci in range(NCH):
                k8 = 8 * hh + ci
                g.wait_ge(dCHd.h, 3 * k8 + 3)
                g.tensor_scalar_mul(slot, slot, dcy[:, k8:k8 + 1])
                g.drain()
                g.tensor_add(slot, slot, kvt[ci % 2][:, :])
                gGP.inc(g.drain())
            if hh + 2 < NH:
                gp_sign(g, hh + 2)

        @sec("scalar")
        def _(sc, hh=hh):
            for ci in range(NCH):
                k8 = 8 * hh + ci
                sc.wait_ge(dCHd.h, 3 * k8 + 3)
                dKVS.issue(sc.dma_start(
                    out=kvbuf[:, k8 * 128:(k8 + 1) * 128],
                    in_=kvt[ci % 2][:, :]), k8)
            sc.wait_ge(gGP.h, GP0 + 8 * (hh + 1))
            dTIN.issue(sc.dma_start(out=T_in[:, hh * 128:(hh + 1) * 128],
                                    in_=Tslot[hh % 2]), hh)

    @sec("gpsimd")
    def _(g):
        dTIN.wait_all(g)
        gCC.inc(g.collective_compute(
            "AllGather", ALU.bypass,
            replica_groups=[[0, 1], [2, 3], [4, 5], [6, 7]],
            ins=[T_in[:, :].opt()], outs=[T_out[:, :].opt()]))
    P2_AG = gCC.n

    @sec("sync")
    def _(sync):
        sync.wait_ge(gCC.h, P2_AG)
        dTOL.inc(sync.dma_start(out=Ssl[:, :], in_=T_out[0:128, :]), 16)
    P2_TOL = dTOL.n

    @sec("gpsimd")
    def _(g):
        g.wait_ge(dTOL.h, P2_TOL)
        g.tensor_scalar_mul(Ssl[:, :], Ssl[:, :], mselc[:, 0:1])
        gGP.inc(g.drain())
    P2_SINIT = gGP.n

    def chbufs(ci):
        return (bufQ, bufK, bufV) if ci % 2 == 0 else (scrA, scrB, scrC)

    ath = [attnT[:, 0:1024], attnT[:, 1024:2048]]

    dQCL = Bank([dXB.c[0], dXB.c[1], dSPB.c[0], dSPB.c[1],
                 dXC.c[0], dXC.c[1]])
    dKVL = Bank([dX1.c[0], dX1.c[1]])
    dMRW = Bank([dSQ1, dMM1])
    bdP = [dWL2.c[0], dWL2.c[1]]
    bdB = [c.n for c in bdP]
    CHB = dCHp.n
    DHB = dCHd.n
    mtsl = [xtmp[0:64, 2816:2832], xtmp[0:64, 2832:2848]]
    pmt = pbig0[0:64, 1984:2000]

    @sec("vector")
    def _(v):
        v.wait_ge(dMMq.h, 64)
        v.memset(bd[0][:, :], 0.0)
        v.drain()
        v.memset(bd[1][:, :], 0.0)
        gDV.inc(v.drain())
    P2_BD0 = gDV.n

    @sec("sync")
    def _(sync):
        sync.wait_ge(gDV.h, P2_BD0)
        for ci in range(NCH):
            if ci == 0:
                dSPQ.wait_all(sync)
            if ci >= 2:
                sync.wait_ge(dSC.h, 10 * (ci - 1))
                sync.wait_ge(dGRD.h, 48 * (ci - 1))
                sync.wait_ge(dREC.h, 16 * (ci - 1))
            qb, kb, vb = chbufs(ci)
            for j3, (srcb, dstb) in enumerate(((qbuf, qb), (kbuf, kb),
                                               (vbuf, vb))):
                dQCL.issue(sync.dma_start(
                    out=dstb[:, 0:2048].rearrange("p (a b) -> p a b", a=16),
                    in_=srcb[:, :].rearrange("p (a b) -> p a b", a=16)[
                        :, :, ci * 128:(ci + 1) * 128]), 3 * ci + j3)
            for half in range(2):
                H = 2 * ci + half
                sync.wait_ge(dCHd.h, DHB + H + 1)
                if H >= 2:
                    sync.wait_ge(dSC.h, 5 * (H - 2) + 3)
                dMRW.issue(sync.dma_start(
                    out=Mrow[half][:, :], in_=mtsl[half][:, :]), H)
            for half in range(2):
                H = 2 * ci + half
                sync.wait_ge(dDN.h, 5 * H + 5)
                for j in range(8 * half, 8 * half + 8):
                    Gn = 16 * ci + j
                    if Gn >= 2:
                        sync.wait_ge(dGRP.h, 2 * (Gn - 2) + 2)
                    for l4 in range(8):
                        l = 8 * j + l4
                        bdP[j % 2].inc(sync.dma_start(
                            out=bd[j % 2][16 * l4:16 * l4 + 16,
                                          16 * l4:16 * l4 + 16],
                            in_=attnT[:, l * 16:(l + 1) * 16]), 16)

    @sec("scalar")
    def _(sc):
        for ci in range(NCH):
            for half in range(2):
                H = 2 * ci + half
                sc.wait_ge(dDN.h, 5 * H + 3)
                sc.activation(ath[half], ath[half], AF.Exp)
                gAC.inc(sc.drain())
            if ci == NCH - 1:
                continue
            if ci == 0:
                dKVS.wait_all(sc)
                sc.wait_ge(dMMq.h, 64)
            else:
                sc.wait_ge(gGP.h, P2_SINIT + 16 * ci)
            dKVL.issue(sc.dma_start(
                out=kvsl[:, :].rearrange("p (a b) -> p a b", a=16),
                in_=kvbuf[:, :].rearrange("p (a b) -> p a b", a=16)[
                    :, :, ci * 128:(ci + 1) * 128]), ci)

    ACB = gAC.n - 16

    @sec("tensor")
    def _(t):
        t.wait_ge(dCHd.h, 384)
        t.wait_ge(dDQq.h, 64)
        for ci in range(NCH):
            qb, kb, vb = chbufs(ci)
            qhl = qb[:, 0:2048].rearrange("p (a b) -> p a b", a=16)
            khl = kb[:, 0:2048].rearrange("p (a b) -> p a b", a=16)
            dQCL.wait(t, 3 * ci + 0)
            dQCL.wait(t, 3 * ci + 1)
            for half in range(2):
                H = 2 * ci + half
                if H >= 1:
                    t.wait_ge(dDN.h, 5 * (H - 1) + 1)
                for l in range(64 * half, 64 * half + 64):
                    lo = (l - 64 * half) * 16
                    t.matmul(psc[:, lo:lo + 16], qhl[:, :, l], khl[:, :, l],
                             start=True, stop=True)
                dSC.inc(t.drain())
                if H >= 1:
                    t.wait_ge(dDN.h, 5 * (H - 1) + 3)
                for l in range(64 * half, 64 * half + 64):
                    lo = (l - 64 * half) * 16
                    t.matmul(pscT[:, lo:lo + 16], khl[:, :, l], qhl[:, :, l],
                             start=True, stop=True)
                dSC.inc(t.drain())
                t.wait_ge(dDN.h, 5 * H + 1)
                if H >= 2:
                    t.wait_ge(dCHd.h, DHB + H - 1)
                t.transpose(pmt[:, :], Mst[half][:, 0:64], ident[0:16, 0:16])
                dCHp.inc(t.drain())
                dMRW.wait(t, H)
                if H >= 1:
                    t.wait_ge(dDN.h, 5 * (H - 1) + 5)
                for s5 in range(2):
                    t.matmul(pdn16[:, s5 * 512:(s5 + 1) * 512],
                             ones[32:33, 0:16],
                             Mrow[half][:, s5 * 512:(s5 + 1) * 512],
                             start=True, stop=True)
                dSC.inc(t.drain())
                t.wait_ge(gAC.h, ACB + H + 1)
                t.wait_ge(dDN.h, 5 * H + 2)
                for s5 in range(2):
                    t.matmul(pdn1[:, s5 * 512:(s5 + 1) * 512], ones[0:16, 0:1],
                             ath[half][:, s5 * 512:(s5 + 1) * 512],
                             start=True, stop=True)
                dSC.inc(t.drain())
                t.wait_ge(dDN.h, 5 * H + 4)
                for s5 in range(2):
                    t.matmul(pdn16[:, s5 * 512:(s5 + 1) * 512],
                             ones[64:65, 0:16],
                             rzrow[half][:, s5 * 512:(s5 + 1) * 512],
                             start=True, stop=True)
                dSC.inc(t.drain())
            for j in range(16):
                Gn = 16 * ci + j
                jp = j % 2
                t.wait_ge(dGRD.h, 3 * Gn + 1)
                if Gn >= 2:
                    t.wait_ge(dGRD.h, 3 * (Gn - 2) + 2)
                t.transpose(pv[jp][:, :], v2g[jp][:, :], ident[:, :])
                dGRP.inc(t.drain())
                t.wait_ge(dGRD.h, 3 * Gn + 2)
                t.wait_ge(bdP[Gn % 2].h,
                          bdB[Gn % 2] + 128 * (Gn // 2 + 1))
                if Gn >= 2:
                    t.wait_ge(dGRD.h, 3 * (Gn - 2) + 3)
                t.matmul(plo[jp][:, :], v2t[jp][:, :], bd[jp][:, :],
                         start=True, stop=True)
                dGRP.inc(t.drain())
            t.wait_ge(gGP.h, P2_SINIT + 16 * ci)
            for h in range(16):
                hp = h % 2
                if 16 * ci + h >= 2:
                    t.wait_ge(dRAD.h, 16 * ci + h - 1)
                t.matmul(prec[hp][:, :], Ssl[:, h * 128:(h + 1) * 128],
                         qb[:, h * 128:(h + 1) * 128], start=True, stop=True)
                dREC.inc(t.drain())

    @sec("vector")
    def _(v):
        for ci in range(NCH):
            qb, kb, vb = chbufs(ci)
            vlh = vb[:, 0:2048].rearrange("p (a b) -> p b a", a=16)
            for half in range(2):
                H = 2 * ci + half
                v.wait_ge(dSC.h, 5 * H + 1)
                if H >= 2:
                    v.wait_ge(dCHp.h, CHB + H - 1)
                v.reduce_max(Mst[half][:, 0:64],
                             psc[:, :].rearrange("p (a b) -> p a b", a=64),
                             axis=AX.X)
                dDN.inc(v.drain())
                v.wait_ge(dCHp.h, CHB + H + 1)
                if H >= 2:
                    dMRW.wait(v, H - 2)
                v.tensor_copy(mtsl[half][:, :], pmt[:, :])
                dCHd.inc(v.drain())
                v.wait_ge(dSC.h, 5 * H + 3)
                v.tensor_copy(Mrep_sb[half][:, :], pdn16[:, :])
                dDN.inc(v.drain())
                if H >= 2:
                    v.wait_ge(bdP[0].h, bdB[0] + 128 * 4 * (H - 1))
                    v.wait_ge(bdP[1].h, bdB[1] + 128 * 4 * (H - 1))
                v.scalar_tensor_tensor(ath[half], pscT[:, :], ATT_SCALE,
                                       Mrep_sb[half][:, :], ALU.mult,
                                       ALU.subtract)
                dDN.inc(v.drain())
                v.wait_ge(dSC.h, 5 * H + 4)
                if H >= 2:
                    v.wait_ge(dSC.h, 5 * (H - 2) + 5)
                v.reciprocal(rzrow[half][:, :], pdn1[:, :])
                dDN.inc(v.drain())
                v.wait_ge(dSC.h, 5 * H + 5)
                v.tensor_mul(ath[half], ath[half], pdn16[:, :])
                dDN.inc(v.drain())
            dQCL.wait(v, 3 * ci + 2)
            for j in range(16):
                Gn = 16 * ci + j
                jp = j % 2
                if Gn >= 2:
                    v.wait_ge(dGRP.h, 2 * (Gn - 2) + 1)
                v.tensor_copy(
                    v2g[jp][:, :].rearrange("p (a b) -> p a b", a=8),
                    vlh[:, 8 * j:8 * j + 8, :])
                dGRD.inc(v.drain())
                v.wait_ge(dGRP.h, 2 * Gn + 1)
                v.tensor_copy(v2t[jp][:, :], pv[jp][:, :])
                dGRD.inc(v.drain())
                v.wait_ge(dGRP.h, 2 * Gn + 2)
                v.tensor_copy(
                    olh[:, ci * 128 + 8 * j:ci * 128 + 8 * j + 8, :],
                    plo[jp][:, :].rearrange("p (a b) -> p a b", a=8))
                dGRD.inc(v.drain())
            for h in range(16):
                v.wait_ge(dREC.h, 16 * ci + h + 1)
                v.tensor_add(outf[:, h, ci * 128:(ci + 1) * 128],
                             outf[:, h, ci * 128:(ci + 1) * 128],
                             prec[h % 2][:, :])
                dRAD.inc(v.drain())

    @sec("gpsimd")
    def _(g):
        for ci in range(NCH - 1):
            dKVL.wait(g, ci)
            g.wait_ge(dREC.h, 16 * (ci + 1))
            for h in range(16):
                g.tensor_scalar_mul(Ssl[:, h * 128:(h + 1) * 128],
                                    Ssl[:, h * 128:(h + 1) * 128],
                                    dcy[:, h * 8 + ci:h * 8 + ci + 1])
                g.drain()
                g.tensor_add(Ssl[:, h * 128:(h + 1) * 128],
                             Ssl[:, h * 128:(h + 1) * 128],
                             kvsl[:, h * 128:(h + 1) * 128])
                g.drain()
            gGP.inc(g.nop(), 16)
    P2_OUTF = 128

    def emit_projload(sync, ot):
        dPJL.issue(sync.dma_start(
            out=(scrA if ot % 2 == 0 else scrB)[:, :],
            in_=wprojT[:, ot * 2048:(ot + 1) * 2048]), ot)

    @sec("sync")
    def _(sync):
        sync.wait_ge(dREC.h, 128)
        sync.wait_ge(dGRD.h, 384)
        emit_projload(sync, 0)
        emit_projload(sync, 1)

    @sec("gpsimd")
    def _(g):
        for ot in range(CCH):
            dPJL.wait(g, ot)
            if ot < 2:
                g.wait_ge(dMMq.h, 64)
            if ot >= 2:
                g.wait_ge(dPJM.h, ot - 1)
            st = scrA if ot % 2 == 0 else scrB
            par = ot % 2
            dst = (AR2[:, 0:2816] if par == 0 else AR2[:, 2816:5632]).bitcast(BF16)
            g.tensor_scalar(dst[:, 0:2048], st[:, :], 0.0, 0.5,
                            ALU.is_gt, ALU.subtract)
            dPJS.inc(g.drain())

    @sec("scalar")
    def _(sc):
        sc.wait_ge(dRAD.h, P2_OUTF)
        for fc in range(CCH):
            if fc >= 1:
                sc.wait_ge(dPJD.h, fc)
            sc.activation(scrC[:, 0:1024], outf[:, fc, :], AF.Square)
            dLN2.inc(sc.drain())

    @sec("tensor")
    def _(t):
        for fc in range(CCH):
            t.wait_ge(dLN2.h, fc + 1)
            if fc >= 1:
                t.wait_ge(dVA2.h, fc)
            for tg in range(8):
                t.matmul(pbig1[:, 1536 + tg:1537 + tg],
                         scrC[:, tg * 128:(tg + 1) * 128], ones[:, 0:1],
                         start=True, stop=True)
            dPJD.inc(t.drain())
    RMS_MM = dPJD.n

    @sec("vector")
    def _(v):
        v.memset(stats[:, 0:8], 0.0)
        v.drain()
        for fc in range(CCH):
            v.wait_ge(dPJD.h, fc + 1)
            v.tensor_add(stats[:, 0:8], stats[:, 0:8], pbig1[:, 1536:1544])
            dVA2.inc(v.drain())
        v.tensor_scalar(stats[:, 8:16], stats[:, 0:8], 1.0 / C, RMS_EPS,
                        ALU.mult, ALU.add)
        gDV.inc(v.drain())
    P4_MS = gDV.n

    @sec("scalar")
    def _(sc):
        sc.wait_ge(gDV.h, P4_MS)
        sc.activation(stats[:, 8:16], stats[:, 8:16], AF.Sqrt)
        gAC.inc(sc.drain())
    P4_SQ = gAC.n

    @sec("vector")
    def _(v):
        v.wait_ge(gAC.h, P4_SQ)
        v.reciprocal(stats[:, 8:16], stats[:, 8:16])
        gDV.inc(v.drain())

    emit_col2rep(lambda: stats[:, 8:16], 8, lambda: bufV[:, 0:1024])

    @sec("vector")
    def _(v):
        v.wait_ge(gDV.h, gDV.n)
        v.memset(bufQ[:, 0:1024], -1e30)
        v.memset(bufQ[:, 1024:2048], 1e30)
        v.drain()
        for fc in range(CCH):
            v.tensor_mul(scrC[:, 0:1024], outf[:, fc, :], bufV[:, 0:1024])
            v.drain()
            v.tensor_scalar_mul(scrC[:, 0:1024], scrC[:, 0:1024],
                                nwc[:, fc:fc + 1])
            v.drain()
            v.tensor_max(bufQ[:, 0:1024], bufQ[:, 0:1024], scrC[:, 0:1024])
            v.drain()
            v.tensor_tensor(bufQ[:, 1024:2048], bufQ[:, 1024:2048],
                            scrC[:, 0:1024], ALU.min)
            v.drain()
        gDV.inc(v.nop())
    P4_ACC = gDV.n

    @sec("tensor")
    def _(t):
        t.wait_ge(gDV.h, P4_ACC)
        for tg in range(8):
            t.transpose(pbig1[:, tg * 128:(tg + 1) * 128],
                        bufQ[:, tg * 128:(tg + 1) * 128], ident[:, :])
            t.transpose(pbig1[:, 1024 + tg * 128:1024 + (tg + 1) * 128],
                        bufQ[:, 1024 + tg * 128:1024 + (tg + 1) * 128],
                        ident[:, :])
        gPE.inc(t.drain())
    P4_TR = gPE.n

    @sec("vector")
    def _(v):
        v.wait_ge(gPE.h, P4_TR)
        v.reduce_max(stats[:, 16:24],
                     pbig1[:, 0:1024].rearrange("p (a b) -> p a b", a=8),
                     axis=AX.X)
        v.drain()
        v.reduce_max(stats[:, 24:32],
                     pbig1[:, 1024:2048].rearrange("p (a b) -> p a b", a=8),
                     axis=AX.X, negate=True)
        v.drain()
        v.tensor_max(stats[:, 16:24], stats[:, 16:24], stats[:, 24:32])
        v.drain()
        v.tensor_scalar_max(stats[:, 16:24], stats[:, 16:24], QEPS)
        v.drain()
        v.reciprocal(stats[:, 24:32], stats[:, 16:24])
        v.drain()
        v.tensor_scalar_mul(stats[:, 24:32], stats[:, 24:32], 127.0)
        gDV.inc(v.drain())

    emit_col2rep(lambda: stats[:, 16:24], 8, lambda: bufK[:, 1024:2048])
    P4_REP2 = emit_col2rep(lambda: stats[:, 24:32], 8, lambda: bufK[:, 0:1024])

    @sec("vector")
    def _(v):
        v.wait_ge(gDV.h, P4_REP2)
        v.tensor_mul(bufV[:, 0:1024], bufV[:, 0:1024], bufK[:, 0:1024])
        gDV.inc(v.drain())
    P4_COMB = gDV.n

    @sec("vector")
    def _(v):
        v.wait_ge(gDV.h, P4_COMB)
        for fc in range(CCH):
            if fc >= 1:
                v.wait_ge(dPQB.h, fc)
            v.tensor_mul(scrC[:, 0:1024], outf[:, fc, :], bufV[:, 0:1024])
            v.drain()
            v.tensor_scalar(t16[:, 0:1024], scrC[:, 0:1024], nwc[:, fc:fc + 1],
                            256.5, ALU.mult, ALU.add)
            dPQA.inc(v.drain())

    @sec("scalar")
    def _(sc):
        for fc in range(CCH):
            sc.wait_ge(dPQA.h, fc + 1)
            sc.activation(pjq[:, fc, :], t16[:, 0:1024], AF.Copy, bias=-256.0)
            dPQB.inc(sc.drain())
    P4_PJQ = dPQB.n

    @sec("sync")
    def _(sync):
        sync.wait_ge(dPQB.h, P4_PJQ)
        for ot in range(CCH):
            if ot >= 2:
                dXNW.wait(sync, ot - 2)
            dst = bufK[:, 0:1024] if ot % 2 == 0 else bufV[:, 1024:2048]
            dX2L.issue(sync.dma_start(out=dst,
                                      in_=xT[ot * 128:(ot + 1) * 128, :]), ot)
            if ot + 2 < CCH:
                sync.wait_ge(dPJS.h, ot + 1)
                emit_projload(sync, ot + 2)

    def p4_statmm(t, j):
        t.wait_ge(dH2.h, j + 1)
        if j >= 1:
            t.wait_ge(dVA3.h, j)
        for tg in range(8):
            t.matmul(pbig1[:, 1536 + tg:1537 + tg],
                     scrC[:, tg * 128:(tg + 1) * 128], ones[:, 0:1],
                     start=True, stop=True)
            t.matmul(pbig1[:, 1544 + tg:1545 + tg],
                     small[:, 768 + tg * 128:768 + (tg + 1) * 128],
                     ones[:, 0:1], start=True, stop=True)
        dH2A.inc(t.drain())

    @sec("tensor")
    def _(t):
        t.wait_ge(dPQB.h, P4_PJQ)
        for ot in range(CCH):
            t.wait_ge(dPJS.h, ot + 1)
            if ot >= 2:
                t.wait_ge(dPJD.h, RMS_MM + ot - 1)
            par = ot % 2
            ptile = pbig0[:, 0:1024] if par == 0 else pbig0[:, 1024:2048]
            for kc in range(CCH):
                lhs = (AR2[:, 0:2816] if par == 0 else
                       AR2[:, 2816:5632]).bitcast(BF16)[:, kc * 128:(kc + 1) * 128]
                for h5 in range(2):
                    t.matmul(ptile[:, h5 * 512:(h5 + 1) * 512], lhs,
                             pjq[:, kc, h5 * 512:(h5 + 1) * 512],
                             start=(kc == 0), stop=(kc == CCH - 1))
            dPJM.inc(t.drain())
            if ot >= 2:
                p4_statmm(t, ot - 2)
        p4_statmm(t, CCH - 2)
        p4_statmm(t, CCH - 1)

    def p4_statacc(v, j):
        v.wait_ge(dH2A.h, j + 1)
        v.tensor_add(stats[:, 32:48], stats[:, 32:48], pbig1[:, 1536:1552])
        dVA3.inc(v.drain())

    @sec("vector")
    def _(v):
        v.memset(stats[:, 32:48], 0.0)
        v.drain()
        for ot in range(CCH):
            if ot >= 2:
                p4_statacc(v, ot - 2)
            v.wait_ge(dPJM.h, ot + 1)
            par = ot % 2
            ptile = pbig0[:, 0:1024] if par == 0 else pbig0[:, 1024:2048]
            v.tensor_mul(scrC[:, 1024:2048], ptile, bufK[:, 1024:2048])
            v.drain()
            v.tensor_scalar(scrC[:, 1024:2048], scrC[:, 1024:2048],
                            kcol[:, 2:3], bpj[:, ot:ot + 1], ALU.mult, ALU.add)
            dPJD.inc(v.drain())
            dX2L.wait(v, ot)
            if ot >= 1:
                v.wait_ge(dH2.h, ot)
                v.wait_ge(dH2A.h, ot)
            x2t = bufK[:, 0:1024] if ot % 2 == 0 else bufV[:, 1024:2048]
            v.scalar_tensor_tensor(scrC[:, 0:1024], scrC[:, 1024:2048],
                                   modcol[:, 32 + ot:33 + ot], x2t,
                                   ALU.mult, ALU.add)
            dXN2.inc(v.drain())
        p4_statacc(v, CCH - 2)
        p4_statacc(v, CCH - 1)

    @sec("scalar")
    def _(sc):
        for ot in range(CCH):
            sc.wait_ge(dXN2.h, ot + 1)
            dXNW.issue(sc.dma_start(out=xnewbuf[ot * 128:(ot + 1) * 128, :],
                                    in_=scrC[:, 0:1024]), ot)
            dXNW.wait(sc, ot)
            sc.activation(small[:, 768:1792], scrC[:, 0:1024], AF.Square)
            dH2.inc(sc.drain())


    @sec("vector")
    def _(v):
        v.wait_ge(dVA3.h, CCH)
        v.tensor_scalar_mul(stats[:, 32:40], stats[:, 32:40], 1.0 / C)
        v.drain()
        v.tensor_scalar_mul(stats[:, 40:48], stats[:, 40:48], 1.0 / C)
        v.drain()
        v.tensor_mul(stats[:, 48:56], stats[:, 32:40], stats[:, 32:40])
        v.drain()
        v.tensor_sub(stats[:, 40:48], stats[:, 40:48], stats[:, 48:56])
        v.drain()
        v.tensor_scalar_add(stats[:, 40:48], stats[:, 40:48], LN_EPS)
        gDV.inc(v.drain())
    P5_VAR = gDV.n

    @sec("scalar")
    def _(sc):
        sc.wait_ge(gDV.h, P5_VAR)
        sc.activation(stats[:, 40:48], stats[:, 40:48], AF.Sqrt)
        gAC.inc(sc.drain())
    P5_SQRT = gAC.n

    @sec("vector")
    def _(v):
        v.wait_ge(gAC.h, P5_SQRT)
        v.reciprocal(stats[:, 40:48], stats[:, 40:48])
        gDV.inc(v.drain())

    emit_col2rep(lambda: stats[:, 32:40], 8, lambda: ex[:, 0:1024])
    P5_REPS = emit_col2rep(lambda: stats[:, 40:48], 8, lambda: ex[:, 1024:2048])

    @sec("vector")
    def _(v):
        v.wait_ge(gDV.h, P5_REPS)
        v.memset(bufQ[:, 0:1024], -1e30)
        v.memset(bufQ[:, 1024:2048], 1e30)
        gDV.inc(v.drain())
    P5_ACC0 = gDV.n

    @sec("sync")
    def _(sync):
        sync.wait_ge(gDV.h, P5_ACC0)
        dXNW.wait_all(sync)
        for ic in range(CCH):
            if ic >= 2:
                dH2S.wait(sync, ic - 2)
            dst = scrA if ic % 2 == 0 else scrB
            dXNL.issue(sync.dma_start(out=dst[:, 0:1024],
                                      in_=xnewbuf[ic * 128:(ic + 1) * 128, :]), ic)

    @sec("gpsimd")
    def _(g):
        for ic in range(CCH):
            dXNL.wait(g, ic)
            if ic >= 1:
                dH2S.wait(g, ic - 1)
            xin_t = (scrA if ic % 2 == 0 else scrB)[:, 0:1024]
            g.tensor_sub(scrC[:, 0:1024], xin_t, ex[:, 0:1024])
            g.drain()
            g.tensor_mul(scrC[:, 0:1024], scrC[:, 0:1024], ex[:, 1024:2048])
            g.drain()
            g.tensor_scalar(scrC[:, 0:1024], scrC[:, 0:1024],
                            modcol[:, 64 + ic:65 + ic],
                            modcol[:, 48 + ic:49 + ic], ALU.mult, ALU.add)
            dH2.inc(g.drain())

    @sec("vector")
    def _(v):
        for ic in range(CCH):
            v.wait_ge(dH2.h, CCH + ic + 1)
            v.tensor_max(bufQ[:, 0:1024], bufQ[:, 0:1024], scrC[:, 0:1024])
            v.drain()
            v.tensor_tensor(bufQ[:, 1024:2048], bufQ[:, 1024:2048],
                            scrC[:, 0:1024], ALU.min)
            dH2A.inc(v.drain())

    @sec("scalar")
    def _(sc):
        for ic in range(CCH):
            sc.wait_ge(dH2A.h, CCH + ic + 1)
            dH2S.issue(sc.dma_start(out=hbuf[ic * 128:(ic + 1) * 128, 0:1024],
                                    in_=scrC[:, 0:1024]), ic)

    @sec("tensor")
    def _(t):
        t.wait_ge(dH2A.h, 2 * CCH)
        for tg in range(8):
            t.transpose(pbig1[:, tg * 128:(tg + 1) * 128],
                        bufQ[:, tg * 128:(tg + 1) * 128], ident[:, :])
            t.transpose(pbig1[:, 1024 + tg * 128:1024 + (tg + 1) * 128],
                        bufQ[:, 1024 + tg * 128:1024 + (tg + 1) * 128],
                        ident[:, :])
        gPE.inc(t.drain())
    P5_TR = gPE.n

    @sec("vector")
    def _(v):
        v.wait_ge(gPE.h, P5_TR)
        v.reduce_max(stats[:, 48:56],
                     pbig1[:, 0:1024].rearrange("p (a b) -> p a b", a=8),
                     axis=AX.X)
        v.drain()
        v.reduce_max(stats[:, 56:64],
                     pbig1[:, 1024:2048].rearrange("p (a b) -> p a b", a=8),
                     axis=AX.X, negate=True)
        v.drain()
        v.tensor_max(stats[:, 48:56], stats[:, 48:56], stats[:, 56:64])
        v.drain()
        v.tensor_scalar_max(stats[:, 48:56], stats[:, 48:56], QEPS)
        v.drain()
        v.reciprocal(stats[:, 56:64], stats[:, 48:56])
        v.drain()
        v.tensor_scalar_mul(stats[:, 56:64], stats[:, 56:64], 127.0)
        gDV.inc(v.drain())

    P5_R127 = emit_col2rep(lambda: stats[:, 56:64], 8, lambda: bufV[:, 0:1024])

    @sec("sync")
    def _(sync):
        sync.wait_ge(gDV.h, P5_R127)
        dH2S.wait_all(sync)
        for ic in range(CCH):
            if ic >= 2:
                sync.wait_ge(dH2Q.h, ic - 1)
            dst = scrA if ic % 2 == 0 else scrB
            dH2L.issue(sync.dma_start(out=dst[:, 0:1024],
                                      in_=hbuf[ic * 128:(ic + 1) * 128, 0:1024]),
                       ic)

    @sec("gpsimd")
    def _(g):
        for ic in range(CCH):
            dH2L.wait(g, ic)
            if ic >= 1:
                g.wait_ge(dH2Q.h, ic)
            xin_t = (scrA if ic % 2 == 0 else scrB)[:, 0:1024]
            g.tensor_mul(scrC[:, 0:1024], xin_t, bufV[:, 0:1024])
            g.drain()
            g.tensor_scalar(t16[:, 0:1024], scrC[:, 0:1024], 1.0, 256.5,
                            ALU.mult, ALU.add)
            dH2.inc(g.drain())

    @sec("scalar")
    def _(sc):
        for ic in range(CCH):
            sc.wait_ge(dH2.h, 2 * CCH + ic + 1)
            sc.activation(h2q[:, ic, :], t16[:, 0:1024], AF.Copy, bias=-256.0)
            dH2Q.inc(sc.drain())
    P5_H2Q = dH2Q.n

    emit_col2rep(lambda: stats[:, 48:56], 8, lambda: bufK[:, 0:1024])

    @sec("vector")
    def _(v):
        v.wait_ge(gDV.h, gDV.n)
        v.tensor_scalar_mul(bufK[:, 1024:2048], bufK[:, 0:1024], kcol[:, 8:9])
        v.drain()
        v.tensor_scalar_mul(bufK[:, 0:1024], bufK[:, 0:1024], kcol[:, 6:7])
        gDV.inc(v.drain())
    P5_REP13 = gDV.n

    def p6a_st(ot):
        if ot % 2 == 0:
            return scrA, scrB
        return small, bufV

    @sec("sync")
    def _(sync):
        sync.wait_ge(dH2Q.h, P5_H2Q)
        sync.wait_ge(gDV.h, P5_REP13)
        for ot in range(HCH):
            if ot >= 2:
                sync.wait_ge(dFWV.h, 2 * (ot - 1))
            s1t, s3t = p6a_st(ot)
            dFWL.issue(sync.dma_start(
                out=s1t[:, :],
                in_=w1T[:, ot * 2048:(ot + 1) * 2048]), 2 * ot)
            dFWL.issue(sync.dma_start(
                out=s3t[:, :],
                in_=w3T[:, ot * 2048:(ot + 1) * 2048]), 2 * ot + 1)

    @sec("gpsimd")
    def _(g):
        for ot in range(HCH):
            dFWL.wait(g, 2 * ot)
            dFWL.wait(g, 2 * ot + 1)
            if ot >= 2:
                g.wait_ge(dFMM.h, ot - 1)
            s1t, s3t = p6a_st(ot)
            par = ot % 2
            w1d = (AR2[:, 0:2816] if par == 0 else AR2[:, 2816:5632]).bitcast(
                BF16)[:, 0:2048]
            w3d = (AR2[:, 0:2816] if par == 0 else AR2[:, 2816:5632]).bitcast(
                BF16)[:, 2048:4096]
            for (wt, dst, thr, nthr) in ((s1t, w1d, 3, 9), (s3t, w3d, 5, 11)):
                g.tensor_scalar(dst, wt[:, :], kcol[:, thr:thr + 1], None,
                                ALU.is_ge)
                g.drain()
            dFWS.inc(g.nop(), 2)

    @sec("tensor")
    def _(t):
        for ot in range(HCH):
            t.wait_ge(dFWV.h, 2 * (ot + 1))
            if ot >= 2:
                t.wait_ge(dFH3.h, 3 * (ot - 2) + 2)
            par = ot % 2
            ptile = pbig0 if par == 0 else pbig1
            wb = (AR2[:, 0:2816] if par == 0 else AR2[:, 2816:5632]).bitcast(BF16)
            for kc in range(CCH):
                l1 = wb[:, kc * 128:(kc + 1) * 128]
                l3 = wb[:, 2048 + kc * 128:2048 + (kc + 1) * 128]
                for h5 in range(2):
                    t.matmul(ptile[:, h5 * 512:(h5 + 1) * 512], l1,
                             h2q[:, kc, h5 * 512:(h5 + 1) * 512],
                             start=(kc == 0), stop=(kc == CCH - 1))
                    t.matmul(ptile[:, 1024 + h5 * 512:1024 + (h5 + 1) * 512], l3,
                             h2q[:, kc, h5 * 512:(h5 + 1) * 512],
                             start=(kc == 0), stop=(kc == CCH - 1))
            dFMM.inc(t.drain())

    P6A_GAC0 = gAC.n

    def p6a_pass2(v, ot):
        v.wait_ge(dFWS.h, 2 * (ot + 1))
        s1t, s3t = p6a_st(ot)
        par2 = ot % 2
        w1d = (AR2[:, 0:2816] if par2 == 0 else AR2[:, 2816:5632]).bitcast(
            BF16)[:, 0:2048]
        w3d = (AR2[:, 0:2816] if par2 == 0 else AR2[:, 2816:5632]).bitcast(
            BF16)[:, 2048:4096]
        for (wt, dst, nthr) in ((s1t, w1d, 9), (s3t, w3d, 11)):
            v.scalar_tensor_tensor(dst, wt[:, :], kcol[:, nthr:nthr + 1],
                                   dst, ALU.is_le, ALU.subtract)
            dFWV.inc(v.drain())

    @sec("vector")
    def _(v):
        v.memset(ex[:, 0:1024], -1e30)
        v.memset(ex[:, 1024:2048], 1e30)
        v.drain()
        p6a_pass2(v, 0)
        p6a_pass2(v, 1)
        for ot in range(HCH):
            if ot + 2 < HCH:
                p6a_pass2(v, ot + 2)
            v.wait_ge(dFMM.h, ot + 1)
            if ot >= 2:
                dFSP.wait(v, ot - 2)
            par = ot % 2
            ptile = pbig0 if par == 0 else pbig1
            h3o = scrC[:, 0:1024] if ot % 2 == 0 else scrC[:, 1024:2048]
            v.tensor_mul(bufQ[:, 0:1024], ptile[:, 0:1024], bufK[:, 0:1024])
            dFH3.inc(v.drain())
            v.tensor_mul(bufQ[:, 1024:2048], ptile[:, 1024:2048],
                         bufK[:, 1024:2048])
            dFH3.inc(v.drain())
            gAC_t = P6A_GAC0 + ot + 1

            v.wait_ge(gAC.h, gAC_t)
            v.tensor_mul(h3o, bufQ[:, 0:1024], bufQ[:, 1024:2048])
            v.drain()
            v.tensor_max(ex[:, 0:1024], ex[:, 0:1024], h3o)
            v.drain()
            v.tensor_tensor(ex[:, 1024:2048], ex[:, 1024:2048], h3o, ALU.min)
            dFH3.inc(v.drain())

    @sec("scalar")
    def _(sc):
        for ot in range(HCH):
            sc.wait_ge(dFH3.h, 3 * ot + 1)
            sc.activation(bufQ[:, 0:1024], bufQ[:, 0:1024], AF.Silu)
            gAC.inc(sc.drain())
            sc.wait_ge(dFH3.h, 3 * ot + 3)
            h3o = scrC[:, 0:1024] if ot % 2 == 0 else scrC[:, 1024:2048]
            dFSP.issue(sc.dma_start(out=h3buf[ot * 128:(ot + 1) * 128, :],
                                    in_=h3o), ot)

    @sec("tensor")
    def _(t):
        t.wait_ge(dFH3.h, 3 * HCH)
        for tg in range(8):
            t.transpose(pbig1[:, tg * 128:(tg + 1) * 128],
                        ex[:, tg * 128:(tg + 1) * 128], ident[:, :])
            t.transpose(pbig1[:, 1024 + tg * 128:1024 + (tg + 1) * 128],
                        ex[:, 1024 + tg * 128:1024 + (tg + 1) * 128],
                        ident[:, :])
        gPE.inc(t.drain())
    P6_TR = gPE.n

    @sec("vector")
    def _(v):
        v.wait_ge(gPE.h, P6_TR)
        v.reduce_max(stats[:, 64:72],
                     pbig1[:, 0:1024].rearrange("p (a b) -> p a b", a=8),
                     axis=AX.X)
        v.drain()
        v.reduce_max(stats[:, 72:80],
                     pbig1[:, 1024:2048].rearrange("p (a b) -> p a b", a=8),
                     axis=AX.X, negate=True)
        v.drain()
        v.tensor_max(stats[:, 64:72], stats[:, 64:72], stats[:, 72:80])
        v.drain()
        v.tensor_scalar_max(stats[:, 64:72], stats[:, 64:72], QEPS)
        v.drain()
        v.reciprocal(stats[:, 72:80], stats[:, 64:72])
        v.drain()
        v.tensor_scalar_mul(stats[:, 72:80], stats[:, 72:80], 127.0)
        gDV.inc(v.drain())

    emit_col2rep(lambda: stats[:, 72:80], 8, lambda: bufK[:, 0:1024])
    P6_R2A = emit_col2rep(lambda: stats[:, 64:72], 8, lambda: bufK[:, 1024:2048])

    @sec("vector")
    def _(v):
        v.wait_ge(gDV.h, P6_R2A)
        v.tensor_scalar_mul(bufK[:, 1024:2048], bufK[:, 1024:2048], kcol[:, 7:8])
        gDV.inc(v.drain())
    P6_REP2 = gDV.n

    @sec("sync")
    def _(sync):
        sync.wait_ge(gDV.h, P6_REP2)
        dFSP.wait_all(sync)
        for jc in range(HCH):
            if jc >= 2:
                sync.wait_ge(dH3Q.h, jc - 1)
            dst = scrA[:, 0:1024] if jc % 2 == 0 else scrA[:, 1024:2048]
            dH3L.issue(sync.dma_start(out=dst,
                                      in_=h3buf[jc * 128:(jc + 1) * 128, :]), jc)

    @sec("gpsimd")
    def _(g):
        for jc in range(HCH):
            dH3L.wait(g, jc)
            if jc >= 1:
                g.wait_ge(dH3Q.h, jc)
            src = scrA[:, 0:1024] if jc % 2 == 0 else scrA[:, 1024:2048]
            g.tensor_mul(scrB[:, 0:1024], src, bufK[:, 0:1024])
            g.drain()
            g.tensor_scalar(t16[:, 0:1024], scrB[:, 0:1024], 1.0, 256.5,
                            ALU.mult, ALU.add)
            dH2.inc(g.drain())

    @sec("scalar")
    def _(sc):
        for jc in range(HCH):
            sc.wait_ge(dH2.h, 3 * CCH + jc + 1)
            sc.activation(h3qf[:, jc * 1024:(jc + 1) * 1024], t16[:, 0:1024],
                          AF.Copy, bias=-256.0)
            dH3Q.inc(sc.drain())
    P6B_H3Q = dH3Q.n

    W2P = [(0, 16), (16, 16), (32, 12)]

    @sec("sync")
    def _(sync):
        sync.wait_ge(dH3Q.h, P6B_H3Q)
        for o2 in range(CCH):
            for pi, (k0, kn) in enumerate(W2P):
                k = o2 * 3 + pi
                if k >= 2:
                    sync.wait_ge(dW2V.h, k - 1)
                st = scrC if k % 2 == 0 else small
                dW2L.issue(sync.dma_start(
                    out=st[:, 0:kn * 128],
                    in_=w2T[:, o2 * HID + k0 * 128:
                            o2 * HID + (k0 + kn) * 128]), k)
            if o2 >= 2:
                dYOUT.wait(sync, o2 - 2)
            dst = bufV[:, 0:1024] if o2 % 2 == 0 else bufV[:, 1024:2048]
            dXN2L.issue(sync.dma_start(out=dst,
                                       in_=xnewbuf[o2 * 128:(o2 + 1) * 128, :]),
                        o2)

    @sec("gpsimd")
    def _(g):
        for o2 in range(CCH):
            par = o2 % 2
            for pi, (k0, kn) in enumerate(W2P):
                k = o2 * 3 + pi
                dW2L.wait(g, k)
                if o2 >= 2 and pi == 0:
                    g.wait_ge(dYMM.h, o2 - 1)
                st = scrC if k % 2 == 0 else small
                dst = (AR2[:, 0:2816] if par == 0 else
                       AR2[:, 2816:5632]).bitcast(BF16)[:,
                                                        k0 * 128:(k0 + kn) * 128]
                g.tensor_scalar(dst, st[:, 0:kn * 128], kcol[:, 4:5], None,
                                ALU.is_ge)
                dW2S.inc(g.drain())


    @sec("tensor")
    def _(t):
        for o2 in range(CCH):
            t.wait_ge(dW2V.h, 3 * (o2 + 1))
            if o2 >= 2:
                t.wait_ge(dYDQ.h, o2 - 1)
            par = o2 % 2
            ptile = pbig0[:, 0:1024] if par == 0 else pbig1[:, 0:1024]
            wb = (AR2[:, 0:2816] if par == 0 else AR2[:, 2816:5632]).bitcast(BF16)
            for kc in range(HCH):
                lhs = wb[:, kc * 128:(kc + 1) * 128]
                for h5 in range(2):
                    t.matmul(ptile[:, h5 * 512:(h5 + 1) * 512], lhs,
                             h3qf[:, kc * 1024 + h5 * 512:
                                  kc * 1024 + (h5 + 1) * 512],
                             start=(kc == 0), stop=(kc == HCH - 1))
            dYMM.inc(t.drain())

    def p6b_pass2(v, o2):
        par2 = o2 % 2
        for pi, (k0, kn) in enumerate(W2P):
            k = o2 * 3 + pi
            v.wait_ge(dW2S.h, k + 1)
            st = scrC if k % 2 == 0 else small
            dst = (AR2[:, 0:2816] if par2 == 0 else
                   AR2[:, 2816:5632]).bitcast(BF16)[:, k0 * 128:(k0 + kn) * 128]
            v.scalar_tensor_tensor(dst, st[:, 0:kn * 128], kcol[:, 10:11],
                                   dst, ALU.is_le, ALU.subtract)
            dW2V.inc(v.drain())

    @sec("vector")
    def _(v):
        p6b_pass2(v, 0)
        p6b_pass2(v, 1)
        for o2 in range(CCH):
            if o2 + 2 < CCH:
                p6b_pass2(v, o2 + 2)
            v.wait_ge(dYMM.h, o2 + 1)
            par = o2 % 2
            ptile = pbig0[:, 0:1024] if par == 0 else pbig1[:, 0:1024]
            v.tensor_mul(scrB[:, 1024:2048], ptile, bufK[:, 1024:2048])
            dYDQ.inc(v.drain())
            dXN2L.wait(v, o2)
            if o2 >= 2:
                dYOUT.wait(v, o2 - 2)
            xnt = bufV[:, 0:1024] if o2 % 2 == 0 else bufV[:, 1024:2048]
            dst = bufQ[:, 0:1024] if o2 % 2 == 0 else bufQ[:, 1024:2048]
            v.scalar_tensor_tensor(dst, scrB[:, 1024:2048],
                                   modcol[:, 80 + o2:81 + o2], xnt,
                                   ALU.mult, ALU.add)
            dH2.inc(v.drain())

    @sec("scalar")
    def _(sc):
        for o2 in range(CCH):
            sc.wait_ge(dH2.h, 3 * CCH + HCH + o2 + 1)
            src = bufQ[:, 0:1024] if o2 % 2 == 0 else bufQ[:, 1024:2048]
            dYOUT.issue(sc.dma_start(out=yT[o2 * 128:(o2 + 1) * 128, :],
                                     in_=src), o2)

    @sec("gpsimd")
    def _(g):
        dYOUT.wait_all(g)

    for cm in reversed(ctxs):
        cm.__exit__(None, None, None)
    return nc


def _prep(inputs):
    x = np.ascontiguousarray(inputs["L0_attn_x"], dtype=np.float32)
    qkv_w = np.asarray(inputs["L0_attn_qkv_w"], dtype=np.float32)
    qkv_b = np.asarray(inputs["L0_attn_qkv_b"], dtype=np.float32)
    gate_w = np.asarray(inputs["L0_attn_gate_w"], dtype=np.float32)
    gate_b = np.asarray(inputs["L0_attn_gate_b"], dtype=np.float32)
    proj_w = np.asarray(inputs["L0_attn_proj_w"], dtype=np.float32)
    proj_b = np.asarray(inputs["L0_attn_proj_b"], dtype=np.float32)
    norm_w = np.asarray(inputs["L0_attn_norm_w"], dtype=np.float32)
    w1 = np.asarray(inputs["L0_ffn_w1"], dtype=np.float32)
    w2 = np.asarray(inputs["L0_ffn_w2"], dtype=np.float32)
    w3 = np.asarray(inputs["L0_ffn_w3"], dtype=np.float32)
    cc = np.asarray(inputs["L0_c"], dtype=np.float32)
    wmod = np.asarray(inputs["L0_wmod"], dtype=np.float32)

    def tilestage(block):
        q = block.reshape(-1, 128, block.shape[1])
        return q.transpose(1, 0, 2).reshape(128, -1)

    w1T = np.concatenate(
        [tilestage(w1.T[:, ot * 128:(ot + 1) * 128]) for ot in range(HCH)],
        axis=1)
    w3T = np.concatenate(
        [tilestage(w3.T[:, ot * 128:(ot + 1) * 128]) for ot in range(HCH)],
        axis=1)
    w2T = np.concatenate(
        [tilestage(w2.T[:, o2 * 128:(o2 + 1) * 128]) for o2 in range(CCH)],
        axis=1)
    cT = np.ascontiguousarray(cc.T)
    identv = np.eye(128, dtype=np.float32)

    seg_arr = {"qkv": qkv_w, "gate": gate_w, "proj": proj_w,
               "w1": w1, "w2": w2, "w3": w3}

    cols = []
    bcols = []
    for h in range(NH):
        qs = qkv_w[h * 128:(h + 1) * 128, :]
        ks = qkv_w[2048 + h * 128:2048 + (h + 1) * 128, :]
        vs = qkv_w[4096 + h * 128:4096 + (h + 1) * 128, :]
        gs = gate_w[h * 128:(h + 1) * 128, :]
        cols += [qs.T, ks.T, vs.T, gs.T]
        bcols += [qkv_b[h * 128:(h + 1) * 128],
                  qkv_b[2048 + h * 128:2048 + (h + 1) * 128],
                  qkv_b[4096 + h * 128:4096 + (h + 1) * 128],
                  gate_b[h * 128:(h + 1) * 128]]
    wqkvg_flat = np.concatenate(cols, axis=1)
    wqkvgT = np.concatenate(
        [tilestage(wqkvg_flat[4 * p4 * 128:(4 * p4 + 4) * 128,
                              hh * 512:(hh + 1) * 512])
         for hh in range(NH) for p4 in range(4)], axis=1)
    bqkvg = np.ascontiguousarray(np.stack(bcols, axis=1))
    wprojT = np.concatenate(
        [tilestage(proj_w.T[:, ot * 128:(ot + 1) * 128]) for ot in range(CCH)],
        axis=1)
    normw = np.ascontiguousarray(norm_w.reshape(16, 128).T)
    bpjc = np.ascontiguousarray(proj_b.reshape(16, 128).T)

    maps = []
    for core in range(8):
        b = core // 2
        side = core % 2
        xTb = np.ascontiguousarray(x[b].T[:, side * 1024:(side + 1) * 1024])
        wmod_shc = np.ascontiguousarray(wmod[:, core * 1536:(core + 1) * 1536])
        wshp = []
        for s in SEGS:
            a = seg_arr[s].reshape(-1)
            n8 = a.size // 8
            wshp.append(a[core * n8:(core + 1) * n8].reshape(128, -1))
        wshc = np.ascontiguousarray(np.concatenate(wshp, axis=1))
        boh = np.zeros((128, 4), dtype=np.float32)
        boh[:, b] = 1.0
        mselv = np.full((128, 1), float(side), dtype=np.float32)
        maps.append({
            "xT": xTb, "wqkvgT": wqkvgT, "bqkvg": bqkvg,
            "wprojT": wprojT, "bproj": bpjc, "normw": normw,
            "w1T": w1T, "w3T": w3T, "w2T": w2T,
            "wmod_sh": wmod_shc, "cTin": cT, "wsh": wshc,
            "bonehot": boh, "msel": mselv, "identin": identv,
        })
    return maps


def kernel(**inputs):
    if "nc" not in _CACHE:
        _CACHE["nc"] = build()
    nc = _CACHE["nc"]
    maps = _prep(inputs)
    res = run_bass_kernel_spmd(nc, maps, core_ids=list(range(8)))
    y = np.empty((B, NTOK, C), dtype=np.float32)
    for core in range(8):
        b = core // 2
        side = core % 2
        y[b, side * 1024:(side + 1) * 1024, :] = res.results[core]["yT"].T
    return y


# version 08 (final)
# speedup vs baseline: 1.6141x; 1.6053x over previous
import numpy as np
import ml_dtypes

import concourse.bass as bass
import concourse.mybir as mybir
from concourse.bass_utils import run_bass_kernel_spmd

F32 = mybir.dt.float32
BF16 = mybir.dt.bfloat16
I16 = mybir.dt.int16
AF = mybir.ActivationFunctionType
ALU = mybir.AluOpType
AX = mybir.AxisListType

B = 4
NTOK = 2048
C = 2048
D = 128
NH = 16
NCH = 8
MYTOK = 1024
HID = 5632
CCH = 16
HCH = 44
ATT_SCALE = float(D) ** -0.5
LN_EPS = 1e-5
RMS_EPS = 1e-6
QEPS = 1e-5

SIZES = {"qkv": 6144 * 2048, "gate": 2048 * 2048, "proj": 2048 * 2048,
         "w1": HID * 2048, "w2": 2048 * HID, "w3": HID * 2048}
SEGS = ["qkv", "gate", "proj", "w1", "w2", "w3"]
SEG_F = [SIZES[s] // 8 // 128 for s in SEGS]
WSH_F = sum(SEG_F)

_CACHE = {}


def build():
    nc = bass.Bass("TRN2", target_bir_lowering=False, debug=False)

    xT = nc.dram_tensor("xT", [C, MYTOK], F32, kind="ExternalInput")
    wqkvgT = nc.dram_tensor("wqkvgT", [128, NH * 4 * 2048], F32,
                            kind="ExternalInput")
    bqkvg = nc.dram_tensor("bqkvg", [128, NH * 4], F32, kind="ExternalInput")
    wprojT = nc.dram_tensor("wprojT", [128, CCH * 2048], F32,
                            kind="ExternalInput")
    bproj = nc.dram_tensor("bproj", [128, CCH], F32, kind="ExternalInput")
    normw = nc.dram_tensor("normw", [128, CCH], F32, kind="ExternalInput")
    w1T = nc.dram_tensor("w1T", [128, HCH * 2048], F32, kind="ExternalInput")
    w3T = nc.dram_tensor("w3T", [128, HCH * 2048], F32, kind="ExternalInput")
    w2T = nc.dram_tensor("w2T", [128, CCH * HID], F32, kind="ExternalInput")
    wmod_sh = nc.dram_tensor("wmod_sh", [C, 1536], F32, kind="ExternalInput")
    cTin = nc.dram_tensor("cTin", [C, B], F32, kind="ExternalInput")
    wsh = nc.dram_tensor("wsh", [128, WSH_F], F32, kind="ExternalInput")
    bonehot = nc.dram_tensor("bonehot", [128, 4], F32, kind="ExternalInput")
    msel = nc.dram_tensor("msel", [128, 1], F32, kind="ExternalInput")
    identin = nc.dram_tensor("identin", [128, 128], F32, kind="ExternalInput")

    yT = nc.dram_tensor("yT", [C, MYTOK], F32, kind="ExternalOutput")

    mod_in = nc.dram_tensor("mod_in", [1536, 4], F32)
    mod_out = nc.dram_tensor("mod_out", [12288, 4], F32)
    wsum_in = nc.dram_tensor("wsum_in", [1, 8], F32)
    wsum_out = nc.dram_tensor("wsum_out", [1, 8], F32)
    hbuf = nc.dram_tensor("hbuf", [C, MYTOK], F32)
    qbuf = nc.dram_tensor("qbuf", [128, NH * MYTOK], F32)
    kbuf = nc.dram_tensor("kbuf", [128, NH * MYTOK], F32)
    vbuf = nc.dram_tensor("vbuf", [128, NH * MYTOK], F32)
    kvbuf = nc.dram_tensor("kvbuf", [128, NH * NCH * 128], F32)
    T_in = nc.dram_tensor("T_in", [128, NH * 128], F32)
    T_out = nc.dram_tensor("T_out", [256, NH * 128], F32)
    h3buf = nc.dram_tensor("h3buf", [HID, MYTOK], F32)
    xnewbuf = nc.dram_tensor("xnewbuf", [C, MYTOK], F32)

    ctxs = []

    def E(cm):
        ctxs.append(cm)
        return cm.__enter__()

    AR1 = E(nc.sbuf_tensor("AR1", [128, 24576], F32))
    AR2 = E(nc.sbuf_tensor("AR2", [128, 8192], F32))
    bufQ = E(nc.sbuf_tensor("bufQ", [128, 2048], F32))
    bufK = E(nc.sbuf_tensor("bufK", [128, 2048], F32))
    bufV = E(nc.sbuf_tensor("bufV", [128, 2048], F32))
    scrA = E(nc.sbuf_tensor("scrA", [128, 2048], F32))
    scrB = E(nc.sbuf_tensor("scrB", [128, 2048], F32))
    scrC = E(nc.sbuf_tensor("scrC", [128, 2048], F32))
    ex = E(nc.sbuf_tensor("ex", [128, 2048], F32))
    smaxrep = E(nc.sbuf_tensor("smaxrep", [128, 1024], F32))
    small = E(nc.sbuf_tensor("small", [128, 2048], F32))
    t16 = E(nc.sbuf_tensor("t16", [128, 1024], I16))
    stats = E(nc.sbuf_tensor("stats", [128, 256], F32))
    modcol = E(nc.sbuf_tensor("modcol", [128, 96], F32))
    kcol = E(nc.sbuf_tensor("kcol", [128, 16], F32))
    bqg = E(nc.sbuf_tensor("bqg", [128, NH * 4], F32))
    bpj = E(nc.sbuf_tensor("bpj", [128, CCH], F32))
    nwc = E(nc.sbuf_tensor("nwc", [128, CCH], F32))
    mselc = E(nc.sbuf_tensor("mselc", [128, 1], F32))
    bhot = E(nc.sbuf_tensor("bhot", [128, 4], F32))
    ident = E(nc.sbuf_tensor("ident", [128, 128], F32))
    ones = E(nc.sbuf_tensor("ones", [128, 128], F32))
    dcy = E(nc.sbuf_tensor("dcy", [128, 128], F32))
    ttmp = E(nc.sbuf_tensor("ttmp", [128, 1024], F32))
    Ssl = ex

    pbig0 = E(nc.psum_tensor("pbig0", [128, 2048], F32))
    pbig1 = E(nc.psum_tensor("pbig1", [128, 2048], F32))

    class Ctr:
        def __init__(self, name):
            self.h = E(nc.semaphore(name))
            self.n = 0

        def inc(self, ins, k=1):
            ins.then_inc(self.h, k)
            self.n += k
            return self.n

    class DSem:

        def __init__(self, name, nway=2):
            self.c = [Ctr(name + chr(97 + j)) for j in range(nway)]
            self.nway = nway

        def issue(self, ins, i):
            self.c[i % self.nway].inc(ins, 16)

        def wait(self, eng, i):
            eng.wait_ge(self.c[i % self.nway].h,
                        (i // self.nway + 1) * 16)

        def wait_all(self, eng):
            for c in self.c:
                eng.wait_ge(c.h, c.n)

    class Bank:

        def __init__(self, ctrs):
            self.c = ctrs
            self.b = [c.n for c in ctrs]
            self.nway = len(ctrs)

        def issue(self, ins, i):
            self.c[i % self.nway].inc(ins, 16)

        def wait(self, eng, i):
            eng.wait_ge(self.c[i % self.nway].h,
                        self.b[i % self.nway] + (i // self.nway + 1) * 16)

        def wait_all(self, eng):
            for c in self.c:
                eng.wait_ge(c.h, c.n)

    gIN = Ctr("gIN"); gSP = Ctr("gSP"); gGD = Ctr("gGD"); gCC = Ctr("gCC")
    gPE = Ctr("gPE"); gDV = Ctr("gDV"); gAC = Ctr("gAC"); gGP = Ctr("gGP")
    dWSH = DSem("dWSH"); dABS = Ctr("dABS"); dMOD = DSem("dMOD"); dMDM = Ctr("dMDM")
    dX1 = DSem("dX1"); dSQ1 = Ctr("dSQ1"); dMM1 = Ctr("dMM1")
    dXB = DSem("dXB"); dHB = Ctr("dHB"); dACB = Ctr("dACB"); dSPB = DSem("dSPB")
    dXC = DSem("dXC"); dQC = Ctr("dQC"); dHQ = Ctr("dHQ")
    dWL2 = DSem("dWL2"); dSG2 = Ctr("dSG2"); dMMq = Ctr("dMMq"); dDQq = Ctr("dDQq")
    dCHp = Ctr("dCHp"); dCHd = Ctr("dCHd")
    dDCY = Ctr("dDCY"); dTOL = Ctr("dTOL")
    dSC = Ctr("dSC"); dDN = Ctr("dDN")
    dGRP = Ctr("dGRP"); dGRD = Ctr("dGRD"); dREC = Ctr("dREC"); dRAD = Ctr("dRAD")
    dPJL = DSem("dPJL"); dPJS = Ctr("dPJS"); dPJM = Ctr("dPJM"); dPJD = Ctr("dPJD")
    dPQA = Ctr("dPQA"); dPQB = Ctr("dPQB")
    dX2L = DSem("dX2L"); dXNW = DSem("dXNW"); dLN2 = Ctr("dLN2")
    dXNL = DSem("dXNL"); dH2 = Ctr("dH2"); dH2A = Ctr("dH2A"); dH2S = DSem("dH2S")
    dH2L = DSem("dH2L"); dH2Q = Ctr("dH2Q")
    dFWL = DSem("dFWL", 4); dFWS = Ctr("dFWS"); dFMM = Ctr("dFMM"); dFH3 = Ctr("dFH3")
    dFSP = DSem("dFSP")
    dH3L = DSem("dH3L"); dH3Q = Ctr("dH3Q")
    dW2L = DSem("dW2L", 6); dW2S = Ctr("dW2S"); dYMM = Ctr("dYMM"); dYDQ = Ctr("dYDQ")
    dXN2 = Ctr("dXN2"); dXN2L = DSem("dXN2L"); dYOUT = DSem("dYOUT")
    dVA0 = Ctr("dVA0"); dVA1 = Ctr("dVA1"); dVA2 = Ctr("dVA2"); dVA3 = Ctr("dVA3")
    dFWV = Ctr("dFWV"); dW2V = Ctr("dW2V")

    block = E(nc.Block())

    def sec(name):
        return getattr(block, name)

    hqT = AR1[:, 16384:24576].bitcast(BF16)
    outf = AR1[:, 0:16384].rearrange("p (a b) -> p a b", a=16)
    olh = AR1[:, 0:16384].rearrange("p (a b) -> p b a", a=16)
    xtmp = AR1[:, 16384:24576]
    pjq = AR1[:, 16384:24576].bitcast(BF16).rearrange("p (a b) -> p a b", a=16)
    h2q = AR1[:, 16384:24576].bitcast(BF16).rearrange("p (a b) -> p a b", a=16)
    h3qf = AR1[:, 0:22528].bitcast(BF16)
    wqg = [AR2[:, 0:4096].bitcast(BF16).rearrange("p (a b) -> p a b", a=16),
           AR2[:, 4096:8192].bitcast(BF16).rearrange("p (a b) -> p a b", a=16)]
    wsg = [AR2[:, 0:2816].bitcast(BF16).rearrange("p (a b) -> p a b", a=44),
           AR2[:, 2816:5632].bitcast(BF16).rearrange("p (a b) -> p a b", a=44)]

    colT = small[:, 1664:1792]
    Tslot = [small[:, 0:128], small[:, 128:256]]
    kvt = [small[:, 256:384], small[:, 384:512]]
    ktvt = [small[:, 512:768], small[:, 768:1024]]
    kvsl = xtmp[:, 0:2048]
    bd = [xtmp[:, 2048:2176], xtmp[:, 2176:2304]]
    v2g = [xtmp[:, 2304:2432], xtmp[:, 2432:2560]]
    v2t = [xtmp[:, 2560:2688], xtmp[:, 2688:2816]]
    Mrep_sb = [xtmp[0:16, 4096:5120], xtmp[0:16, 5120:6144]]
    Mrow = [xtmp[32:33, 4096:5120], xtmp[32:33, 5120:6144]]
    rzrow = [xtmp[64:65, 4096:5120], xtmp[64:65, 5120:6144]]
    attnT = xtmp[0:16, 6144:8192]
    Mst = [stats[0:16, 128:192], stats[0:16, 192:256]]
    psc = pbig0[0:16, 0:1024]
    pscT = pbig1[0:16, 0:1024]
    pdn16 = pbig1[0:16, 1024:2048]
    pdn1 = pbig1[0:1, 1024:2048]
    pv = [pbig0[:, 1024:1152], pbig0[:, 1152:1280]]
    plo = [pbig0[:, 1280:1408], pbig0[:, 1408:1536]]
    prec = [pbig0[:, 1536:1664], pbig0[:, 1664:1792]]

    @sec("sync")
    def _(sync):
        gIN.inc(sync.dma_start(out=ident[:, :], in_=identin[:, :]), 16)
        gIN.inc(sync.dma_start(out=bqg[:, :], in_=bqkvg[:, :]), 16)
        gIN.inc(sync.dma_start(out=bpj[:, :], in_=bproj[:, :]), 16)
        gIN.inc(sync.dma_start(out=nwc[:, :], in_=normw[:, :]), 16)
        gIN.inc(sync.dma_start(out=mselc[:, :], in_=msel[:, :]), 16)
        gIN.inc(sync.dma_start(out=bhot[:, :], in_=bonehot[:, :]), 16)
        gIN.inc(sync.dma_start(
            out=small[:, 1792:1856].rearrange("p (a b) -> p a b", a=16),
            in_=cTin[:, :].rearrange("(a p) b -> p a b", p=128)), 16)
    P0_CONST = gIN.n

    @sec("vector")
    def _(v):
        gDV.inc(v.memset(ones[:, :], 1.0))
        v.drain()
    P0_ONES = gDV.n

    @sec("scalar")
    def _(sc):
        sc.wait_ge(gIN.h, P0_CONST)
        sc.activation(small[:, 1792:1856], small[:, 1792:1856], AF.Silu)
        gAC.inc(sc.drain())
    P0_SILU = gAC.n

    seg_off = np.cumsum([0] + SEG_F).tolist()
    pieces = []
    for si, f in enumerate(SEG_F):
        o = 0
        while o < f:
            w = min(2048, f - o)
            pieces.append((seg_off[si] + o, w, si))
            o += w
    NP = len(pieces)

    @sec("sync")
    def _(sync):
        for pi, (fo, w, si) in enumerate(pieces):
            if pi >= 2:
                sync.wait_ge(dABS.h, pi - 1)
            dWSH.issue(sync.dma_start(out=(scrA if pi % 2 == 0 else scrB)[:, 0:w],
                                      in_=wsh[:, fo:fo + w]), pi)

    @sec("scalar")
    def _(sc):
        for pi, (fo, w, si) in enumerate(pieces):
            dWSH.wait(sc, pi)
            sc.activation(scrC[:, 0:w], (scrA if pi % 2 == 0 else scrB)[:, 0:w],
                          AF.Abs, accum_out=stats[:, 144 + pi:145 + pi])
            dABS.inc(sc.drain())

    @sec("tensor")
    def _(t):
        t.wait_ge(dABS.h, NP)
        t.wait_ge(gDV.h, P0_ONES)
        for si in range(6):
            cols = [pi for pi, (_, _, s2) in enumerate(pieces) if s2 == si]
            c0, n2 = min(cols), len(cols)
            t.matmul(pbig0[0:n2, 1024 + si:1025 + si],
                     stats[:, 144 + c0:144 + c0 + n2], ones[:, 0:1],
                     start=True, stop=True)
        gPE.inc(t.drain())
    P0_WMM1 = gPE.n

    @sec("vector")
    def _(v):
        v.wait_ge(gPE.h, P0_WMM1)
        for si in range(6):
            cols = [pi for pi, (_, _, s2) in enumerate(pieces) if s2 == si]
            n2 = len(cols)
            v.tensor_copy(small[0:n2, 1100 + si:1101 + si],
                          pbig0[0:n2, 1024 + si:1025 + si])
            v.drain()
        gDV.inc(v.nop())
    P0_WCP = gDV.n

    @sec("tensor")
    def _(t):
        t.wait_ge(gDV.h, P0_WCP)
        for si in range(6):
            cols = [pi for pi, (_, _, s2) in enumerate(pieces) if s2 == si]
            n2 = len(cols)
            t.matmul(pbig0[0:1, 1040 + si:1041 + si],
                     small[0:n2, 1100 + si:1101 + si], ones[0:n2, 0:1],
                     start=True, stop=True)
        gPE.inc(t.drain())
    P0_WMM2 = gPE.n

    @sec("vector")
    def _(v):
        v.wait_ge(gPE.h, P0_WMM2)
        v.tensor_copy(small[0:1, 1110:1116], pbig0[0:1, 1040:1046])
        v.drain()
        v.memset(small[0:1, 1116:1118], 0.0)
        gDV.inc(v.drain())
    P0_WROW = gDV.n

    @sec("gpsimd")
    def _(g):
        g.wait_ge(gDV.h, P0_WROW)
        gGD.inc(g.dma_start(out=wsum_in[:, 0:8], in_=small[0:1, 1110:1118]), 16)
        g.wait_ge(gGD.h, gGD.n)
        gCC.inc(g.collective_compute(
            "AllReduce", ALU.add, replica_groups=[list(range(8))],
            ins=[wsum_in[:, :].opt()], outs=[wsum_out[:, :].opt()]))
    P0_WAR = gCC.n

    @sec("sync")
    def _(sync):
        sync.wait_ge(gCC.h, P0_WAR)
        gIN.inc(sync.dma_start(out=small[0:1, 1120:1128], in_=wsum_out[:, :]), 16)
    P0_WLD = gIN.n

    @sec("vector")
    def _(v):
        v.wait_ge(gIN.h, P0_WLD)
        for si, nm in enumerate(SEGS):
            v.tensor_scalar(small[0:1, 1130 + si:1131 + si],
                            small[0:1, 1120 + si:1121 + si],
                            1.0 / SIZES[nm], QEPS, ALU.mult, ALU.max)
            v.drain()
        v.tensor_scalar_mul(small[0:1, 1140:1142], small[0:1, 1130:1132],
                            2.0 / 127.0)
        v.drain()
        v.tensor_scalar_mul(small[0:1, 1142:1143], small[0:1, 1132:1133],
                            1.0 / 127.0)
        v.drain()
        v.tensor_scalar_mul(small[0:1, 1143:1146], small[0:1, 1133:1136], 0.5)
        v.drain()
        v.tensor_scalar_mul(small[0:1, 1146:1149], small[0:1, 1133:1136],
                            1.0 / 254.0)
        gDV.inc(v.drain())
    P0_KROW = gDV.n

    @sec("tensor")
    def _(t):
        t.wait_ge(gDV.h, P0_KROW)
        t.matmul(pbig0[:, 1056:1065], ones[0:1, :], small[0:1, 1140:1149],
                 start=True, stop=True)
        gPE.inc(t.drain())
    P0_KMM = gPE.n

    @sec("vector")
    def _(v):
        v.wait_ge(gPE.h, P0_KMM)
        v.tensor_copy(kcol[:, 0:9], pbig0[:, 1056:1065])
        v.drain()
        v.tensor_scalar_mul(kcol[:, 9:12], kcol[:, 3:6], -1.0)
        gDV.inc(v.drain())
    P0_KCOL = gDV.n

    @sec("sync")
    def _(sync):
        sync.wait_ge(dABS.h, NP)
        for ic in range(CCH):
            if ic >= 2:
                sync.wait_ge(dMDM.h, ic - 1)
            dMOD.issue(sync.dma_start(
                out=(scrA if ic % 2 == 0 else scrB)[:, 0:1536],
                in_=wmod_sh[ic * 128:(ic + 1) * 128, :]), ic)

    @sec("tensor")
    def _(t):
        t.wait_ge(gAC.h, P0_SILU)
        cs = small[:, 1792:1856].rearrange("p (a b) -> p a b", a=16)
        for ic in range(CCH):
            dMOD.wait(t, ic)
            if ic >= 1:
                t.wait_ge(dVA0.h, ic)
            wt = (scrA if ic % 2 == 0 else scrB)[:, 0:1536].rearrange(
                "p (j o) -> p j o", j=12)
            for jt in range(12):
                t.matmul(pbig0[:, 1536 + jt * 4:1540 + jt * 4], wt[:, jt, :],
                         cs[:, ic, :], start=True, stop=True)
            dMDM.inc(t.drain())

    @sec("vector")
    def _(v):
        v.memset(stats[:, 64:112], 0.0)
        v.drain()
        for ic in range(CCH):
            v.wait_ge(dMDM.h, ic + 1)
            v.tensor_add(stats[:, 64:112], stats[:, 64:112], pbig0[:, 1536:1584])
            dVA0.inc(v.drain())
        gDV.inc(v.nop())
    P0_MACC = gDV.n

    @sec("gpsimd")
    def _(g):
        g.wait_ge(gDV.h, P0_MACC)
        gGD.inc(g.dma_start(
            out=mod_in[:, :].rearrange("(jt jl) b -> jl jt b", jl=128),
            in_=stats[:, 64:112].rearrange("p (jt b) -> p jt b", jt=12)), 16)
        g.wait_ge(gGD.h, gGD.n)
        gCC.inc(g.collective_compute(
            "AllGather", ALU.bypass, replica_groups=[list(range(8))],
            ins=[mod_in[:, :].opt()], outs=[mod_out[:, :].opt()]))
    P0_MAG = gCC.n

    @sec("sync")
    def _(sync):
        sync.wait_ge(gCC.h, P0_MAG)
        gIN.inc(sync.dma_start(
            out=small[:, 1024:1408].rearrange("p (s b) -> p s b", s=96),
            in_=mod_out[:, :].rearrange("(s p) b -> p s b", p=128)), 16)
    P0_MLD = gIN.n

    @sec("vector")
    def _(v):
        v.wait_ge(gIN.h, P0_MLD)
        mo = small[:, 1024:1408].rearrange("p (s b) -> p s b", s=96)
        v.tensor_scalar_mul(modcol[:, :], mo[:, :, 0], bhot[:, 0:1])
        v.drain()
        for b2 in range(1, 4):
            v.scalar_tensor_tensor(modcol[:, :], mo[:, :, b2], bhot[:, b2:b2 + 1],
                                   modcol[:, :], ALU.mult, ALU.add)
            v.drain()
        for s0 in (32, 80):
            sl = modcol[:, s0:s0 + 16]
            v.tensor_scalar(stats[:, 176:192], sl, 0.0, None, ALU.is_ge)
            v.drain()
            v.tensor_scalar(stats[:, 176:192], stats[:, 176:192], 2.0, -1.0,
                            ALU.mult, ALU.add)
            v.drain()
            v.tensor_scalar_mul(stats[:, 192:208], sl, -1.0)
            v.drain()
            v.tensor_max(stats[:, 192:208], stats[:, 192:208], sl)
            v.drain()
            v.tensor_scalar_max(stats[:, 192:208], stats[:, 192:208], 0.25)
            v.drain()
            v.tensor_mul(sl, stats[:, 176:192], stats[:, 192:208])
            v.drain()
        for s0 in (16, 64):
            v.tensor_scalar_add(modcol[:, s0:s0 + 16], modcol[:, s0:s0 + 16], 1.0)
            v.drain()
        gDV.inc(v.nop())
    P0_MODC = gDV.n

    def emit_col2rep(cols_fn, G, dest_fn, pre=()):
        @sec("tensor")
        def _(t):
            for cc, vv in pre:
                t.wait_ge(cc.h, vv)
            t.wait_ge(gDV.h, gDV.n)
            t.transpose(pbig1[0:G, 1536:1664], cols_fn(), ident[:, :])
            gPE.inc(t.drain())
        m1 = gPE.n

        @sec("vector")
        def _(v):
            v.wait_ge(gPE.h, m1)
            v.tensor_copy(colT[0:G, :], pbig1[0:G, 1536:1664])
            gDV.inc(v.drain())
        m2 = gDV.n

        @sec("sync")
        def _(sync):
            sync.wait_ge(gDV.h, m2)
            gIN.inc(sync.dma_start(out=small[0:1, 0:G * 128],
                                   in_=colT[0:G, :]), 16)
        m3 = gIN.n

        @sec("tensor")
        def _(t):
            t.wait_ge(gIN.h, m3)
            for j in range(0, G * 128, 512):
                w = min(512, G * 128 - j)
                t.matmul(pbig1[:, j:j + w], ones[0:1, :], small[0:1, j:j + w],
                         start=True, stop=True)
            gPE.inc(t.drain())
        m4 = gPE.n

        @sec("vector")
        def _(v):
            v.wait_ge(gPE.h, m4)
            v.tensor_copy(dest_fn(), pbig1[:, 0:G * 128])
            gDV.inc(v.drain())
        return gDV.n

    @sec("sync")
    def _(sync):
        sync.wait_ge(dMDM.h, CCH)
        for ic in range(CCH):
            if ic >= 2:
                sync.wait_ge(dMM1.h, ic - 1)
            dX1.issue(sync.dma_start(
                out=(scrA if ic % 2 == 0 else scrB)[:, 0:1024],
                in_=xT[ic * 128:(ic + 1) * 128, :]), ic)

    @sec("scalar")
    def _(sc):
        for ic in range(CCH):
            dX1.wait(sc, ic)
            if ic >= 1:
                sc.wait_ge(dMM1.h, ic)
            sc.activation(scrC[:, 0:1024],
                          (scrA if ic % 2 == 0 else scrB)[:, 0:1024], AF.Square)
            dSQ1.inc(sc.drain())

    @sec("tensor")
    def _(t):
        t.wait_ge(dVA0.h, CCH)
        for ic in range(CCH):
            t.wait_ge(dSQ1.h, ic + 1)
            if ic >= 1:
                t.wait_ge(dVA1.h, ic)
            xin_t = (scrA if ic % 2 == 0 else scrB)
            for tg in range(8):
                t.matmul(pbig0[:, 1536 + tg:1537 + tg],
                         xin_t[:, tg * 128:(tg + 1) * 128], ones[:, 0:1],
                         start=True, stop=True)
                t.matmul(pbig0[:, 1544 + tg:1545 + tg],
                         scrC[:, tg * 128:(tg + 1) * 128], ones[:, 0:1],
                         start=True, stop=True)
            dMM1.inc(t.drain())

    @sec("vector")
    def _(v):
        v.memset(stats[:, 0:16], 0.0)
        v.drain()
        for ic in range(CCH):
            v.wait_ge(dMM1.h, ic + 1)
            v.tensor_add(stats[:, 0:16], stats[:, 0:16], pbig0[:, 1536:1552])
            dVA1.inc(v.drain())
        v.tensor_scalar_mul(stats[:, 32:40], stats[:, 0:8], 1.0 / C)
        v.drain()
        v.tensor_scalar_mul(stats[:, 40:48], stats[:, 8:16], 1.0 / C)
        v.drain()
        v.tensor_mul(stats[:, 208:216], stats[:, 32:40], stats[:, 32:40])
        v.drain()
        v.tensor_sub(stats[:, 40:48], stats[:, 40:48], stats[:, 208:216])
        v.drain()
        v.tensor_scalar_add(stats[:, 40:48], stats[:, 40:48], LN_EPS)
        gDV.inc(v.drain())
    P1_VAR = gDV.n

    @sec("scalar")
    def _(sc):
        sc.wait_ge(gDV.h, P1_VAR)
        sc.activation(stats[:, 40:48], stats[:, 40:48], AF.Sqrt)
        gAC.inc(sc.drain())
    P1_SQRT = gAC.n

    @sec("vector")
    def _(v):
        v.wait_ge(gAC.h, P1_SQRT)
        v.reciprocal(stats[:, 40:48], stats[:, 40:48])
        gDV.inc(v.drain())

    emit_col2rep(lambda: stats[:, 32:40], 8, lambda: bufK[:, 0:1024])
    P1_REPS = emit_col2rep(lambda: stats[:, 40:48], 8, lambda: bufV[:, 0:1024])

    @sec("vector")
    def _(v):
        v.wait_ge(gDV.h, P1_REPS)
        v.memset(bufQ[:, 0:1024], -1e30)
        v.memset(ex[:, 0:1024], 1e30)
        gDV.inc(v.drain())
    P1B_ACC0 = gDV.n

    @sec("sync")
    def _(sync):
        for ic in range(CCH):
            if ic >= 2:
                sync.wait_ge(dHB.h, 3 * (ic - 1))
            dXB.issue(sync.dma_start(
                out=(scrA if ic % 2 == 0 else scrB)[:, 0:1024],
                in_=xT[ic * 128:(ic + 1) * 128, :]), ic)

    @sec("vector")
    def _(v):
        v.wait_ge(gDV.h, P1B_ACC0)
        for ic in range(CCH):
            dXB.wait(v, ic)
            if ic >= 1:
                dSPB.wait(v, ic - 1)
            xin_t = (scrA if ic % 2 == 0 else scrB)
            v.tensor_sub(scrC[:, 0:1024], xin_t[:, 0:1024], bufK[:, 0:1024])
            v.drain()
            v.tensor_mul(scrC[:, 0:1024], scrC[:, 0:1024], bufV[:, 0:1024])
            dHB.inc(v.drain())
            v.wait_ge(dHB.h, 3 * ic + 3)
            v.tensor_max(bufQ[:, 0:1024], bufQ[:, 0:1024], scrC[:, 0:1024])
            v.drain()
            v.tensor_tensor(ex[:, 0:1024], ex[:, 0:1024], scrC[:, 0:1024],
                            ALU.min)
            dACB.inc(v.drain())

    @sec("scalar")
    def _(sc):
        for ic in range(CCH):
            sc.wait_ge(dHB.h, 3 * ic + 1)
            sc.activation(scrC[:, 0:1024], scrC[:, 0:1024], AF.Identity,
                          bias=modcol[:, ic:ic + 1],
                          scale=modcol[:, 16 + ic:17 + ic])
            dHB.inc(sc.drain(), 2)
            sc.wait_ge(dACB.h, ic + 1)
            dSPB.issue(sc.dma_start(out=hbuf[ic * 128:(ic + 1) * 128, :],
                                    in_=scrC[:, 0:1024]), ic)

    @sec("tensor")
    def _(t):
        t.wait_ge(dACB.h, CCH)
        for tg in range(8):
            t.transpose(pbig0[:, tg * 128:(tg + 1) * 128],
                        bufQ[:, tg * 128:(tg + 1) * 128], ident[:, :])
            t.transpose(pbig1[:, tg * 128:(tg + 1) * 128],
                        ex[:, tg * 128:(tg + 1) * 128], ident[:, :])
        gPE.inc(t.drain())
    P1B_TR = gPE.n

    @sec("vector")
    def _(v):
        v.wait_ge(gPE.h, P1B_TR)
        v.reduce_max(stats[:, 64:72],
                     pbig0[:, 0:1024].rearrange("p (a b) -> p a b", a=8),
                     axis=AX.X)
        v.drain()
        v.reduce_max(stats[:, 80:88],
                     pbig1[:, 0:1024].rearrange("p (a b) -> p a b", a=8),
                     axis=AX.X, negate=True)
        v.drain()
        v.tensor_max(stats[:, 64:72], stats[:, 64:72], stats[:, 80:88])
        v.drain()
        v.tensor_scalar_max(stats[:, 64:72], stats[:, 64:72], QEPS)
        v.drain()
        v.reciprocal(stats[:, 96:104], stats[:, 64:72])
        v.drain()
        v.tensor_scalar_mul(stats[:, 96:104], stats[:, 96:104], 127.0)
        gDV.inc(v.drain())

    emit_col2rep(lambda: stats[:, 64:72], 8, lambda: smaxrep[:, 0:1024])
    P1B_REPS = emit_col2rep(lambda: stats[:, 96:104], 8, lambda: bufK[:, 0:1024])

    @sec("sync")
    def _(sync):
        sync.wait_ge(gDV.h, P1B_REPS)
        dSPB.wait_all(sync)
        for ic in range(CCH):
            if ic >= 2:
                sync.wait_ge(dQC.h, ic - 1)
            dXC.issue(sync.dma_start(
                out=(scrA if ic % 2 == 0 else scrB)[:, 0:1024],
                in_=hbuf[ic * 128:(ic + 1) * 128, :]), ic)

    @sec("vector")
    def _(v):
        for ic in range(CCH):
            dXC.wait(v, ic)
            if ic >= 1:
                v.wait_ge(dHQ.h, ic)
            xin_t = (scrA if ic % 2 == 0 else scrB)
            v.tensor_mul(scrC[:, 0:1024], xin_t[:, 0:1024], bufK[:, 0:1024])
            v.drain()
            v.tensor_scalar(t16[:, :], scrC[:, 0:1024], 1.0, 256.5,
                            ALU.mult, ALU.add)
            dQC.inc(v.drain())

    @sec("scalar")
    def _(sc):
        for ic in range(CCH):
            sc.wait_ge(dQC.h, ic + 1)
            sc.activation(hqT[:, ic * 1024:(ic + 1) * 1024], t16[:, :],
                          AF.Copy, bias=-256.0)
            dHQ.inc(sc.drain())
    P1C_DONE = dHQ.n

    @sec("sync")
    def _(sync):
        sync.wait_ge(dQC.h, CCH)
        for hh in range(NH):
            for p4 in range(4):
                k = hh * 4 + p4
                if k >= 2:
                    sync.wait_ge(dSG2.h, k - 1)
                st = scrA if k % 2 == 0 else scrB
                dWL2.issue(sync.dma_start(
                    out=st[:, :],
                    in_=wqkvgT[:, k * 2048:(k + 1) * 2048]), k)

    def gp_sign(g, hh):
        for p4 in range(4):
            k = hh * 4 + p4
            dWL2.wait(g, k)
            if hh >= 2 and p4 == 0:
                g.wait_ge(dMMq.h, 4 * (hh - 1))
            st = scrA if k % 2 == 0 else scrB
            dst = (AR2[:, 0:4096] if hh % 2 == 0 else
                   AR2[:, 4096:8192]).bitcast(BF16)
            g.tensor_scalar(dst[:, p4 * 2048:(p4 + 1) * 2048],
                            st[:, :], 0.0, 0.5, ALU.is_gt, ALU.subtract)
            dSG2.inc(g.drain())

    @sec("gpsimd")
    def _(g):
        gp_sign(g, 0)
        gp_sign(g, 1)

    hq3 = hqT.rearrange("p (a b) -> p a b", a=16)
    G0 = gAC.n
    GP0 = gGP.n
    dSPQ = Bank([dWSH.c[0], dWSH.c[1], dABS])
    dKVS = Bank([dMOD.c[0], dMOD.c[1]])
    dTIN = Bank([dMDM, dVA0])

    for hh in range(NH):
        par = hh % 2
        wv = wqg[par]
        sgn_ready = (hh * 4 + 4)

        @sec("tensor")
        def _(t, hh=hh, wv=wv, sgn_ready=sgn_ready):
            t.wait_ge(dSG2.h, sgn_ready)
            if hh == 0:
                t.wait_ge(dHQ.h, P1C_DONE)
                t.wait_ge(gDV.h, P0_KCOL)
                t.wait_ge(gDV.h, P0_MODC)
            else:
                t.wait_ge(dCHd.h, 24 * hh)
                t.wait_ge(dDQq.h, 4 * hh)
            for ti in range(4):
                ptile = pbig0 if ti < 2 else pbig1
                off = 1024 * (ti % 2)
                for kc in range(CCH):
                    lhs = wv[:, kc, ti * 128:(ti + 1) * 128]
                    for h5 in range(2):
                        t.matmul(ptile[:, off + h5 * 512:off + (h5 + 1) * 512],
                                 lhs, hq3[:, kc, h5 * 512:(h5 + 1) * 512],
                                 start=(kc == 0), stop=(kc == CCH - 1))
                dMMq.inc(t.drain())

        @sec("vector")
        def _(v, hh=hh):
            base = 4 * hh
            for ti, dst in ((0, bufQ), (1, bufK), (2, bufV)):
                v.wait_ge(dMMq.h, base + ti + 1)
                if hh > 0:
                    dSPQ.wait(v, 3 * (hh - 1) + ti)
                    if ti >= 1:
                        v.wait_ge(dCHp.h, 16 * hh)
                ptile = pbig0 if ti < 2 else pbig1
                off = 1024 * (ti % 2)
                v.tensor_mul(dst[:, 0:1024], ptile[:, off:off + 1024],
                             smaxrep[:, 0:1024])
                v.drain()
                v.tensor_scalar(dst[:, 0:1024], dst[:, 0:1024], kcol[:, 0:1],
                                bqg[:, hh * 4 + ti:hh * 4 + ti + 1],
                                ALU.mult, ALU.add)
                dDQq.inc(v.drain())
            v.wait_ge(dMMq.h, base + 4)
            if hh > 0:
                v.wait_ge(gAC.h, G0 + hh)
            v.tensor_mul(ex[:, 0:1024], pbig1[:, 1024:2048], smaxrep[:, 0:1024])
            dDQq.inc(v.drain())

        @sec("scalar")
        def _(sc, hh=hh):
            for ti, src in ((0, bufQ), (1, bufK), (2, bufV)):
                sc.wait_ge(dDQq.h, 4 * hh + ti + 1)
                dst = (qbuf, kbuf, vbuf)[ti]
                dSPQ.issue(sc.dma_start(
                    out=dst[:, hh * 1024:(hh + 1) * 1024],
                    in_=src[:, 0:1024]), 3 * hh + ti)
            sc.wait_ge(dDQq.h, 4 * hh + 4)
            sc.activation(ex[:, 0:1024], ex[:, 0:1024], AF.Sigmoid,
                          bias=bqg[:, hh * 4 + 3:hh * 4 + 4],
                          scale=kcol[:, 1:2])
            gAC.inc(sc.drain())

        @sec("vector")
        def _(v, hh=hh):
            v.wait_ge(gAC.h, G0 + hh + 1)
            v.reduce_sum(dcy[:, hh * 8:(hh + 1) * 8],
                         ex[:, 0:1024].rearrange("p (a b) -> p a b", a=8),
                         axis=AX.X)
            v.drain()
            v.tensor_scalar_mul(dcy[:, hh * 8:(hh + 1) * 8],
                                dcy[:, hh * 8:(hh + 1) * 8], 1.0 / 128.0)
            dDCY.inc(v.drain())

        @sec("tensor")
        def _(t, hh=hh):
            t.wait_ge(dDQq.h, 4 * hh + 3)
            for ci in range(NCH):
                k8 = 8 * hh + ci
                cp = ci % 2
                if k8 >= 2:
                    t.wait_ge(dCHd.h, 3 * (k8 - 2) + 1)
                t.transpose(pbig0[:, 256 * cp:256 * cp + 128],
                            bufK[:, ci * 128:(ci + 1) * 128], ident[:, :])
                t.transpose(pbig0[:, 256 * cp + 128:256 * cp + 256],
                            bufV[:, ci * 128:(ci + 1) * 128], ident[:, :])
                dCHp.inc(t.drain())
                t.wait_ge(dCHd.h, 3 * k8 + 1)
                if k8 >= 2:
                    t.wait_ge(dCHd.h, 3 * (k8 - 2) + 3)
                t.matmul(pbig0[:, 1024 + 128 * cp:1152 + 128 * cp],
                         ktvt[cp][:, 0:128], ktvt[cp][:, 128:256],
                         start=True, stop=True)
                dCHp.inc(t.drain())

        @sec("vector")
        def _(v, hh=hh):
            for ci in range(NCH):
                k8 = 8 * hh + ci
                cp = ci % 2
                v.wait_ge(dCHp.h, 2 * k8 + 1)
                if k8 >= 2:
                    v.wait_ge(dCHp.h, 2 * (k8 - 2) + 2)
                v.tensor_copy(ktvt[cp][:, :], pbig0[:, 256 * cp:256 * cp + 256])
                dCHd.inc(v.drain())
                v.wait_ge(dCHp.h, 2 * k8 + 2)
                if k8 >= 2:
                    v.wait_ge(gGP.h, GP0 + k8 - 1)
                    dKVS.wait(v, k8 - 2)
                v.tensor_copy(kvt[cp][:, :],
                              pbig0[:, 1024 + 128 * cp:1152 + 128 * cp])
                dCHd.inc(v.drain(), 2)

        @sec("gpsimd")
        def _(g, hh=hh):
            g.wait_ge(dDCY.h, hh + 1)
            if hh >= 2:
                dTIN.wait(g, hh - 2)
            slot = Tslot[hh % 2]
            g.memset(slot, 0.0)
            g.drain()
            for ci in range(NCH):
                k8 = 8 * hh + ci
                g.wait_ge(dCHd.h, 3 * k8 + 3)
                g.tensor_scalar_mul(slot, slot, dcy[:, k8:k8 + 1])
                g.drain()
                g.tensor_add(slot, slot, kvt[ci % 2][:, :])
                gGP.inc(g.drain())
            if hh + 2 < NH:
                gp_sign(g, hh + 2)

        @sec("scalar")
        def _(sc, hh=hh):
            for ci in range(NCH):
                k8 = 8 * hh + ci
                sc.wait_ge(dCHd.h, 3 * k8 + 3)
                dKVS.issue(sc.dma_start(
                    out=kvbuf[:, k8 * 128:(k8 + 1) * 128],
                    in_=kvt[ci % 2][:, :]), k8)
            sc.wait_ge(gGP.h, GP0 + 8 * (hh + 1))
            dTIN.issue(sc.dma_start(out=T_in[:, hh * 128:(hh + 1) * 128],
                                    in_=Tslot[hh % 2]), hh)

    @sec("gpsimd")
    def _(g):
        dTIN.wait_all(g)
        gCC.inc(g.collective_compute(
            "AllGather", ALU.bypass,
            replica_groups=[[0, 1], [2, 3], [4, 5], [6, 7]],
            ins=[T_in[:, :].opt()], outs=[T_out[:, :].opt()]))
    P2_AG = gCC.n

    @sec("sync")
    def _(sync):
        sync.wait_ge(gCC.h, P2_AG)
        dTOL.inc(sync.dma_start(out=Ssl[:, :], in_=T_out[0:128, :]), 16)
    P2_TOL = dTOL.n

    @sec("gpsimd")
    def _(g):
        g.wait_ge(dTOL.h, P2_TOL)
        g.tensor_scalar_mul(Ssl[:, :], Ssl[:, :], mselc[:, 0:1])
        gGP.inc(g.drain())
    P2_SINIT = gGP.n

    def chbufs(ci):
        return (bufQ, bufK, bufV) if ci % 2 == 0 else (scrA, scrB, scrC)

    ath = [attnT[:, 0:1024], attnT[:, 1024:2048]]

    dQCL = Bank([dXB.c[0], dXB.c[1], dSPB.c[0], dSPB.c[1],
                 dXC.c[0], dXC.c[1]])
    dKVL = Bank([dX1.c[0], dX1.c[1]])
    dMRW = Bank([dSQ1, dMM1])
    bdP = [dWL2.c[0], dWL2.c[1]]
    bdB = [c.n for c in bdP]
    CHB = dCHp.n
    DHB = dCHd.n
    mtsl = [xtmp[0:64, 2816:2832], xtmp[0:64, 2832:2848]]
    pmt = pbig0[0:64, 1984:2000]

    @sec("vector")
    def _(v):
        v.wait_ge(dMMq.h, 64)
        v.memset(bd[0][:, :], 0.0)
        v.drain()
        v.memset(bd[1][:, :], 0.0)
        gDV.inc(v.drain())
    P2_BD0 = gDV.n

    @sec("sync")
    def _(sync):
        sync.wait_ge(gDV.h, P2_BD0)
        for ci in range(NCH):
            if ci == 0:
                dSPQ.wait_all(sync)
            if ci >= 2:
                sync.wait_ge(dSC.h, 10 * (ci - 1))
                sync.wait_ge(dGRD.h, 48 * (ci - 1))
                sync.wait_ge(dREC.h, 16 * (ci - 1))
            qb, kb, vb = chbufs(ci)
            for j3, (srcb, dstb) in enumerate(((qbuf, qb), (kbuf, kb),
                                               (vbuf, vb))):
                dQCL.issue(sync.dma_start(
                    out=dstb[:, 0:2048].rearrange("p (a b) -> p a b", a=16),
                    in_=srcb[:, :].rearrange("p (a b) -> p a b", a=16)[
                        :, :, ci * 128:(ci + 1) * 128]), 3 * ci + j3)
            for half in range(2):
                H = 2 * ci + half
                sync.wait_ge(dCHd.h, DHB + H + 1)
                if H >= 2:
                    sync.wait_ge(dSC.h, 5 * (H - 2) + 3)
                dMRW.issue(sync.dma_start(
                    out=Mrow[half][:, :], in_=mtsl[half][:, :]), H)
            for half in range(2):
                H = 2 * ci + half
                sync.wait_ge(dDN.h, 5 * H + 5)
                for j in range(8 * half, 8 * half + 8):
                    Gn = 16 * ci + j
                    if Gn >= 2:
                        sync.wait_ge(dGRP.h, 2 * (Gn - 2) + 2)
                    for l4 in range(8):
                        l = 8 * j + l4
                        bdP[j % 2].inc(sync.dma_start(
                            out=bd[j % 2][16 * l4:16 * l4 + 16,
                                          16 * l4:16 * l4 + 16],
                            in_=attnT[:, l * 16:(l + 1) * 16]), 16)

    @sec("scalar")
    def _(sc):
        for ci in range(NCH):
            for half in range(2):
                H = 2 * ci + half
                sc.wait_ge(dDN.h, 5 * H + 3)
                sc.activation(ath[half], ath[half], AF.Exp)
                gAC.inc(sc.drain())
            if ci == NCH - 1:
                continue
            if ci == 0:
                dKVS.wait_all(sc)
                sc.wait_ge(dMMq.h, 64)
            else:
                sc.wait_ge(gGP.h, P2_SINIT + 16 * ci)
            dKVL.issue(sc.dma_start(
                out=kvsl[:, :].rearrange("p (a b) -> p a b", a=16),
                in_=kvbuf[:, :].rearrange("p (a b) -> p a b", a=16)[
                    :, :, ci * 128:(ci + 1) * 128]), ci)

    ACB = gAC.n - 16

    @sec("tensor")
    def _(t):
        t.wait_ge(dCHd.h, 384)
        t.wait_ge(dDQq.h, 64)
        for ci in range(NCH):
            qb, kb, vb = chbufs(ci)
            qhl = qb[:, 0:2048].rearrange("p (a b) -> p a b", a=16)
            khl = kb[:, 0:2048].rearrange("p (a b) -> p a b", a=16)
            dQCL.wait(t, 3 * ci + 0)
            dQCL.wait(t, 3 * ci + 1)
            for half in range(2):
                H = 2 * ci + half
                if H >= 1:
                    t.wait_ge(dDN.h, 5 * (H - 1) + 1)
                for l in range(64 * half, 64 * half + 64):
                    lo = (l - 64 * half) * 16
                    t.matmul(psc[:, lo:lo + 16], qhl[:, :, l], khl[:, :, l],
                             start=True, stop=True)
                dSC.inc(t.drain())
                if H >= 1:
                    t.wait_ge(dDN.h, 5 * (H - 1) + 3)
                for l in range(64 * half, 64 * half + 64):
                    lo = (l - 64 * half) * 16
                    t.matmul(pscT[:, lo:lo + 16], khl[:, :, l], qhl[:, :, l],
                             start=True, stop=True)
                dSC.inc(t.drain())
                t.wait_ge(dDN.h, 5 * H + 1)
                if H >= 2:
                    t.wait_ge(dCHd.h, DHB + H - 1)
                t.transpose(pmt[:, :], Mst[half][:, 0:64], ident[0:16, 0:16])
                dCHp.inc(t.drain())
                dMRW.wait(t, H)
                if H >= 1:
                    t.wait_ge(dDN.h, 5 * (H - 1) + 5)
                for s5 in range(2):
                    t.matmul(pdn16[:, s5 * 512:(s5 + 1) * 512],
                             ones[32:33, 0:16],
                             Mrow[half][:, s5 * 512:(s5 + 1) * 512],
                             start=True, stop=True)
                dSC.inc(t.drain())
                t.wait_ge(gAC.h, ACB + H + 1)
                t.wait_ge(dDN.h, 5 * H + 2)
                for s5 in range(2):
                    t.matmul(pdn1[:, s5 * 512:(s5 + 1) * 512], ones[0:16, 0:1],
                             ath[half][:, s5 * 512:(s5 + 1) * 512],
                             start=True, stop=True)
                dSC.inc(t.drain())
                t.wait_ge(dDN.h, 5 * H + 4)
                for s5 in range(2):
                    t.matmul(pdn16[:, s5 * 512:(s5 + 1) * 512],
                             ones[64:65, 0:16],
                             rzrow[half][:, s5 * 512:(s5 + 1) * 512],
                             start=True, stop=True)
                dSC.inc(t.drain())
            for j in range(16):
                Gn = 16 * ci + j
                jp = j % 2
                t.wait_ge(dGRD.h, 3 * Gn + 1)
                if Gn >= 2:
                    t.wait_ge(dGRD.h, 3 * (Gn - 2) + 2)
                t.transpose(pv[jp][:, :], v2g[jp][:, :], ident[:, :])
                dGRP.inc(t.drain())
                t.wait_ge(dGRD.h, 3 * Gn + 2)
                t.wait_ge(bdP[Gn % 2].h,
                          bdB[Gn % 2] + 128 * (Gn // 2 + 1))
                if Gn >= 2:
                    t.wait_ge(dGRD.h, 3 * (Gn - 2) + 3)
                t.matmul(plo[jp][:, :], v2t[jp][:, :], bd[jp][:, :],
                         start=True, stop=True)
                dGRP.inc(t.drain())
            t.wait_ge(gGP.h, P2_SINIT + 16 * ci)
            for h in range(16):
                hp = h % 2
                if 16 * ci + h >= 2:
                    t.wait_ge(dRAD.h, 16 * ci + h - 1)
                t.matmul(prec[hp][:, :], Ssl[:, h * 128:(h + 1) * 128],
                         qb[:, h * 128:(h + 1) * 128], start=True, stop=True)
                dREC.inc(t.drain())

    @sec("vector")
    def _(v):
        for ci in range(NCH):
            qb, kb, vb = chbufs(ci)
            vlh = vb[:, 0:2048].rearrange("p (a b) -> p b a", a=16)
            for half in range(2):
                H = 2 * ci + half
                v.wait_ge(dSC.h, 5 * H + 1)
                if H >= 2:
                    v.wait_ge(dCHp.h, CHB + H - 1)
                v.reduce_max(Mst[half][:, 0:64],
                             psc[:, :].rearrange("p (a b) -> p a b", a=64),
                             axis=AX.X)
                dDN.inc(v.drain())
                v.wait_ge(dCHp.h, CHB + H + 1)
                if H >= 2:
                    dMRW.wait(v, H - 2)
                v.tensor_copy(mtsl[half][:, :], pmt[:, :])
                dCHd.inc(v.drain())
                v.wait_ge(dSC.h, 5 * H + 3)
                v.tensor_copy(Mrep_sb[half][:, :], pdn16[:, :])
                dDN.inc(v.drain())
                if H >= 2:
                    v.wait_ge(bdP[0].h, bdB[0] + 128 * 4 * (H - 1))
                    v.wait_ge(bdP[1].h, bdB[1] + 128 * 4 * (H - 1))
                v.scalar_tensor_tensor(ath[half], pscT[:, :], ATT_SCALE,
                                       Mrep_sb[half][:, :], ALU.mult,
                                       ALU.subtract)
                dDN.inc(v.drain())
                v.wait_ge(dSC.h, 5 * H + 4)
                if H >= 2:
                    v.wait_ge(dSC.h, 5 * (H - 2) + 5)
                v.reciprocal(rzrow[half][:, :], pdn1[:, :])
                dDN.inc(v.drain())
                v.wait_ge(dSC.h, 5 * H + 5)
                v.tensor_mul(ath[half], ath[half], pdn16[:, :])
                dDN.inc(v.drain())
            dQCL.wait(v, 3 * ci + 2)
            for j in range(16):
                Gn = 16 * ci + j
                jp = j % 2
                if Gn >= 2:
                    v.wait_ge(dGRP.h, 2 * (Gn - 2) + 1)
                v.tensor_copy(
                    v2g[jp][:, :].rearrange("p (a b) -> p a b", a=8),
                    vlh[:, 8 * j:8 * j + 8, :])
                dGRD.inc(v.drain())
                v.wait_ge(dGRP.h, 2 * Gn + 1)
                v.tensor_copy(v2t[jp][:, :], pv[jp][:, :])
                dGRD.inc(v.drain())
                v.wait_ge(dGRP.h, 2 * Gn + 2)
                v.tensor_copy(
                    olh[:, ci * 128 + 8 * j:ci * 128 + 8 * j + 8, :],
                    plo[jp][:, :].rearrange("p (a b) -> p a b", a=8))
                dGRD.inc(v.drain())
            for h in range(16):
                v.wait_ge(dREC.h, 16 * ci + h + 1)
                v.tensor_add(outf[:, h, ci * 128:(ci + 1) * 128],
                             outf[:, h, ci * 128:(ci + 1) * 128],
                             prec[h % 2][:, :])
                dRAD.inc(v.drain())

    @sec("gpsimd")
    def _(g):
        for ci in range(NCH - 1):
            dKVL.wait(g, ci)
            g.wait_ge(dREC.h, 16 * (ci + 1))
            for h in range(16):
                g.tensor_scalar_mul(Ssl[:, h * 128:(h + 1) * 128],
                                    Ssl[:, h * 128:(h + 1) * 128],
                                    dcy[:, h * 8 + ci:h * 8 + ci + 1])
                g.drain()
                g.tensor_add(Ssl[:, h * 128:(h + 1) * 128],
                             Ssl[:, h * 128:(h + 1) * 128],
                             kvsl[:, h * 128:(h + 1) * 128])
                g.drain()
            gGP.inc(g.nop(), 16)
    P2_OUTF = 128

    def emit_projload(sync, ot):
        dPJL.issue(sync.dma_start(
            out=(scrA if ot % 2 == 0 else scrB)[:, :],
            in_=wprojT[:, ot * 2048:(ot + 1) * 2048]), ot)

    @sec("sync")
    def _(sync):
        sync.wait_ge(dREC.h, 128)
        sync.wait_ge(dGRD.h, 384)
        emit_projload(sync, 0)
        emit_projload(sync, 1)

    def pjsign(sc, ot):
        dPJL.wait(sc, ot)
        if ot < 2:
            sc.wait_ge(dMMq.h, 64)
        if ot >= 2:
            sc.wait_ge(dPJM.h, ot - 1)
        st = scrA if ot % 2 == 0 else scrB
        par = ot % 2
        dst = (AR2[:, 0:2816] if par == 0 else AR2[:, 2816:5632]).bitcast(BF16)
        sc.activation(dst[:, 0:2048], st[:, :], AF.Sign)
        dPJS.inc(sc.drain())

    @sec("scalar")
    def _(sc):
        sc.wait_ge(dRAD.h, P2_OUTF)
        for fc in range(CCH):
            if fc >= 1:
                sc.wait_ge(dPJD.h, fc)
            sc.activation(scrC[:, 0:1024], outf[:, fc, :], AF.Square)
            dLN2.inc(sc.drain())

    @sec("tensor")
    def _(t):
        for fc in range(CCH):
            t.wait_ge(dLN2.h, fc + 1)
            if fc >= 1:
                t.wait_ge(dVA2.h, fc)
            for tg in range(8):
                t.matmul(pbig1[:, 1536 + tg:1537 + tg],
                         scrC[:, tg * 128:(tg + 1) * 128], ones[:, 0:1],
                         start=True, stop=True)
            dPJD.inc(t.drain())
    RMS_MM = dPJD.n

    @sec("vector")
    def _(v):
        v.memset(stats[:, 0:8], 0.0)
        v.drain()
        for fc in range(CCH):
            v.wait_ge(dPJD.h, fc + 1)
            v.tensor_add(stats[:, 0:8], stats[:, 0:8], pbig1[:, 1536:1544])
            dVA2.inc(v.drain())
        v.tensor_scalar(stats[:, 8:16], stats[:, 0:8], 1.0 / C, RMS_EPS,
                        ALU.mult, ALU.add)
        gDV.inc(v.drain())
    P4_MS = gDV.n

    @sec("scalar")
    def _(sc):
        sc.wait_ge(gDV.h, P4_MS)
        sc.activation(stats[:, 8:16], stats[:, 8:16], AF.Sqrt)
        gAC.inc(sc.drain())
    P4_SQ = gAC.n

    @sec("vector")
    def _(v):
        v.wait_ge(gAC.h, P4_SQ)
        v.reciprocal(stats[:, 8:16], stats[:, 8:16])
        gDV.inc(v.drain())

    emit_col2rep(lambda: stats[:, 8:16], 8, lambda: bufV[:, 0:1024])

    @sec("vector")
    def _(v):
        v.wait_ge(gDV.h, gDV.n)
        v.memset(bufQ[:, 0:1024], -1e30)
        v.memset(bufQ[:, 1024:2048], 1e30)
        v.drain()
        for fc in range(CCH):
            v.tensor_mul(scrC[:, 0:1024], outf[:, fc, :], bufV[:, 0:1024])
            v.drain()
            v.tensor_scalar_mul(scrC[:, 0:1024], scrC[:, 0:1024],
                                nwc[:, fc:fc + 1])
            v.drain()
            v.tensor_max(bufQ[:, 0:1024], bufQ[:, 0:1024], scrC[:, 0:1024])
            v.drain()
            v.tensor_tensor(bufQ[:, 1024:2048], bufQ[:, 1024:2048],
                            scrC[:, 0:1024], ALU.min)
            v.drain()
        gDV.inc(v.nop())
    P4_ACC = gDV.n

    @sec("tensor")
    def _(t):
        t.wait_ge(gDV.h, P4_ACC)
        for tg in range(8):
            t.transpose(pbig1[:, tg * 128:(tg + 1) * 128],
                        bufQ[:, tg * 128:(tg + 1) * 128], ident[:, :])
            t.transpose(pbig1[:, 1024 + tg * 128:1024 + (tg + 1) * 128],
                        bufQ[:, 1024 + tg * 128:1024 + (tg + 1) * 128],
                        ident[:, :])
        gPE.inc(t.drain())
    P4_TR = gPE.n

    @sec("vector")
    def _(v):
        v.wait_ge(gPE.h, P4_TR)
        v.reduce_max(stats[:, 16:24],
                     pbig1[:, 0:1024].rearrange("p (a b) -> p a b", a=8),
                     axis=AX.X)
        v.drain()
        v.reduce_max(stats[:, 24:32],
                     pbig1[:, 1024:2048].rearrange("p (a b) -> p a b", a=8),
                     axis=AX.X, negate=True)
        v.drain()
        v.tensor_max(stats[:, 16:24], stats[:, 16:24], stats[:, 24:32])
        v.drain()
        v.tensor_scalar_max(stats[:, 16:24], stats[:, 16:24], QEPS)
        v.drain()
        v.reciprocal(stats[:, 24:32], stats[:, 16:24])
        v.drain()
        v.tensor_scalar_mul(stats[:, 24:32], stats[:, 24:32], 127.0)
        gDV.inc(v.drain())

    emit_col2rep(lambda: stats[:, 16:24], 8, lambda: bufK[:, 1024:2048])
    P4_REP2 = emit_col2rep(lambda: stats[:, 24:32], 8, lambda: bufK[:, 0:1024])

    @sec("vector")
    def _(v):
        v.wait_ge(gDV.h, P4_REP2)
        v.tensor_mul(bufV[:, 0:1024], bufV[:, 0:1024], bufK[:, 0:1024])
        gDV.inc(v.drain())
    P4_COMB = gDV.n

    @sec("vector")
    def _(v):
        v.wait_ge(gDV.h, P4_COMB)
        for fc in range(CCH):
            if fc >= 1:
                v.wait_ge(dPQB.h, fc)
            v.tensor_mul(scrC[:, 0:1024], outf[:, fc, :], bufV[:, 0:1024])
            v.drain()
            v.tensor_scalar(t16[:, 0:1024], scrC[:, 0:1024], nwc[:, fc:fc + 1],
                            256.5, ALU.mult, ALU.add)
            dPQA.inc(v.drain())

    @sec("scalar")
    def _(sc):
        for fc in range(CCH):
            sc.wait_ge(dPQA.h, fc + 1)
            sc.activation(pjq[:, fc, :], t16[:, 0:1024], AF.Copy, bias=-256.0)
            dPQB.inc(sc.drain())
        pjsign(sc, 0)
        pjsign(sc, 1)
    P4_PJQ = dPQB.n

    @sec("sync")
    def _(sync):
        sync.wait_ge(dPQB.h, P4_PJQ)
        for ot in range(CCH):
            if ot >= 2:
                dXNW.wait(sync, ot - 2)
            dst = bufK[:, 0:1024] if ot % 2 == 0 else bufV[:, 1024:2048]
            dX2L.issue(sync.dma_start(out=dst,
                                      in_=xT[ot * 128:(ot + 1) * 128, :]), ot)
            if ot + 2 < CCH:
                sync.wait_ge(dPJS.h, ot + 1)
                emit_projload(sync, ot + 2)

    def p4_statmm(t, j):
        t.wait_ge(dH2.h, j + 1)
        if j >= 1:
            t.wait_ge(dVA3.h, j)
        for tg in range(8):
            t.matmul(pbig1[:, 1536 + tg:1537 + tg],
                     scrC[:, tg * 128:(tg + 1) * 128], ones[:, 0:1],
                     start=True, stop=True)
            t.matmul(pbig1[:, 1544 + tg:1545 + tg],
                     small[:, 768 + tg * 128:768 + (tg + 1) * 128],
                     ones[:, 0:1], start=True, stop=True)
        dH2A.inc(t.drain())

    @sec("tensor")
    def _(t):
        t.wait_ge(dPQB.h, P4_PJQ)
        for ot in range(CCH):
            t.wait_ge(dPJS.h, ot + 1)
            if ot >= 2:
                t.wait_ge(dPJD.h, RMS_MM + ot - 1)
            par = ot % 2
            ptile = pbig0[:, 0:1024] if par == 0 else pbig0[:, 1024:2048]
            for kc in range(CCH):
                lhs = (AR2[:, 0:2816] if par == 0 else
                       AR2[:, 2816:5632]).bitcast(BF16)[:, kc * 128:(kc + 1) * 128]
                for h5 in range(2):
                    t.matmul(ptile[:, h5 * 512:(h5 + 1) * 512], lhs,
                             pjq[:, kc, h5 * 512:(h5 + 1) * 512],
                             start=(kc == 0), stop=(kc == CCH - 1))
            dPJM.inc(t.drain())
            if ot >= 2:
                p4_statmm(t, ot - 2)
        p4_statmm(t, CCH - 2)
        p4_statmm(t, CCH - 1)

    def p4_statacc(v, j):
        v.wait_ge(dH2A.h, j + 1)
        v.tensor_add(stats[:, 32:48], stats[:, 32:48], pbig1[:, 1536:1552])
        dVA3.inc(v.drain())

    @sec("vector")
    def _(v):
        v.memset(stats[:, 32:48], 0.0)
        v.drain()
        for ot in range(CCH):
            if ot >= 2:
                p4_statacc(v, ot - 2)
            v.wait_ge(dPJM.h, ot + 1)
            par = ot % 2
            ptile = pbig0[:, 0:1024] if par == 0 else pbig0[:, 1024:2048]
            v.tensor_mul(scrC[:, 1024:2048], ptile, bufK[:, 1024:2048])
            v.drain()
            v.tensor_scalar(scrC[:, 1024:2048], scrC[:, 1024:2048],
                            kcol[:, 2:3], bpj[:, ot:ot + 1], ALU.mult, ALU.add)
            dPJD.inc(v.drain())
            dX2L.wait(v, ot)
            if ot >= 1:
                v.wait_ge(dH2.h, ot)
                v.wait_ge(dH2A.h, ot)
            x2t = bufK[:, 0:1024] if ot % 2 == 0 else bufV[:, 1024:2048]
            v.scalar_tensor_tensor(scrC[:, 0:1024], scrC[:, 1024:2048],
                                   modcol[:, 32 + ot:33 + ot], x2t,
                                   ALU.mult, ALU.add)
            dXN2.inc(v.drain())
        p4_statacc(v, CCH - 2)
        p4_statacc(v, CCH - 1)

    @sec("scalar")
    def _(sc):
        for ot in range(CCH):
            if ot + 2 < CCH:
                pjsign(sc, ot + 2)
            sc.wait_ge(dXN2.h, ot + 1)
            dXNW.issue(sc.dma_start(out=xnewbuf[ot * 128:(ot + 1) * 128, :],
                                    in_=scrC[:, 0:1024]), ot)
            dXNW.wait(sc, ot)
            sc.activation(small[:, 768:1792], scrC[:, 0:1024], AF.Square)
            dH2.inc(sc.drain())


    @sec("vector")
    def _(v):
        v.wait_ge(dVA3.h, CCH)
        v.tensor_scalar_mul(stats[:, 32:40], stats[:, 32:40], 1.0 / C)
        v.drain()
        v.tensor_scalar_mul(stats[:, 40:48], stats[:, 40:48], 1.0 / C)
        v.drain()
        v.tensor_mul(stats[:, 48:56], stats[:, 32:40], stats[:, 32:40])
        v.drain()
        v.tensor_sub(stats[:, 40:48], stats[:, 40:48], stats[:, 48:56])
        v.drain()
        v.tensor_scalar_add(stats[:, 40:48], stats[:, 40:48], LN_EPS)
        gDV.inc(v.drain())
    P5_VAR = gDV.n

    @sec("scalar")
    def _(sc):
        sc.wait_ge(gDV.h, P5_VAR)
        sc.activation(stats[:, 40:48], stats[:, 40:48], AF.Sqrt)
        gAC.inc(sc.drain())
    P5_SQRT = gAC.n

    @sec("vector")
    def _(v):
        v.wait_ge(gAC.h, P5_SQRT)
        v.reciprocal(stats[:, 40:48], stats[:, 40:48])
        gDV.inc(v.drain())

    emit_col2rep(lambda: stats[:, 32:40], 8, lambda: ex[:, 0:1024])
    P5_REPS = emit_col2rep(lambda: stats[:, 40:48], 8, lambda: ex[:, 1024:2048])

    @sec("vector")
    def _(v):
        v.wait_ge(gDV.h, P5_REPS)
        v.memset(bufQ[:, 0:1024], -1e30)
        v.memset(bufQ[:, 1024:2048], 1e30)
        gDV.inc(v.drain())
    P5_ACC0 = gDV.n

    @sec("sync")
    def _(sync):
        sync.wait_ge(gDV.h, P5_ACC0)
        dXNW.wait_all(sync)
        for ic in range(CCH):
            if ic >= 2:
                dH2S.wait(sync, ic - 2)
            dst = scrA if ic % 2 == 0 else scrB
            dXNL.issue(sync.dma_start(out=dst[:, 0:1024],
                                      in_=xnewbuf[ic * 128:(ic + 1) * 128, :]), ic)

    @sec("vector")
    def _(v):
        for ic in range(CCH):
            dXNL.wait(v, ic)
            if ic >= 1:
                dH2S.wait(v, ic - 1)
            xin_t = (scrA if ic % 2 == 0 else scrB)[:, 0:1024]
            v.tensor_sub(scrC[:, 0:1024], xin_t, ex[:, 0:1024])
            v.drain()
            v.tensor_mul(scrC[:, 0:1024], scrC[:, 0:1024], ex[:, 1024:2048])
            dHB.inc(v.drain())
            v.wait_ge(dH2.h, CCH + ic + 1)
            v.tensor_max(bufQ[:, 0:1024], bufQ[:, 0:1024], scrC[:, 0:1024])
            v.drain()
            v.tensor_tensor(bufQ[:, 1024:2048], bufQ[:, 1024:2048],
                            scrC[:, 0:1024], ALU.min)
            dH2A.inc(v.drain())

    @sec("scalar")
    def _(sc):
        for ic in range(CCH):
            sc.wait_ge(dHB.h, 48 + ic + 1)
            sc.activation(scrC[:, 0:1024], scrC[:, 0:1024], AF.Identity,
                          bias=modcol[:, 48 + ic:49 + ic],
                          scale=modcol[:, 64 + ic:65 + ic])
            dH2.inc(sc.drain())
            sc.wait_ge(dH2A.h, CCH + ic + 1)
            dH2S.issue(sc.dma_start(out=hbuf[ic * 128:(ic + 1) * 128, 0:1024],
                                    in_=scrC[:, 0:1024]), ic)

    @sec("tensor")
    def _(t):
        t.wait_ge(dH2A.h, 2 * CCH)
        for tg in range(8):
            t.transpose(pbig1[:, tg * 128:(tg + 1) * 128],
                        bufQ[:, tg * 128:(tg + 1) * 128], ident[:, :])
            t.transpose(pbig1[:, 1024 + tg * 128:1024 + (tg + 1) * 128],
                        bufQ[:, 1024 + tg * 128:1024 + (tg + 1) * 128],
                        ident[:, :])
        gPE.inc(t.drain())
    P5_TR = gPE.n

    @sec("vector")
    def _(v):
        v.wait_ge(gPE.h, P5_TR)
        v.reduce_max(stats[:, 48:56],
                     pbig1[:, 0:1024].rearrange("p (a b) -> p a b", a=8),
                     axis=AX.X)
        v.drain()
        v.reduce_max(stats[:, 56:64],
                     pbig1[:, 1024:2048].rearrange("p (a b) -> p a b", a=8),
                     axis=AX.X, negate=True)
        v.drain()
        v.tensor_max(stats[:, 48:56], stats[:, 48:56], stats[:, 56:64])
        v.drain()
        v.tensor_scalar_max(stats[:, 48:56], stats[:, 48:56], QEPS)
        v.drain()
        v.reciprocal(stats[:, 56:64], stats[:, 48:56])
        v.drain()
        v.tensor_scalar_mul(stats[:, 56:64], stats[:, 56:64], 127.0)
        gDV.inc(v.drain())

    P5_R127 = emit_col2rep(lambda: stats[:, 56:64], 8, lambda: bufV[:, 0:1024])

    @sec("sync")
    def _(sync):
        sync.wait_ge(gDV.h, P5_R127)
        dH2S.wait_all(sync)
        for ic in range(CCH):
            if ic >= 2:
                sync.wait_ge(dH2Q.h, ic - 1)
            dst = scrA if ic % 2 == 0 else scrB
            dH2L.issue(sync.dma_start(out=dst[:, 0:1024],
                                      in_=hbuf[ic * 128:(ic + 1) * 128, 0:1024]),
                       ic)

    @sec("vector")
    def _(v):
        for ic in range(CCH):
            dH2L.wait(v, ic)
            if ic >= 1:
                v.wait_ge(dH2Q.h, ic)
            xin_t = (scrA if ic % 2 == 0 else scrB)[:, 0:1024]
            v.tensor_mul(scrC[:, 0:1024], xin_t, bufV[:, 0:1024])
            v.drain()
            v.tensor_scalar(t16[:, 0:1024], scrC[:, 0:1024], 1.0, 256.5,
                            ALU.mult, ALU.add)
            dH2.inc(v.drain())

    @sec("scalar")
    def _(sc):
        for ic in range(CCH):
            sc.wait_ge(dH2.h, 2 * CCH + ic + 1)
            sc.activation(h2q[:, ic, :], t16[:, 0:1024], AF.Copy, bias=-256.0)
            dH2Q.inc(sc.drain())
    P5_H2Q = dH2Q.n

    emit_col2rep(lambda: stats[:, 48:56], 8, lambda: bufK[:, 0:1024])

    @sec("vector")
    def _(v):
        v.wait_ge(gDV.h, gDV.n)
        v.tensor_scalar_mul(bufK[:, 1024:2048], bufK[:, 0:1024], kcol[:, 8:9])
        v.drain()
        v.tensor_scalar_mul(bufK[:, 0:1024], bufK[:, 0:1024], kcol[:, 6:7])
        gDV.inc(v.drain())
    P5_REP13 = gDV.n

    def p6a_st(ot):
        if ot % 2 == 0:
            return scrA, scrB
        return small, bufV

    @sec("sync")
    def _(sync):
        sync.wait_ge(dH2Q.h, P5_H2Q)
        sync.wait_ge(gDV.h, P5_REP13)
        for ot in range(HCH):
            if ot >= 2:
                sync.wait_ge(dFWS.h, 4 * (ot - 1))
            s1t, s3t = p6a_st(ot)
            dFWL.issue(sync.dma_start(
                out=s1t[:, :],
                in_=w1T[:, ot * 2048:(ot + 1) * 2048]), 2 * ot)
            dFWL.issue(sync.dma_start(
                out=s3t[:, :],
                in_=w3T[:, ot * 2048:(ot + 1) * 2048]), 2 * ot + 1)

    tern_tmp = [ttmp[:, :].bitcast(BF16), smaxrep[:, 0:1024].bitcast(BF16)]
    w2tmp = [AR1[:, 22528:23552].bitcast(BF16),
             AR1[:, 23552:24576].bitcast(BF16)]

    def p6a_signs(sc, ot):
        dFWL.wait(sc, 2 * ot)
        dFWL.wait(sc, 2 * ot + 1)
        if ot >= 2:
            sc.wait_ge(dFMM.h, ot - 1)
            sc.wait_ge(dFWV.h, 2 * (ot - 1))
        s1t, s3t = p6a_st(ot)
        par = ot % 2
        w1d = (AR2[:, 0:2816] if par == 0 else AR2[:, 2816:5632]).bitcast(
            BF16)[:, 0:2048]
        w3d = (AR2[:, 0:2816] if par == 0 else AR2[:, 2816:5632]).bitcast(
            BF16)[:, 2048:4096]
        sc.activation(w1d, s1t[:, :], AF.Sign, bias=kcol[:, 9:10])
        dFWS.inc(sc.drain())
        sc.activation(tern_tmp[ot % 2], s1t[:, :], AF.Sign,
                      bias=kcol[:, 3:4])
        dFWS.inc(sc.drain())
        sc.activation(w3d, s3t[:, :], AF.Sign, bias=kcol[:, 11:12])
        dFWS.inc(sc.drain())
        sc.wait_ge(dFWV.h, 2 * ot + 1)
        sc.activation(tern_tmp[ot % 2], s3t[:, :], AF.Sign,
                      bias=kcol[:, 5:6])
        dFWS.inc(sc.drain())

    @sec("tensor")
    def _(t):
        for ot in range(HCH):
            t.wait_ge(dFWV.h, 2 * (ot + 1))
            if ot >= 2:
                t.wait_ge(dFH3.h, 3 * (ot - 2) + 2)
            par = ot % 2
            ptile = pbig0 if par == 0 else pbig1
            wb = (AR2[:, 0:2816] if par == 0 else AR2[:, 2816:5632]).bitcast(BF16)
            for kc in range(CCH):
                l1 = wb[:, kc * 128:(kc + 1) * 128]
                l3 = wb[:, 2048 + kc * 128:2048 + (kc + 1) * 128]
                for h5 in range(2):
                    t.matmul(ptile[:, h5 * 512:(h5 + 1) * 512], l1,
                             h2q[:, kc, h5 * 512:(h5 + 1) * 512],
                             start=(kc == 0), stop=(kc == CCH - 1))
                    t.matmul(ptile[:, 1024 + h5 * 512:1024 + (h5 + 1) * 512], l3,
                             h2q[:, kc, h5 * 512:(h5 + 1) * 512],
                             start=(kc == 0), stop=(kc == CCH - 1))
            dFMM.inc(t.drain())

    P6A_GAC0 = gAC.n

    def p6a_pass2(v, ot):
        par2 = ot % 2
        w1d = (AR2[:, 0:2816] if par2 == 0 else AR2[:, 2816:5632]).bitcast(
            BF16)[:, 0:2048]
        w3d = (AR2[:, 0:2816] if par2 == 0 else AR2[:, 2816:5632]).bitcast(
            BF16)[:, 2048:4096]
        for i5, dst in enumerate((w1d, w3d)):
            v.wait_ge(dFWS.h, 4 * ot + 2 * (i5 + 1))
            v.tensor_add(dst, dst, tern_tmp[ot % 2])
            dFWV.inc(v.drain())

    @sec("vector")
    def _(v):
        v.memset(ex[:, 0:1024], -1e30)
        v.memset(ex[:, 1024:2048], 1e30)
        v.drain()
        p6a_pass2(v, 0)
        p6a_pass2(v, 1)
        for ot in range(HCH):
            if ot + 2 < HCH:
                p6a_pass2(v, ot + 2)
            v.wait_ge(dFMM.h, ot + 1)
            if ot >= 2:
                dFSP.wait(v, ot - 2)
            par = ot % 2
            ptile = pbig0 if par == 0 else pbig1
            h3o = scrC[:, 0:1024] if ot % 2 == 0 else scrC[:, 1024:2048]
            v.tensor_mul(bufQ[:, 0:1024], ptile[:, 0:1024], bufK[:, 0:1024])
            dFH3.inc(v.drain())
            v.tensor_mul(bufQ[:, 1024:2048], ptile[:, 1024:2048],
                         bufK[:, 1024:2048])
            dFH3.inc(v.drain())
            gAC_t = P6A_GAC0 + ot + 1

            v.wait_ge(gAC.h, gAC_t)
            v.tensor_mul(h3o, bufQ[:, 0:1024], bufQ[:, 1024:2048])
            v.drain()
            v.tensor_max(ex[:, 0:1024], ex[:, 0:1024], h3o)
            v.drain()
            v.tensor_tensor(ex[:, 1024:2048], ex[:, 1024:2048], h3o, ALU.min)
            dFH3.inc(v.drain())

    @sec("scalar")
    def _(sc):
        p6a_signs(sc, 0)
        p6a_signs(sc, 1)
        for ot in range(HCH):
            if ot + 2 < HCH:
                p6a_signs(sc, ot + 2)
            sc.wait_ge(dFH3.h, 3 * ot + 1)
            sc.activation(bufQ[:, 0:1024], bufQ[:, 0:1024], AF.Silu)
            gAC.inc(sc.drain())
            sc.wait_ge(dFH3.h, 3 * ot + 3)
            h3o = scrC[:, 0:1024] if ot % 2 == 0 else scrC[:, 1024:2048]
            dFSP.issue(sc.dma_start(out=h3buf[ot * 128:(ot + 1) * 128, :],
                                    in_=h3o), ot)

    @sec("tensor")
    def _(t):
        t.wait_ge(dFH3.h, 3 * HCH)
        for tg in range(8):
            t.transpose(pbig1[:, tg * 128:(tg + 1) * 128],
                        ex[:, tg * 128:(tg + 1) * 128], ident[:, :])
            t.transpose(pbig1[:, 1024 + tg * 128:1024 + (tg + 1) * 128],
                        ex[:, 1024 + tg * 128:1024 + (tg + 1) * 128],
                        ident[:, :])
        gPE.inc(t.drain())
    P6_TR = gPE.n

    @sec("vector")
    def _(v):
        v.wait_ge(gPE.h, P6_TR)
        v.reduce_max(stats[:, 64:72],
                     pbig1[:, 0:1024].rearrange("p (a b) -> p a b", a=8),
                     axis=AX.X)
        v.drain()
        v.reduce_max(stats[:, 72:80],
                     pbig1[:, 1024:2048].rearrange("p (a b) -> p a b", a=8),
                     axis=AX.X, negate=True)
        v.drain()
        v.tensor_max(stats[:, 64:72], stats[:, 64:72], stats[:, 72:80])
        v.drain()
        v.tensor_scalar_max(stats[:, 64:72], stats[:, 64:72], QEPS)
        v.drain()
        v.reciprocal(stats[:, 72:80], stats[:, 64:72])
        v.drain()
        v.tensor_scalar_mul(stats[:, 72:80], stats[:, 72:80], 127.0)
        gDV.inc(v.drain())

    emit_col2rep(lambda: stats[:, 72:80], 8, lambda: bufK[:, 0:1024])
    P6_R2A = emit_col2rep(lambda: stats[:, 64:72], 8, lambda: bufK[:, 1024:2048])

    @sec("vector")
    def _(v):
        v.wait_ge(gDV.h, P6_R2A)
        v.tensor_scalar_mul(bufK[:, 1024:2048], bufK[:, 1024:2048], kcol[:, 7:8])
        gDV.inc(v.drain())
    P6_REP2 = gDV.n

    @sec("sync")
    def _(sync):
        sync.wait_ge(gDV.h, P6_REP2)
        dFSP.wait_all(sync)
        for jc in range(HCH):
            if jc >= 2:
                sync.wait_ge(dH3Q.h, jc - 1)
            dst = scrA[:, 0:1024] if jc % 2 == 0 else scrA[:, 1024:2048]
            dH3L.issue(sync.dma_start(out=dst,
                                      in_=h3buf[jc * 128:(jc + 1) * 128, :]), jc)

    @sec("vector")
    def _(v):
        for jc in range(HCH):
            dH3L.wait(v, jc)
            if jc >= 1:
                v.wait_ge(dH3Q.h, jc)
            src = scrA[:, 0:1024] if jc % 2 == 0 else scrA[:, 1024:2048]
            v.tensor_mul(scrB[:, 0:1024], src, bufK[:, 0:1024])
            v.drain()
            v.tensor_scalar(t16[:, 0:1024], scrB[:, 0:1024], 1.0, 256.5,
                            ALU.mult, ALU.add)
            dH2.inc(v.drain())

    @sec("scalar")
    def _(sc):
        for jc in range(HCH):
            sc.wait_ge(dH2.h, 3 * CCH + jc + 1)
            sc.activation(h3qf[:, jc * 1024:(jc + 1) * 1024], t16[:, 0:1024],
                          AF.Copy, bias=-256.0)
            dH3Q.inc(sc.drain())
    P6B_H3Q = dH3Q.n

    W2P = [(0, 16), (16, 16), (32, 12)]

    @sec("sync")
    def _(sync):
        sync.wait_ge(dH3Q.h, P6B_H3Q)
        for o2 in range(CCH):
            for pi, (k0, kn) in enumerate(W2P):
                k = o2 * 3 + pi
                if k >= 2:
                    sync.wait_ge(dW2S.h, 2 * (k - 1))
                st = scrC if k % 2 == 0 else small
                dW2L.issue(sync.dma_start(
                    out=st[:, 0:kn * 128],
                    in_=w2T[:, o2 * HID + k0 * 128:
                            o2 * HID + (k0 + kn) * 128]), k)
            if o2 >= 2:
                dYOUT.wait(sync, o2 - 2)
            dst = bufV[:, 0:1024] if o2 % 2 == 0 else bufV[:, 1024:2048]
            dXN2L.issue(sync.dma_start(out=dst,
                                       in_=xnewbuf[o2 * 128:(o2 + 1) * 128, :]),
                        o2)

    def w2_signs(sc, o2):
        par = o2 % 2
        for pi, (k0, kn) in enumerate(W2P):
            k = o2 * 3 + pi
            dW2L.wait(sc, k)
            if o2 >= 2 and pi == 0:
                sc.wait_ge(dYMM.h, o2 - 1)
                sc.wait_ge(dW2V.h, 3 * (o2 - 1))
            st = scrC if k % 2 == 0 else small
            dst = (AR2[:, 0:2816] if par == 0 else
                   AR2[:, 2816:5632]).bitcast(BF16)[:,
                                                    k0 * 128:(k0 + kn) * 128]
            sc.activation(dst, st[:, 0:kn * 128], AF.Sign,
                          bias=kcol[:, 10:11])
            dW2S.inc(sc.drain())
            if pi >= 1:
                sc.wait_ge(dW2V.h, 3 * o2 + pi)
            sc.activation(w2tmp[o2 % 2][:, 0:kn * 128],
                          st[:, 0:kn * 128], AF.Sign, bias=kcol[:, 4:5])
            dW2S.inc(sc.drain())


    @sec("tensor")
    def _(t):
        for o2 in range(CCH):
            t.wait_ge(dW2V.h, 3 * (o2 + 1))
            if o2 >= 2:
                t.wait_ge(dYDQ.h, o2 - 1)
            par = o2 % 2
            ptile = pbig0[:, 0:1024] if par == 0 else pbig1[:, 0:1024]
            wb = (AR2[:, 0:2816] if par == 0 else AR2[:, 2816:5632]).bitcast(BF16)
            for kc in range(HCH):
                lhs = wb[:, kc * 128:(kc + 1) * 128]
                for h5 in range(2):
                    t.matmul(ptile[:, h5 * 512:(h5 + 1) * 512], lhs,
                             h3qf[:, kc * 1024 + h5 * 512:
                                  kc * 1024 + (h5 + 1) * 512],
                             start=(kc == 0), stop=(kc == HCH - 1))
            dYMM.inc(t.drain())

    def p6b_pass2(v, o2):
        par2 = o2 % 2
        for pi, (k0, kn) in enumerate(W2P):
            k = o2 * 3 + pi
            v.wait_ge(dW2S.h, 2 * (k + 1))
            dst = (AR2[:, 0:2816] if par2 == 0 else
                   AR2[:, 2816:5632]).bitcast(BF16)[:, k0 * 128:(k0 + kn) * 128]
            v.tensor_add(dst, dst, w2tmp[o2 % 2][:, 0:kn * 128])
            dW2V.inc(v.drain())

    @sec("vector")
    def _(v):
        p6b_pass2(v, 0)
        p6b_pass2(v, 1)
        for o2 in range(CCH):
            if o2 + 2 < CCH:
                p6b_pass2(v, o2 + 2)
            v.wait_ge(dYMM.h, o2 + 1)
            par = o2 % 2
            ptile = pbig0[:, 0:1024] if par == 0 else pbig1[:, 0:1024]
            v.tensor_mul(scrB[:, 1024:2048], ptile, bufK[:, 1024:2048])
            dYDQ.inc(v.drain())
            dXN2L.wait(v, o2)
            if o2 >= 2:
                dYOUT.wait(v, o2 - 2)
            xnt = bufV[:, 0:1024] if o2 % 2 == 0 else bufV[:, 1024:2048]
            dst = bufQ[:, 0:1024] if o2 % 2 == 0 else bufQ[:, 1024:2048]
            v.scalar_tensor_tensor(dst, scrB[:, 1024:2048],
                                   modcol[:, 80 + o2:81 + o2], xnt,
                                   ALU.mult, ALU.add)
            dH2.inc(v.drain())

    @sec("scalar")
    def _(sc):
        w2_signs(sc, 0)
        w2_signs(sc, 1)
        for o2 in range(2, CCH):
            w2_signs(sc, o2)
            sc.wait_ge(dH2.h, 3 * CCH + HCH + o2 - 1)
            src6 = bufQ[:, 0:1024] if o2 % 2 == 0 else bufQ[:, 1024:2048]
            dYOUT.issue(sc.dma_start(
                out=yT[(o2 - 2) * 128:(o2 - 1) * 128, :], in_=src6), o2 - 2)
        for o2 in (14, 15):
            sc.wait_ge(dH2.h, 3 * CCH + HCH + o2 + 1)
            src6 = bufQ[:, 0:1024] if o2 % 2 == 0 else bufQ[:, 1024:2048]
            dYOUT.issue(sc.dma_start(
                out=yT[o2 * 128:(o2 + 1) * 128, :], in_=src6), o2)

    @sec("gpsimd")
    def _(g):
        dYOUT.wait_all(g)

    for cm in reversed(ctxs):
        cm.__exit__(None, None, None)
    return nc


def _prep(inputs):
    x = np.ascontiguousarray(inputs["L0_attn_x"], dtype=np.float32)
    qkv_w = np.asarray(inputs["L0_attn_qkv_w"], dtype=np.float32)
    qkv_b = np.asarray(inputs["L0_attn_qkv_b"], dtype=np.float32)
    gate_w = np.asarray(inputs["L0_attn_gate_w"], dtype=np.float32)
    gate_b = np.asarray(inputs["L0_attn_gate_b"], dtype=np.float32)
    proj_w = np.asarray(inputs["L0_attn_proj_w"], dtype=np.float32)
    proj_b = np.asarray(inputs["L0_attn_proj_b"], dtype=np.float32)
    norm_w = np.asarray(inputs["L0_attn_norm_w"], dtype=np.float32)
    w1 = np.asarray(inputs["L0_ffn_w1"], dtype=np.float32)
    w2 = np.asarray(inputs["L0_ffn_w2"], dtype=np.float32)
    w3 = np.asarray(inputs["L0_ffn_w3"], dtype=np.float32)
    cc = np.asarray(inputs["L0_c"], dtype=np.float32)
    wmod = np.asarray(inputs["L0_wmod"], dtype=np.float32)

    def tilestage(block):
        q = block.reshape(-1, 128, block.shape[1])
        return q.transpose(1, 0, 2).reshape(128, -1)

    w1T = np.concatenate(
        [tilestage(w1.T[:, ot * 128:(ot + 1) * 128]) for ot in range(HCH)],
        axis=1)
    w3T = np.concatenate(
        [tilestage(w3.T[:, ot * 128:(ot + 1) * 128]) for ot in range(HCH)],
        axis=1)
    w2T = np.concatenate(
        [tilestage(w2.T[:, o2 * 128:(o2 + 1) * 128]) for o2 in range(CCH)],
        axis=1)
    cT = np.ascontiguousarray(cc.T)
    identv = np.eye(128, dtype=np.float32)

    seg_arr = {"qkv": qkv_w, "gate": gate_w, "proj": proj_w,
               "w1": w1, "w2": w2, "w3": w3}

    cols = []
    bcols = []
    for h in range(NH):
        qs = qkv_w[h * 128:(h + 1) * 128, :]
        ks = qkv_w[2048 + h * 128:2048 + (h + 1) * 128, :]
        vs = qkv_w[4096 + h * 128:4096 + (h + 1) * 128, :]
        gs = gate_w[h * 128:(h + 1) * 128, :]
        cols += [qs.T, ks.T, vs.T, gs.T]
        bcols += [qkv_b[h * 128:(h + 1) * 128],
                  qkv_b[2048 + h * 128:2048 + (h + 1) * 128],
                  qkv_b[4096 + h * 128:4096 + (h + 1) * 128],
                  gate_b[h * 128:(h + 1) * 128]]
    wqkvg_flat = np.concatenate(cols, axis=1)
    wqkvgT = np.concatenate(
        [tilestage(wqkvg_flat[4 * p4 * 128:(4 * p4 + 4) * 128,
                              hh * 512:(hh + 1) * 512])
         for hh in range(NH) for p4 in range(4)], axis=1)
    bqkvg = np.ascontiguousarray(np.stack(bcols, axis=1))
    wprojT = np.concatenate(
        [tilestage(proj_w.T[:, ot * 128:(ot + 1) * 128]) for ot in range(CCH)],
        axis=1)
    normw = np.ascontiguousarray(norm_w.reshape(16, 128).T)
    bpjc = np.ascontiguousarray(proj_b.reshape(16, 128).T)

    maps = []
    for core in range(8):
        b = core // 2
        side = core % 2
        xTb = np.ascontiguousarray(x[b].T[:, side * 1024:(side + 1) * 1024])
        wmod_shc = np.ascontiguousarray(wmod[:, core * 1536:(core + 1) * 1536])
        wshp = []
        for s in SEGS:
            a = seg_arr[s].reshape(-1)
            n8 = a.size // 8
            wshp.append(a[core * n8:(core + 1) * n8].reshape(128, -1))
        wshc = np.ascontiguousarray(np.concatenate(wshp, axis=1))
        boh = np.zeros((128, 4), dtype=np.float32)
        boh[:, b] = 1.0
        mselv = np.full((128, 1), float(side), dtype=np.float32)
        maps.append({
            "xT": xTb, "wqkvgT": wqkvgT, "bqkvg": bqkvg,
            "wprojT": wprojT, "bproj": bpjc, "normw": normw,
            "w1T": w1T, "w3T": w3T, "w2T": w2T,
            "wmod_sh": wmod_shc, "cTin": cT, "wsh": wshc,
            "bonehot": boh, "msel": mselv, "identin": identv,
        })
    return maps


def kernel(**inputs):
    if "nc" not in _CACHE:
        _CACHE["nc"] = build()
    nc = _CACHE["nc"]
    maps = _prep(inputs)
    res = run_bass_kernel_spmd(nc, maps, core_ids=list(range(8)))
    y = np.empty((B, NTOK, C), dtype=np.float32)
    for core in range(8):
        b = core // 2
        side = core % 2
        y[b, side * 1024:(side + 1) * 1024, :] = res.results[core]["yT"].T
    return y
